# version 04 (final)
# baseline (speedup 1.0000x reference)
import numpy as np
from contextlib import ExitStack

import concourse.bass as bass
import concourse.mybir as mybir

F32 = mybir.dt.float32
BF16 = mybir.dt.bfloat16
I32 = mybir.dt.int32
U8 = mybir.dt.uint8
AF = mybir.ActivationFunctionType
ALU = mybir.AluOpType
AX = mybir.AxisListType

STOP = None
JOBS_LIM = None
T = 2048
D = 1024
NCH = 8
NTT = 16
NEG = -1.0e38


class Ctr:
    def __init__(self, h):
        self.h = h
        self.n = 0

    def inc(self, inst, k=1):
        inst.then_inc(self.h, k)
        self.n += k
        return inst

    def dma(self, inst):
        return self.inc(inst, 16)

    def wait_all(self, eng):
        eng.wait_ge(self.h, self.n)


class Prog:
    def __init__(self):
        self.nc = bass.Bass("TRN2", target_bir_lowering=False)
        self.es = ExitStack()
        self._n = 0
        self._sems = []

    def sbuf(self, name, shape, dt=F32, arena=None):
        return (arena or self.es).enter_context(self.nc.sbuf_tensor(name, shape, dt))

    def psum(self, name, shape, dt=F32, arena=None):
        return (arena or self.es).enter_context(self.nc.psum_tensor(name, shape, dt))

    def sem(self, name=None):
        self._n += 1
        c = Ctr(self.nc.alloc_semaphore(name=name or f"s{self._n}"))
        self._sems.append(c)
        return c

    def sem_mark(self):
        return len(self._sems)

    def sem_free_since(self, mark):
        hs = [c.h for c in self._sems[mark:]]
        if not hs:
            return
        del self._sems[mark:]
        for h in hs:
            self.nc.gpsimd.dma_reset(range(h.num, h.num + 1))
        self.nc.clear_and_free_semaphores(hs)
        self.nc.all_engine_barrier()

    def din(self, name, shape, dt=F32):
        return self.nc.dram_tensor(name, shape, dt, kind="ExternalInput").ap()

    def dout(self, name, shape, dt=F32):
        return self.nc.dram_tensor(name, shape, dt, kind="ExternalOutput").ap()

    def block(self):
        return self.nc.Block()


def chunks(sb, n=NCH):
    return sb.ap().rearrange("p (c t) -> p c t", c=n)


def tiles(sb, e, ntt=NTT):
    return sb.ap().rearrange("p (tt e) -> p tt e", tt=ntt)



def emit_consts(p, ident_bf=False, ident_f32=False, tril_s=False):
    c = {}
    c["ones_f"] = p.sbuf("c_ones_f", [1, 128], F32)
    c["ones_col"] = p.sbuf("c_ones_col", [128, 1], F32)
    c["ones_bf"] = p.sbuf("c_ones_bf", [1, 128], BF16)
    c["iota_f"] = p.sbuf("c_iota_f", [1, 512], F32)
    c["iota_d"] = iota_d = p.sbuf("c_iota_d", [128, 128], I32)
    iota_fi = p.sbuf("c_iota_fi", [1, 512], I32)
    if ident_bf:
        c["ident_bf"] = p.sbuf("c_ident_bf", [128, 128], BF16)
        c["ident_bf64"] = p.sbuf("c_ident_bf64", [128, 128], BF16)
    if ident_f32:
        c["ident_f"] = p.sbuf("c_ident_f", [128, 128], F32)
    if tril_s:
        c["tril_s"] = p.sbuf("c_tril_s", [128, 128], F32)
        c["stairs16"] = p.sbuf("c_stairs16", [128, 31], F32)
        c["stairs16b"] = p.sbuf("c_stairs16b", [128, 31], BF16)
    with p.block() as b:
        @b.gpsimd
        def _(g):
            g.iota(iota_d[:], [[1, 128]], channel_multiplier=-1,
                   allow_small_or_imprecise_dtypes=True)
            g.iota(iota_fi[:], [[1, 512]], channel_multiplier=0,
                   allow_small_or_imprecise_dtypes=True)
        @b.vector
        def _(v):
            v.memset(c["ones_f"][:], 1.0)
            v.memset(c["ones_col"][:], 1.0)
            v.memset(c["ones_bf"][:], 1.0)
    with p.block() as b:
        @b.vector
        def _(v):
            if ident_bf:
                v.tensor_scalar(c["ident_bf"][:], iota_d[:], 0, None, op0=ALU.is_equal)
                v.tensor_scalar(c["ident_bf64"][:], iota_d[:], -64, None, op0=ALU.is_equal)
            if ident_f32:
                v.tensor_scalar(c["ident_f"][:], iota_d[:], 0, None, op0=ALU.is_equal)
            if tril_s:
                v.tensor_scalar(c["tril_s"][:], iota_d[:], 0, None, op0=ALU.is_gt)
                v.memset(c["stairs16"][:], 0.0)
                v.memset(c["stairs16b"][:], 0.0)
            v.tensor_copy(c["iota_f"][:], iota_fi[:])
    if tril_s:
        with p.block() as b:
            @b.vector
            def _(v):
                v.memset(c["stairs16"][:, 15:16], 1.0)
                v.memset(c["stairs16b"][:, 15:16], 1.0)
    return c


def emit_load_fm(p, hbm_fm, sb, nch=NCH, queue="sync", cast=False):
    ld = p.sem()
    with p.block() as b:
        src = hbm_fm.rearrange("(c p) t -> p c t", p=128)
        sb = chunks(sb, nch)
        if queue == "sync":
            @b.sync
            def _(sy):
                ld.dma(sy.dma_start(out=sb, in_=src))
        else:
            @b.gpsimd
            def _(g):
                ld.dma(g.dma_start(out=sb, in_=src))
        @b.vector
        def _(v):
            v.wait_ge(ld.h, 16)
        @b.tensor
        def _(t):
            t.wait_ge(ld.h, 16)
        @b.scalar
        def _(s):
            s.wait_ge(ld.h, 16)
    return ld



def emit_ln(p, c, x_sb, xn_bf, tag, arena=None, eps=1e-5, make_xn=True):
    stats2 = p.sbuf(f"{tag}_stats2", [2, T], F32, arena)
    rstd_row = p.sbuf(f"{tag}_rstd", [1, T], F32, arena)
    mu_b = p.sbuf(f"{tag}_mu_b", [128, T], F32, arena)
    rstd_b = p.sbuf(f"{tag}_rstd_b", [128, T], F32, arena)
    mu_row = stats2[0:1, :]
    out = dict(mu_row=mu_row, rstd_row=rstd_row, mu_b=mu_b, rstd_b=rstd_b, stats2=stats2)

    ar = ExitStack()
    lhs_mu = p.sbuf(f"{tag}_lhs_mu", [128, 2], F32, ar)
    lhs_sq = p.sbuf(f"{tag}_lhs_sq", [128, 2], F32, ar)
    ps_st = p.psum(f"{tag}_ps_st", [2, T], F32, ar)
    ps_bc = p.psum(f"{tag}_ps_bc", [128, T], F32, ar)
    scr = p.sbuf(f"{tag}_scr", [128, T], F32, ar)
    scr2 = p.sbuf(f"{tag}_scr2", [128, T], F32, ar)
    s_sq = p.sem()
    s_mm = p.sem()
    s_x = p.sem()
    s_bc = p.sem()
    xc = chunks(x_sb)

    with p.block() as b:
        @b.vector
        def _(v):
            v.memset(lhs_mu[:], 0.0)
            v.memset(lhs_sq[:], 0.0)
    with p.block() as b:
        @b.vector
        def _(v):
            v.memset(lhs_mu[:, 0:1], 1.0 / D)
            v.memset(lhs_sq[:, 1:2], 1.0 / D)

    with p.block() as b:
        @b.scalar
        def _(s):
            for ch in range(NCH):
                if ch >= 1:
                    s.wait_ge(s_mm.h, ch)
                s_sq.inc(s.activation(out=scr[:], in_=xc[:, ch], func=AF.Square))
        @b.tensor
        def _(t):
            for ch in range(NCH):
                for j in range(4):
                    t.matmul(ps_st[:, 512 * j:512 * (j + 1)], lhs_mu[:],
                             xc[:, ch, 512 * j:512 * (j + 1)],
                             start=(ch == 0), stop=False)
            for ch in range(NCH):
                t.wait_ge(s_sq.h, ch + 1)
                for j in range(4):
                    inst = t.matmul(ps_st[:, 512 * j:512 * (j + 1)], lhs_sq[:],
                                    scr[:, 512 * j:512 * (j + 1)],
                                    start=False, stop=(ch == NCH - 1))
                    if j == 3:
                        s_mm.inc(inst)
    with p.block() as b:
        @b.vector
        def _(v):
            v.tensor_copy(stats2[:], ps_st[:])
    with p.block() as b:
        @b.gpsimd
        def _(g):
            s_x.dma(g.dma_start(out=rstd_row[:], in_=stats2[1:2, :]))
            g.wait_ge(s_x.h, 16)
    with p.block() as b:
        @b.vector
        def _(v):
            v.tensor_tensor(scr[0:1, :], mu_row, mu_row, op=ALU.mult)
    with p.block() as b:
        @b.vector
        def _(v):
            v.tensor_tensor(rstd_row[:], rstd_row[:], scr[0:1, :], op=ALU.subtract)
    with p.block() as b:
        @b.vector
        def _(v):
            v.tensor_scalar_add(rstd_row[:], rstd_row[:], float(eps))
    with p.block() as b:
        @b.scalar
        def _(s):
            s.activation(out=rstd_row[:], in_=rstd_row[:], func=AF.Sqrt)
    with p.block() as b:
        @b.vector
        def _(v):
            v.reciprocal(rstd_row[:], rstd_row[:])
    with p.block() as b:
        @b.tensor
        def _(t):
            for j in range(4):
                inst = t.matmul(ps_bc[:, 512 * j:512 * (j + 1)], c["ones_f"][:],
                                mu_row[:, 512 * j:512 * (j + 1)], start=True, stop=True)
                s_bc.inc(inst)
        @b.vector
        def _(v):
            v.wait_ge(s_bc.h, 4)
            s_bc.inc(v.tensor_copy(mu_b[:], ps_bc[:]))
    with p.block() as b:
        @b.tensor
        def _(t):
            t.wait_ge(s_bc.h, 5)
            for j in range(4):
                inst = t.matmul(ps_bc[:, 512 * j:512 * (j + 1)], c["ones_f"][:],
                                rstd_row[:, 512 * j:512 * (j + 1)], start=True, stop=True)
                s_bc.inc(inst)
        @b.vector
        def _(v):
            v.wait_ge(s_bc.h, 9)
            v.tensor_copy(rstd_b[:], ps_bc[:])
    if make_xn:
        s_n1 = p.sem()
        s_n2 = p.sem()
        with p.block() as b:
            @b.vector
            def _(v):
                for ch in range(NCH):
                    dst = scr if ch % 2 == 0 else scr2
                    if ch >= 2:
                        v.wait_ge(s_n2.h, ch - 1)
                    s_n1.inc(v.tensor_tensor(dst[:], xc[:, ch], mu_b[:], op=ALU.subtract))
            @b.gpsimd
            def _(g):
                for ch in range(NCH):
                    src = scr if ch % 2 == 0 else scr2
                    g.wait_ge(s_n1.h, ch + 1)
                    s_n2.inc(g.tensor_tensor(chunks(xn_bf)[:, ch], src[:], rstd_b[:], op=ALU.mult))
    with p.block() as b:
        @b.vector
        def _(v):
            pass
    ar.close()
    return out



def emit_scores(p, c, x_sb, ln, w_sb, E, s_tm, tag):
    assert E <= 110
    ar = ExitStack()
    cs_row = p.sbuf(f"{tag}_cs", [1, E], F32, ar)
    s_fm = p.sbuf(f"{tag}_s_fm", [E, T], F32, ar)
    ar_ps = ExitStack()
    ps_s = p.psum(f"{tag}_ps_s", [E, T], F32, ar_ps)
    ps_cs = p.psum(f"{tag}_ps_cs", [1, E], F32, ar_ps)
    sm = p.sem()
    wv = w_sb.ap().rearrange("p (c e) -> p c e", c=NCH)
    xc = chunks(x_sb)

    with p.block() as b:
        @b.tensor
        def _(t):
            for ch in range(NCH):
                inst = t.matmul(ps_cs[:], c["ones_col"][:], wv[:, ch], start=(ch == 0),
                                stop=(ch == NCH - 1))
            sm.inc(inst)
        @b.vector
        def _(v):
            v.wait_ge(sm.h, 1)
            sm.inc(v.tensor_scalar_mul(cs_row[:], ps_cs[:], -1.0))
    with p.block() as b:
        @b.tensor
        def _(t):
            t.wait_ge(sm.h, 2)
            for j in range(4):
                sl = slice(512 * j, 512 * (j + 1))
                for ch in range(NCH):
                    t.matmul(ps_s[:, sl], wv[:, ch], xc[:, ch, sl],
                             start=(ch == 0), stop=False)
                inst = t.matmul(ps_s[:, sl], cs_row[:], ln["mu_row"][:, sl],
                                start=False, stop=True)
            sm.inc(inst)
        @b.vector
        def _(v):
            v.wait_ge(sm.h, 3)
            v.tensor_tensor(s_fm[:], ps_s[:], ln["rstd_b"][0:E, :], op=ALU.mult)
    with p.block() as b:
        @b.vector
        def _(v):
            pass
    ar_ps.close()
    emit_transpose_to_tm(p, c, s_fm, E, s_tm, f"{tag}_t", dt=F32)
    with p.block() as b:
        @b.vector
        def _(v):
            pass
    ar.close()


def emit_transpose_to_tm(p, c, src_fm, E, dst_tm, tag, dt=F32, ntt=NTT,
                         dst_ntt=None):
    ident = (c["ident_f"] if dt == F32 else c["ident_bf"])[:]
    ar = ExitStack()
    ps = [p.psum(f"{tag}_tr{i}", [128, max(E, 1)], dt, ar) for i in range(2)]
    sm_t = p.sem()
    sm_e = p.sem()
    dv = tiles(dst_tm, E, dst_ntt or ntt)
    with p.block() as b:
        @b.tensor
        def _(t):
            for tt in range(ntt):
                if tt >= 2:
                    t.wait_ge(sm_e.h, tt - 1)
                sm_t.inc(t.transpose(ps[tt % 2][:, 0:E],
                                     src_fm[0:E, 128 * tt:128 * (tt + 1)],
                                     ident[0:E, 0:E]))
        @b.vector
        def _(v):
            for tt in range(ntt):
                v.wait_ge(sm_t.h, tt + 1)
                sm_e.inc(v.tensor_copy(dv[:, tt], ps[tt % 2][:, 0:E]))
    with p.block() as b:
        @b.vector
        def _(v):
            pass
    ar.close()


def emit_transpose_tm_to_fm(p, c, src_tm, E, dst_fm, tag, dt=F32, ntt=NTT,
                            accum=False):
    ident = (c["ident_f"] if dt == F32 else c["ident_bf"])[:]
    ar = ExitStack()
    ps = [p.psum(f"{tag}_tr{i}", [max(E, 1), 128], dt, ar) for i in range(2)]
    sm_t = p.sem()
    sm_e = p.sem()
    sv = tiles(src_tm, E, ntt)
    with p.block() as b:
        @b.tensor
        def _(t):
            for tt in range(ntt):
                if tt >= 2:
                    t.wait_ge(sm_e.h, tt - 1)
                sm_t.inc(t.transpose(ps[tt % 2][0:E, :], sv[:, tt], ident))
        @b.vector
        def _(v):
            for tt in range(ntt):
                v.wait_ge(sm_t.h, tt + 1)
                dst = dst_fm[0:E, 128 * tt:128 * (tt + 1)]
                if accum:
                    sm_e.inc(v.tensor_tensor(dst, dst, ps[tt % 2][0:E, :], op=ALU.add))
                else:
                    sm_e.inc(v.tensor_copy(dst, ps[tt % 2][0:E, :]))
    with p.block() as b:
        @b.vector
        def _(v):
            pass
    ar.close()



def emit_softmax_tm(p, c, s_tm, E, tag, arena=None):
    ar = ExitStack()
    m_col = p.sbuf(f"{tag}_m", [128, NTT], F32, ar)
    z_col = p.sbuf(f"{tag}_z", [128, NTT], F32, ar)
    sv = tiles(s_tm, E)
    with p.block() as b:
        @b.vector
        def _(v):
            v.tensor_reduce(m_col[:], sv, axis=AX.X, op=ALU.max)
    with p.block() as b:
        @b.vector
        def _(v):
            v.tensor_scalar_mul(m_col[:], m_col[:], -1.0)
    with p.block() as b:
        @b.scalar
        def _(s):
            for tt in range(NTT):
                s.activation(out=sv[:, tt], in_=sv[:, tt], func=AF.Exp,
                             bias=m_col[:, tt:tt + 1])
    with p.block() as b:
        @b.vector
        def _(v):
            v.tensor_reduce(z_col[:], sv, axis=AX.X, op=ALU.add)
    with p.block() as b:
        @b.vector
        def _(v):
            v.reciprocal(z_col[:], z_col[:])
    with p.block() as b:
        @b.vector
        def _(v):
            for tt in range(NTT):
                v.tensor_scalar(sv[:, tt], sv[:, tt], z_col[:, tt:tt + 1], None,
                                op0=ALU.mult)
    with p.block() as b:
        @b.vector
        def _(v):
            pass
    ar.close()



def emit_topk_tm(p, c, s_tm, E, K, tag, persist_arena=None):
    BIG = 1.0e9
    onehots, vals = [], []
    for k in range(K):
        onehots.append(p.sbuf(f"{tag}_oh{k}", [128, NTT * E], F32, persist_arena))
        vals.append(p.sbuf(f"{tag}_val{k}", [128, NTT], F32, persist_arena))
    ar = ExitStack()
    m_col = p.sbuf(f"{tag}_m", [128, NTT], F32, ar)
    i_col = p.sbuf(f"{tag}_i", [128, NTT], F32, ar)
    cand = p.sbuf(f"{tag}_cand", [128, NTT * E], F32, ar)
    iota_b = p.sbuf(f"{tag}_iob", [128, E], F32, ar)
    ps_b = p.psum(f"{tag}_psb", [128, E], F32, ar)
    s_x = p.sem()
    sv = tiles(s_tm, E)
    cv = tiles(cand, E)
    with p.block() as b:
        @b.tensor
        def _(t):
            s_x.inc(t.matmul(ps_b[:], c["ones_f"][:], c["iota_f"][:, 0:E],
                             start=True, stop=True))
        @b.vector
        def _(v):
            v.wait_ge(s_x.h, 1)
            v.tensor_copy(iota_b[:], ps_b[:])
    for k in range(K):
        oh = onehots[k]
        val = vals[k]
        ohv = tiles(oh, E)
        with p.block() as b:
            @b.vector
            def _(v):
                v.tensor_reduce(m_col[:], sv, axis=AX.X, op=ALU.max)
        with p.block() as b:
            @b.vector
            def _(v):
                v.tensor_copy(val[:], m_col[:])
        with p.block() as b:
            @b.vector
            def _(v):
                for tt in range(NTT):
                    v.tensor_scalar(cv[:, tt], sv[:, tt], m_col[:, tt:tt + 1], None,
                                    op0=ALU.not_equal)
        with p.block() as b:
            @b.vector
            def _(v):
                v.tensor_scalar_mul(cand[:], cand[:], BIG)
        with p.block() as b:
            @b.vector
            def _(v):
                for tt in range(NTT):
                    v.tensor_tensor(cv[:, tt], cv[:, tt], iota_b[:], op=ALU.add)
        with p.block() as b:
            @b.vector
            def _(v):
                v.tensor_reduce(i_col[:], cv, axis=AX.X, op=ALU.min)
        with p.block() as b:
            @b.vector
            def _(v):
                for tt in range(NTT):
                    v.tensor_scalar(ohv[:, tt], iota_b[:], i_col[:, tt:tt + 1], None,
                                    op0=ALU.is_equal)
        with p.block() as b:
            @b.vector
            def _(v):
                if k < K - 1:
                    v.scalar_tensor_tensor(s_tm[:], oh[:], -2.0 * BIG, s_tm[:],
                                           op0=ALU.mult, op1=ALU.add)
    with p.block() as b:
        @b.vector
        def _(v):
            pass
    ar.close()
    return onehots, vals



def emit_prefix_tokens(p, c, chan_tm, C, coreoff_row, tag, arena=None):
    pref_tm = p.sbuf(f"{tag}_pref", [128, NTT * C], F32, arena)
    cnt_row = p.sbuf(f"{tag}_cnt", [1, C], F32, arena)
    ar = ExitStack()
    ps16 = p.psum(f"{tag}_ps16", [16, C], F32, ar)
    ps_off = p.psum(f"{tag}_ps_off", [16, C], F32, ar)
    ps_cnt = p.psum(f"{tag}_ps_cnt", [1, C], F32, ar)
    tot16 = p.sbuf(f"{tag}_tot16", [16, C], F32, ar)
    off16 = p.sbuf(f"{tag}_off16", [16, C], F32, ar)
    offs_row = p.sbuf(f"{tag}_offs", [1, NTT * C], F32, ar)
    ps_pref = [p.psum(f"{tag}_ps_pref{i}", [128, C], F32, ar) for i in range(2)]
    sm = p.sem()
    se = p.sem()
    cvv = tiles(chan_tm, C)
    pvv = tiles(pref_tm, C)

    with p.block() as b:
        @b.tensor
        def _(t):
            for tt in range(NTT):
                inst = t.matmul(ps16[:], c["stairs16"][:, 15 - tt:31 - tt], cvv[:, tt],
                                start=(tt == 0), stop=(tt == NTT - 1))
            sm.inc(inst)
        @b.vector
        def _(v):
            v.wait_ge(sm.h, 1)
            v.tensor_copy(tot16[:], ps16[:])
    with p.block() as b:
        @b.tensor
        def _(t):
            t.matmul(ps_off[:], c["tril_s"][0:16, 0:16], tot16[:],
                     start=True, stop=(coreoff_row is None))
            if coreoff_row is not None:
                t.matmul(ps_off[:], c["ones_f"][:, 0:16], coreoff_row,
                         start=False, stop=True)
            sm.inc(t.matmul(ps_cnt[:], c["ones_col"][0:16, :], tot16[:], start=True, stop=True))
        @b.vector
        def _(v):
            v.wait_ge(sm.h, 2)
            v.tensor_copy(off16[:], ps_off[:])
            v.tensor_copy(cnt_row[:], ps_cnt[:])
    s_dma = p.sem()
    with p.block() as b:
        @b.gpsimd
        def _(g):
            s_dma.dma(g.dma_start(out=offs_row.ap().rearrange("p (tt cc) -> p tt cc", tt=NTT),
                                  in_=off16[:]))
            g.wait_ge(s_dma.h, s_dma.n)
    with p.block() as b:
        @b.tensor
        def _(t):
            for tt in range(NTT):
                if tt >= 2:
                    t.wait_ge(se.h, tt - 1)
                t.matmul(ps_pref[tt % 2][:], c["tril_s"][:], cvv[:, tt],
                         start=True, stop=False)
                sm.inc(t.matmul(ps_pref[tt % 2][:], c["ones_f"][:],
                                offs_row[:, C * tt:C * (tt + 1)], start=False, stop=True))
        @b.vector
        def _(v):
            for tt in range(NTT):
                v.wait_ge(sm.h, 2 + tt + 1)
                se.inc(v.tensor_copy(pvv[:, tt], ps_pref[tt % 2][:]))
    with p.block() as b:
        @b.vector
        def _(v):
            pass
    ar.close()
    return pref_tm, cnt_row


def emit_counts_row(p, c, tot_row, C, cnt_row, tag):
    with p.block() as b:
        @b.vector
        def _(v):
            v.tensor_reduce(cnt_row, tot_row.ap().rearrange("p (tt c) -> p c tt", tt=NTT),
                            axis=AX.X, op=ALU.add)


def emit_keep(p, c, oh, pref_tm, cap, keep_out, tag):
    with p.block() as b:
        @b.vector
        def _(v):
            v.tensor_scalar(keep_out[:], pref_tm[:], float(cap - 1), None, op0=ALU.is_le)
    with p.block() as b:
        @b.vector
        def _(v):
            v.tensor_tensor(keep_out[:], keep_out[:], oh[:], op=ALU.mult)



def emit_proj_fm(p, c, w_sb, src_bf, dst, tag, nch_in=NCH, nch_out=NCH,
                 col_jobs=None, accum_into=None, dst_bf=True, src_t=T,
                 evac_scale=None):
    if col_jobs is None:
        col_jobs = [(j * 512, j * 512, 512) for j in range(src_t // 512)]
    ar = ExitStack()
    ps = [p.psum(f"{tag}_ps{i}", [128, 512], F32, ar) for i in range(2)]
    sm = p.sem()
    se = p.sem()
    wv = w_sb.ap().rearrange("p (ci oc) -> p ci oc", ci=nch_in)
    sv = chunks(src_bf, nch_in)
    i = 0
    with p.block() as b:
        @b.tensor
        def _(t):
            nonlocal i
            for oc in range(nch_out):
                for (slo, dlo, n) in col_jobs:
                    if i >= 2:
                        t.wait_ge(se.h, i - 1)
                    for ci in range(nch_in):
                        inst = t.matmul(ps[i % 2][:, 0:n],
                                        wv[:, ci, 128 * oc:128 * (oc + 1)],
                                        sv[:, ci, slo:slo + n],
                                        start=(ci == 0), stop=(ci == nch_in - 1))
                    sm.inc(inst)
                    i += 1
        @b.vector
        def _(v):
            j = 0
            dv = chunks(dst, nch_out) if dst is not None else None
            xv = chunks(accum_into, nch_out) if accum_into is not None else None
            for oc in range(nch_out):
                for (slo, dlo, n) in col_jobs:
                    v.wait_ge(sm.h, j + 1)
                    if accum_into is not None:
                        se.inc(v.tensor_tensor(xv[:, oc, dlo:dlo + n],
                                               xv[:, oc, dlo:dlo + n],
                                               ps[j % 2][:, 0:n], op=ALU.add))
                    elif evac_scale is not None:
                        se.inc(v.tensor_scalar_mul(dv[:, oc, dlo:dlo + n],
                                                   ps[j % 2][:, 0:n], float(evac_scale)))
                    else:
                        se.inc(v.tensor_copy(dv[:, oc, dlo:dlo + n], ps[j % 2][:, 0:n]))
                    j += 1
    with p.block() as b:
        @b.vector
        def _(v):
            pass
    ar.close()


def emit_wload(p, queue_eng_sem, w_hbm, w_sb, cast=True):
    raise NotImplementedError



def build_L0(dbg=False):
    p = Prog()
    p.dbg = dbg
    E, K, CAP = 32, 4, 8192
    QPAD = 1024
    NQT = (T - QPAD) // 128
    x_h = p.din("x", [D, T], F32)
    wq_h = p.din("wq", [D, D], F32)
    wk_h = p.din("wk", [D, D], F32)
    wv_h = p.din("wv", [D, D], F32)
    wo_h = p.din("wo", [D, D], F32)
    wg_h = p.din("wg", [D, E], F32)
    w1_h = p.din("w1", [E, D, 512], F32)
    w2_h = p.din("w2", [E, 512, D], F32)
    w3_h = p.din("w3", [E, D, 512], F32)
    ws1_h = p.din("ws1", [D, D], F32)
    ws2_h = p.din("ws2", [D, D], F32)
    ws3_h = p.din("ws3", [D, D], F32)
    cnts_h = p.din("cnts", [8, E], F32)
    pmask_h = p.din("pmask", [8, 1], F32)
    y_h = p.dout("y", [D, T], F32)

    c = emit_consts(p, ident_bf=True, ident_f32=True, tril_s=True)
    x_sb = p.sbuf("x_sb", [128, NCH * T], F32)
    xn_bf = p.sbuf("xn_bf", [128, NCH * T], BF16)
    w_tm = p.sbuf("w_tm", [128, NTT * E], F32)

    emit_load_fm(p, x_h, x_sb)
    ln_ar = ExitStack()
    ln = emit_ln(p, c, x_sb, xn_bf, "ln", arena=ln_ar)

    ar_r = ExitStack()
    s_tm = p.sbuf("s_tm", [128, NTT * E], F32, ar_r)
    wg_sb = p.sbuf("wg_sb", [128, NCH * E], F32, ar_r)
    cnts_sb = p.sbuf("cnts_sb", [8, E], F32, ar_r)
    pmask_sb = p.sbuf("pmask_sb", [8, 1], F32, ar_r)
    coff_sb = p.sbuf("coff_sb", [1, E], F32, ar_r)
    ps_coff = p.psum("ps_coff", [1, E], F32, ar_r)
    ldr = p.sem()
    with p.block() as b:
        @b.sync
        def _(sy):
            ldr.dma(sy.dma_start(out=wg_sb.ap().rearrange("p (ci e) -> p ci e", ci=NCH),
                                 in_=wg_h.rearrange("(ci p) e -> p ci e", p=128)))
            ldr.dma(sy.dma_start(out=cnts_sb[:], in_=cnts_h))
            ldr.dma(sy.dma_start(out=pmask_sb[:], in_=pmask_h))
        @b.tensor
        def _(t):
            t.wait_ge(ldr.h, 48)
            t.matmul(ps_coff[:], pmask_sb[:], cnts_sb[:], start=True, stop=True).then_inc(ldr.h, 1)
        @b.vector
        def _(v):
            v.wait_ge(ldr.h, 49)
            v.tensor_copy(coff_sb[:], ps_coff[:])
    emit_scores(p, c, x_sb, ln, wg_sb, E, s_tm, "sc")
    emit_softmax_tm(p, c, s_tm, E, "sm")
    ohs, vals = emit_topk_tm(p, c, s_tm, E, K, "tk", persist_arena=ar_r)
    with p.block() as b:
        @b.vector
        def _(v):
            v.tensor_tensor(s_tm[:], ohs[0][:], ohs[1][:], op=ALU.add)
    with p.block() as b:
        @b.vector
        def _(v):
            v.tensor_tensor(s_tm[:], s_tm[:], ohs[2][:], op=ALU.add)
    with p.block() as b:
        @b.vector
        def _(v):
            v.tensor_tensor(s_tm[:], s_tm[:], ohs[3][:], op=ALU.add)
    pref_tm, _cnt = emit_prefix_tokens(p, c, s_tm, E, coff_sb[:], "pf", arena=ar_r)
    with p.block() as b:
        @b.vector
        def _(v):
            v.tensor_scalar(s_tm[:], pref_tm[:], float(CAP - 1), None, op0=ALU.is_le)
    with p.block() as b:
        @b.vector
        def _(v):
            v.memset(w_tm[:], 0.0)
    kp = p.sbuf("kp", [128, NTT * E], F32, ar_r)
    for k in range(K):
        with p.block() as b:
            @b.vector
            def _(v):
                v.tensor_tensor(kp[:], s_tm[:], ohs[k][:], op=ALU.mult)
        with p.block() as b:
            @b.vector
            def _(v):
                kv = tiles(kp, E)
                wv_ = tiles(w_tm, E)
                for tt in range(NTT):
                    v.scalar_tensor_tensor(wv_[:, tt], kv[:, tt], vals[k][:, tt:tt + 1],
                                           wv_[:, tt], op0=ALU.mult, op1=ALU.add)
    with p.block() as b:
        @b.vector
        def _(v):
            pass
    ar_r.close()
    ln_ar.close()

    emit_l0_attn(p, c, x_sb, xn_bf, wq_h, wk_h, wv_h, wo_h, QPAD)

    emit_l0_shared(p, c, x_sb, xn_bf, ws1_h, ws2_h, ws3_h)

    emit_l0_moe(p, c, x_sb, xn_bf, w_tm, w1_h, w2_h, w3_h, E)

    st = p.sem()
    with p.block() as b:
        @b.gpsimd
        def _(g):
            st.dma(g.dma_start(out=y_h.rearrange("(cc p) t -> p cc t", p=128),
                               in_=chunks(x_sb)))
            g.wait_ge(st.h, 16)
    return p




def emit_l0_attn(p, c, x_sb, xn_bf, wq_h, wk_h, wv_h, wo_h, QPAD):
    NH, DH = 16, 64
    NQT = (T - QPAD) // 128
    QW = T - QPAD
    WIN0 = 1536
    SINKS = 4
    HG = 2
    ar = ExitStack()
    w_sb = p.sbuf("a_w", [128, NCH * D], BF16, ar)
    q_fm = p.sbuf("a_q", [128, NCH * QW], BF16, ar)
    k_fm = p.sbuf("a_k", [128, NCH * 516], BF16, ar)
    v_fm = p.sbuf("a_v", [128, NCH * 516], BF16, ar)
    vt = p.sbuf("a_vt", [128, NH * 5 * DH], BF16, ar)
    o_fm = p.sbuf("a_o", [128, NCH * QW], BF16, ar)
    masks = p.sbuf("a_masks", [128, NQT * 512], BF16, ar)
    smask = p.sbuf("a_smask", [128, NQT * SINKS], BF16, ar)
    iota512 = p.sbuf("a_iota", [128, 512], I32, ar)
    sc_s = p.sbuf("a_scs", [128, HG * 516], F32, ar)
    pr_bf = p.sbuf("a_prb", [128, HG * 516], BF16, ar)
    pr_t = p.sbuf("a_prt", [128, HG * 5 * 128], BF16, ar)
    zcol = p.sbuf("a_z", [128, HG], F32, ar)
    mcol = p.sbuf("a_m", [128, 2 * HG], F32, ar)

    with p.block() as b:
        @b.gpsimd
        def _(g):
            g.iota(iota512[:], [[1, 512]], channel_multiplier=-1,
                   allow_small_or_imprecise_dtypes=True)
    with p.block() as b:
        @b.vector
        def _(v):
            mv = masks.ap().rearrange("p (q j) -> p q j", q=NQT)
            sv = smask.ap().rearrange("p (q j) -> p q j", q=NQT)
            for qt in range(NQT):
                q0 = QPAD + 128 * qt
                v.tensor_scalar(mv[:, qt], iota512[:], q0 - WIN0, None, op0=ALU.is_le)
                v.tensor_scalar(sv[:, qt], iota512[:, 0:SINKS], q0 - 1532, None, op0=ALU.is_le)
    with p.block() as b:
        @b.vector
        def _(v):
            v.tensor_scalar(masks[:], masks[:], 1.0e38, 1.0e38, op0=ALU.mult, op1=ALU.subtract)
            v.tensor_scalar(smask[:], smask[:], 1.0e38, 1.0e38, op0=ALU.mult, op1=ALU.subtract)

    def load_w(wh):
        ld2 = p.sem()
        with p.block() as b:
            @b.gpsimd
            def _(g):
                ld2.dma(g.dma_start(out=w_sb.ap().rearrange("p (ci o) -> p ci o", ci=NCH),
                                    in_=wh.rearrange("(ci p) o -> p ci o", p=128)))
            @b.tensor
            def _(t):
                t.wait_ge(ld2.h, 16)

    load_w(wq_h)
    emit_proj_fm(p, c, w_sb, xn_bf, q_fm, "a_pq", evac_scale=0.125,
                 col_jobs=[(QPAD + 512 * j, 512 * j, 512) for j in range(QW // 512)])
    load_w(wk_h)
    emit_proj_fm(p, c, w_sb, xn_bf, k_fm, "a_pk",
                 col_jobs=[(0, 0, SINKS), (WIN0, SINKS, 512)])
    load_w(wv_h)
    emit_proj_fm(p, c, w_sb, xn_bf, v_fm, "a_pv",
                 col_jobs=[(0, 0, SINKS), (WIN0, SINKS, 512)])

    str_ = p.sem()
    sev = p.sem()
    ar_vt = ExitStack()
    ps_vt = [p.psum(f"a_psvt{i}", [128, DH], BF16, ar_vt) for i in range(2)]
    vtv = vt.ap().rearrange("p (h kb d) -> p h kb d", h=NH, kb=5)
    with p.block() as b:
        @b.tensor
        def _(t):
            i = 0
            for h in range(NH):
                ci, base = (h * DH) // 128, (h * DH) % 128
                for kb in range(5):
                    if i >= 2:
                        t.wait_ge(sev.h, i - 1)
                    ident = (c["ident_bf"][0:DH, 0:DH] if base == 0
                             else c["ident_bf64"][64:128, 0:DH])
                    if kb < 4:
                        src = v_fm[base:base + DH,
                                   ci * 516 + SINKS + 128 * kb: ci * 516 + SINKS + 128 * (kb + 1)]
                        str_.inc(t.transpose(ps_vt[i % 2][:, :], src, ident,
                                             tile_position=(base, 0)))
                    else:
                        src = v_fm[base:base + DH, ci * 516: ci * 516 + SINKS]
                        str_.inc(t.transpose(ps_vt[i % 2][0:SINKS, :], src, ident,
                                             tile_position=(base, 0)))
                    i += 1
        @b.vector
        def _(v):
            i = 0
            for h in range(NH):
                for kb in range(5):
                    v.wait_ge(str_.h, i + 1)
                    if kb < 4:
                        sev.inc(v.tensor_copy(vtv[:, h, kb], ps_vt[i % 2][:, :]))
                    else:
                        sev.inc(v.tensor_copy(vtv[0:SINKS, h, kb], ps_vt[i % 2][0:SINKS, :]))
                    i += 1

    with p.block() as b:
        @b.vector
        def _(v):
            pass
    ar_vt.close()
    ar_ps = ExitStack()
    ps_sc = p.psum("a_pssc", [128, HG * 1024], F32, ar_ps)
    ps_pt = p.psum("a_pspt", [128, HG * 5 * 128], BF16, ar_ps)
    ps_o = p.psum("a_pso", [128, HG * 128], F32, ar_ps)
    s_sca = p.sem(); s_msk = p.sem(); s_mskr = p.sem(); s_max = p.sem()
    s_exp = p.sem(); s_rcp = p.sem(); s_scl = p.sem(); s_tr = p.sem()
    s_pte = p.sem(); s_av = p.sem(); s_oev = p.sem()
    ov = chunks(o_fm)
    mv = masks.ap().rearrange("p (q j) -> p q j", q=NQT)
    sv2 = smask.ap().rearrange("p (q j) -> p q j", q=NQT)
    jobs = [(qt, hp) for qt in range(NQT) for hp in range(NH // HG)]
    NJ = len(jobs)
    def head_info(hp, g):
        h = hp * HG + g
        return h, (h * DH) // 128, (h * DH) % 128

    def emit_tr_av(t, i):
        qt, hp = jobs[i]
        t.wait_ge(s_scl.h, HG * (i + 1))
        if i >= 1:
            t.wait_ge(s_pte.h, i)
        for g in range(HG):
            for kb in range(4):
                t.transpose(ps_pt[:, (5 * g + kb) * 128:(5 * g + kb) * 128 + 128],
                            pr_bf[:, 516 * g + 128 * kb:516 * g + 128 * (kb + 1)],
                            c["ident_bf"][:])
            inst = t.transpose(ps_pt[0:SINKS, (5 * g + 4) * 128:(5 * g + 4) * 128 + 128],
                               pr_bf[:, 516 * g + 512:516 * g + 516], c["ident_bf"][:])
        s_tr.inc(inst)
        t.wait_ge(s_pte.h, i + 1)
        if i >= 1:
            t.wait_ge(s_oev.h, i)
        for g in range(HG):
            h, ci, base = head_info(hp, g)
            for kb in range(4):
                t.matmul(ps_o[0:DH, 128 * g:128 * (g + 1)], vtv[:, h, kb],
                         pr_t[:, (5 * g + kb) * 128:(5 * g + kb) * 128 + 128],
                         start=(kb == 0), stop=False)
            inst = t.matmul(ps_o[0:DH, 128 * g:128 * (g + 1)], vtv[0:SINKS, h, 4],
                            pr_t[0:SINKS, (5 * g + 4) * 128:(5 * g + 4) * 128 + 128],
                            start=False, stop=True)
        s_av.inc(inst)

    with p.block() as b:
        @b.tensor
        def _(t):
            for i, (qt, hp) in enumerate(jobs):
                if i >= 1:
                    t.wait_ge(s_msk.h, 2 * HG * i)
                for g in range(HG):
                    h, ci, base = head_info(hp, g)
                    qsl = q_fm[base:base + DH, ci * QW + 128 * qt: ci * QW + 128 * (qt + 1)]
                    t.matmul(ps_sc[:, 1024 * g:1024 * g + 512], qsl,
                             k_fm[base:base + DH, ci * 516 + SINKS: ci * 516 + 516],
                             start=True, stop=True, tile_position=(base, 0))
                    inst = t.matmul(ps_sc[:, 1024 * g + 512:1024 * g + 516], qsl,
                                    k_fm[base:base + DH, ci * 516: ci * 516 + SINKS],
                                    start=True, stop=True, tile_position=(base, 0))
                s_sca.inc(inst)
                if i >= 1:
                    emit_tr_av(t, i - 1)
            emit_tr_av(t, NJ - 1)

        @b.vector
        def _(v):
            scv = sc_s.ap().rearrange("p (g x) -> p g x", g=HG)
            for i, (qt, hp) in enumerate(jobs):
                v.wait_ge(s_sca.h, i + 1)
                if i >= 1:
                    v.wait_ge(s_scl.h, HG * i)
                for g in range(HG):
                    v.tensor_tensor(scv[:, g, 0:512], ps_sc[:, 1024 * g:1024 * g + 512],
                                    mv[:, qt], op=ALU.add)
                    s_msk.inc(v.tensor_tensor(scv[:, g, 512:516],
                                              ps_sc[:, 1024 * g + 512:1024 * g + 516],
                                              sv2[:, qt], op=ALU.add), 2)

                v.wait_ge(s_mskr.h, i + 1)
                if i >= 1:
                    v.wait_ge(s_exp.h, HG * i)
                s_max.inc(v.tensor_reduce(mcol[:, 0:HG], scv[:, :, 0:516], axis=AX.X,
                                          op=ALU.max, negate=True))
                v.wait_ge(s_exp.h, HG * (i + 1))
                s_rcp.inc(v.reciprocal(zcol[:], zcol[:]))
                v.wait_ge(s_tr.h, i + 1)
                if i >= 1:
                    v.wait_ge(s_av.h, i)
                for g in range(HG):
                    v.tensor_copy(pr_t[:, 5 * g * 128:5 * g * 128 + 512],
                                  ps_pt[:, 5 * g * 128:5 * g * 128 + 512])
                    inst = v.tensor_copy(pr_t[0:SINKS, (5 * g + 4) * 128:(5 * g + 4) * 128 + 128],
                                         ps_pt[0:SINKS, (5 * g + 4) * 128:(5 * g + 4) * 128 + 128])
                s_pte.inc(inst)

        @b.gpsimd
        def _(g_):
            for i, (qt, hp) in enumerate(jobs):
                g_.wait_ge(s_rcp.h, i + 1)
                if i >= 1:
                    g_.wait_ge(s_tr.h, i)
                for g in range(HG):
                    inst = g_.tensor_scalar(pr_bf[:, 516 * g:516 * g + 516],
                                            sc_s[:, 516 * g:516 * g + 516],
                                            zcol[:, g:g + 1], None, op0=ALU.mult)
                s_scl.inc(inst, HG)

        @b.sync
        def _(sp):
            for i in range(NJ):
                sp.wait_ge(s_msk.h, 2 * HG * (i + 1))
                sp.nop().then_inc(s_mskr.h, 1)

        @b.scalar
        def _(s):
            scv = sc_s.ap().rearrange("p (g x) -> p g x", g=HG)
            for i, (qt, hp) in enumerate(jobs):
                s.wait_ge(s_max.h, i + 1)
                if i >= 1:
                    s.wait_ge(s_scl.h, HG * i)
                for g in range(HG):
                    s_exp.inc(s.activation(out=scv[:, g, 0:516], in_=scv[:, g, 0:516],
                                           func=AF.Exp, bias=mcol[:, g:g + 1],
                                           accum_out=zcol[:, g:g + 1]))
                s.wait_ge(s_av.h, i + 1)
                for g in range(HG):
                    h, ci, base = head_info(hp, g)
                    inst = s.copy(ov[base:base + DH, ci, 128 * qt:128 * (qt + 1)],
                                  ps_o[0:DH, 128 * g:128 * (g + 1)])
                s_oev.inc(inst)

    if getattr(p, "dbg", False):
        dbg_o = p.dout("dbg_o", [128, NCH * QW], BF16)
        dbg_q = p.dout("dbg_q", [128, NCH * QW], BF16)
        dbg_k = p.dout("dbg_k", [128, NCH * 516], BF16)
        dbg_v = p.dout("dbg_v", [128, NCH * 516], BF16)
        sdbg = p.sem()
        with p.block() as b:
            @b.gpsimd
            def _(g):
                sdbg.dma(g.dma_start(out=dbg_o, in_=o_fm[:]))
                sdbg.dma(g.dma_start(out=dbg_q, in_=q_fm[:]))
                sdbg.dma(g.dma_start(out=dbg_k, in_=k_fm[:]))
                sdbg.dma(g.dma_start(out=dbg_v, in_=v_fm[:]))
                g.wait_ge(sdbg.h, sdbg.n)
    with p.block() as b:
        @b.vector
        def _(v):
            nz = 1532 - QPAD
            for cc in range(NCH):
                v.memset(ov[:, cc, 0:nz], 0.0)
    ar_ps.close()
    load_w(wo_h)
    emit_proj_fm(p, c, w_sb, o_fm, None, "a_po", src_t=QW,
                 col_jobs=[(512 * j, QPAD + 512 * j, 512) for j in range(QW // 512)],
                 accum_into=x_sb)
    with p.block() as b:
        @b.vector
        def _(v):
            pass
    ar.close()



def emit_l0_shared(p, c, x_sb, xn_bf, ws1_h, ws2_h, ws3_h):
    ar = ExitStack()
    w1_sb = p.sbuf("sh_w1", [128, NCH * D], BF16, ar)
    w3_sb = p.sbuf("sh_w3", [128, NCH * D], BF16, ar)
    g_bf = p.sbuf("sh_g", [128, NCH * T], BF16, ar)
    t1 = p.sbuf("sh_t1", [128, 2 * 512], BF16, ar)
    ps1 = [p.psum(f"sh_ps1{i}", [128, 512], F32, ar) for i in range(2)]
    ps2 = [p.psum(f"sh_ps2{i}", [128, 512], F32, ar) for i in range(2)]
    ld = p.sem()
    sm1 = p.sem()
    sm2 = p.sem()
    ssi = p.sem()
    sg = p.sem()
    w1v = w1_sb.ap().rearrange("p (ci o) -> p ci o", ci=NCH)
    w3v = w3_sb.ap().rearrange("p (ci o) -> p ci o", ci=NCH)
    xnc = chunks(xn_bf)
    gc = chunks(g_bf)
    with p.block() as b:
        @b.gpsimd
        def _(g):
            ld.dma(g.dma_start(out=w1v, in_=ws1_h.rearrange("(ci p) o -> p ci o", p=128)))
            ld.dma(g.dma_start(out=w3v, in_=ws3_h.rearrange("(ci p) o -> p ci o", p=128)))
        @b.tensor
        def _(t):
            t.wait_ge(ld.h, 32)
    jobs = [(hc, sl) for hc in range(NCH) for sl in range(4)]
    with p.block() as b:
        @b.tensor
        def _(t):
            for i, (hc, sl) in enumerate(jobs):
                if i >= 2:
                    t.wait_ge(ssi.h, i - 1)
                for ci in range(NCH):
                    inst = t.matmul(ps1[i % 2][:], w1v[:, ci, 128 * hc:128 * (hc + 1)],
                                    xnc[:, ci, 512 * sl:512 * (sl + 1)],
                                    start=(ci == 0), stop=(ci == NCH - 1))
                sm1.inc(inst)
                if i >= 2:
                    t.wait_ge(sg.h, i - 1)
                for ci in range(NCH):
                    inst = t.matmul(ps2[i % 2][:], w3v[:, ci, 128 * hc:128 * (hc + 1)],
                                    xnc[:, ci, 512 * sl:512 * (sl + 1)],
                                    start=(ci == 0), stop=(ci == NCH - 1))
                sm2.inc(inst)
        @b.scalar
        def _(s):
            for i, (hc, sl) in enumerate(jobs):
                s.wait_ge(sm1.h, i + 1)
                if i >= 2:
                    s.wait_ge(sg.h, i - 1)
                ssi.inc(s.activation(out=t1[:, 512 * (i % 2):512 * (i % 2) + 512],
                                     in_=ps1[i % 2][:], func=AF.Silu))
        @b.vector
        def _(v):
            for i, (hc, sl) in enumerate(jobs):
                v.wait_ge(ssi.h, i + 1)
                v.wait_ge(sm2.h, i + 1)
                sg.inc(v.tensor_tensor(gc[:, hc, 512 * sl:512 * (sl + 1)],
                                       t1[:, 512 * (i % 2):512 * (i % 2) + 512],
                                       ps2[i % 2][:], op=ALU.mult))
    w2_sb = w1_sb
    ld2 = p.sem()
    with p.block() as b:
        @b.gpsimd
        def _(g):
            ld2.dma(g.dma_start(out=w1v, in_=ws2_h.rearrange("(hc p) o -> p hc o", p=128)))
        @b.tensor
        def _(t):
            t.wait_ge(ld2.h, 16)
    emit_proj_fm(p, c, w1_sb, g_bf, None, "sh_z", accum_into=x_sb)
    with p.block() as b:
        @b.vector
        def _(v):
            pass
    ar.close()


def emit_l0_moe(p, c, x_sb, xn_bf, w_tm, w1_h, w2_h, w3_h, E):
    ar = ExitStack()
    wa = [{
        "w1": p.sbuf(f"m_w1{i}", [128, NCH * 512], BF16, ar),
        "w3": p.sbuf(f"m_w3{i}", [128, NCH * 512], BF16, ar),
        "w2": p.sbuf(f"m_w2{i}", [128, 4 * D], BF16, ar),
    } for i in range(2)]
    y_acc = p.sbuf("m_yacc", [128, 8 * D], F32, ar)
    g_q = p.sbuf("m_g", [128, 4 * 512], BF16, ar)
    t1 = p.sbuf("m_t1", [128, 512], BF16, ar)
    ps_h1 = p.psum("m_ph1", [128, 512], F32, ar)
    ps_h3 = p.psum("m_ph3", [128, 512], F32, ar)
    ps_y = [p.psum(f"m_py{i}", [128, D], F32, ar) for i in range(2)]
    ps_tr = [p.psum(f"m_ptr{i}", [128, 128], F32, ar) for i in range(2)]
    xnc = chunks(xn_bf)
    xc = chunks(x_sb)
    wv_ = tiles(w_tm, E)
    yv = y_acc.ap().rearrange("p (tt o) -> p tt o", tt=8)
    gq = g_q.ap().rearrange("p (hb t) -> p hb t", hb=4)

    for half in range(2):
        t_base = half * 1024
        s_ld = [p.sem(), p.sem()]
        s_h1 = p.sem(); s_h3 = p.sem(); s_si = p.sem()
        s_g = p.sem(); s_y = p.sem(); s_f = p.sem(); s_tp = p.sem(); s_te = p.sem()

        def wload(g_eng, e, buf):
            s_ld[buf].dma(g_eng.dma_start(out=wa[buf]["w1"].ap().rearrange("p (ci h) -> p ci h", ci=NCH),
                                          in_=w1_h[e].rearrange("(ci p) h -> p ci h", p=128)))
            s_ld[buf].dma(g_eng.dma_start(out=wa[buf]["w3"].ap().rearrange("p (ci h) -> p ci h", ci=NCH),
                                          in_=w3_h[e].rearrange("(ci p) h -> p ci h", p=128)))
            s_ld[buf].dma(g_eng.dma_start(out=wa[buf]["w2"].ap().rearrange("p (hc o) -> p hc o", hc=4),
                                          in_=w2_h[e].rearrange("(hc p) o -> p hc o", p=128)))

        with p.block() as b:
            @b.vector
            def _(v):
                v.memset(y_acc[:], 0.0)
            @b.gpsimd
            def _(g):
                wload(g, 0, 0)
        jobs = [(e, q) for e in range(E) for q in range(2)]
        with p.block() as b:
            @b.gpsimd
            def _(g):
                for e in range(1, E):
                    if e >= 2:
                        g.wait_ge(s_y.h, 8 * (e - 1))
                    wload(g, e, e % 2)
            @b.tensor
            def _(t):
                for i, (e, q) in enumerate(jobs):
                    wb = wa[e % 2]
                    w1v = wb["w1"].ap().rearrange("p (ci h) -> p ci h", ci=NCH)
                    w3v = wb["w3"].ap().rearrange("p (ci h) -> p ci h", ci=NCH)
                    w2v = wb["w2"].ap().rearrange("p (hc o) -> p hc o", hc=4)
                    if q == 0:
                        t.wait_ge(s_ld[e % 2].h, 48 * (e // 2 + 1))
                    c0 = t_base + 512 * q
                    for hb in range(4):
                        if 4 * i + hb >= 1:
                            t.wait_ge(s_si.h, 4 * i + hb)
                        for ci in range(NCH):
                            inst = t.matmul(ps_h1[:], w1v[:, ci, 128 * hb:128 * (hb + 1)],
                                            xnc[:, ci, c0:c0 + 512],
                                            start=(ci == 0), stop=(ci == NCH - 1))
                        s_h1.inc(inst)
                        if 4 * i + hb >= 1:
                            t.wait_ge(s_g.h, 4 * i + hb)
                        for ci in range(NCH):
                            inst = t.matmul(ps_h3[:], w3v[:, ci, 128 * hb:128 * (hb + 1)],
                                            xnc[:, ci, c0:c0 + 512],
                                            start=(ci == 0), stop=(ci == NCH - 1))
                        s_h3.inc(inst)
                    t.wait_ge(s_g.h, 4 * (i + 1))
                    for sub in range(4):
                        if 4 * i + sub >= 2:
                            t.wait_ge(s_f.h, 4 * i + sub - 1)
                        for hb in range(4):
                            for o2 in range(2):
                                inst = t.matmul(ps_y[sub % 2][:, 512 * o2:512 * (o2 + 1)],
                                                gq[:, hb, 128 * sub:128 * (sub + 1)],
                                                w2v[:, hb, 512 * o2:512 * (o2 + 1)],
                                                start=(hb == 0), stop=(hb == 3))
                        s_y.inc(inst)
            @b.scalar
            def _(s):
                for i, (e, q) in enumerate(jobs):
                    for hb in range(4):
                        s.wait_ge(s_h1.h, 4 * i + hb + 1)
                        if 4 * i + hb >= 1:
                            s.wait_ge(s_g.h, 4 * i + hb)
                        s_si.inc(s.activation(out=t1[:], in_=ps_h1[:], func=AF.Silu))
            @b.vector
            def _(v):
                for i, (e, q) in enumerate(jobs):
                    for hb in range(4):
                        v.wait_ge(s_si.h, 4 * i + hb + 1)
                        v.wait_ge(s_h3.h, 4 * i + hb + 1)
                        if i >= 1 and hb == 0:
                            v.wait_ge(s_y.h, 4 * i)
                        s_g.inc(v.tensor_tensor(gq[:, hb], t1[:], ps_h3[:], op=ALU.mult))
                    for sub in range(4):
                        v.wait_ge(s_y.h, 4 * i + sub + 1)
                        tt = 4 * q + sub
                        s_f.inc(v.scalar_tensor_tensor(
                            yv[:, tt], ps_y[sub % 2][:],
                            wv_[:, 8 * half + tt, e:e + 1], yv[:, tt],
                            op0=ALU.mult, op1=ALU.add))
        with p.block() as b:
            @b.tensor
            def _(t):
                i = 0
                for tt in range(8):
                    for oc in range(NCH):
                        if i >= 2:
                            t.wait_ge(s_te.h, i - 1)
                        s_tp.inc(t.transpose(ps_tr[i % 2][:],
                                             yv[:, tt, 128 * oc:128 * (oc + 1)],
                                             c["ident_f"][:]))
                        i += 1
            @b.vector
            def _(v):
                i = 0
                for tt in range(8):
                    for oc in range(NCH):
                        v.wait_ge(s_tp.h, i + 1)
                        dst = xc[:, oc, t_base + 128 * tt:t_base + 128 * (tt + 1)]
                        s_te.inc(v.tensor_tensor(dst, dst, ps_tr[i % 2][:], op=ALU.add))
                        i += 1
    with p.block() as b:
        @b.vector
        def _(v):
            pass
    ar.close()



def build_L1():
    p = Prog()
    E, K, CAP = 8, 2, 2560
    NH, DH = 16, 64
    x_h = p.din("x", [D, T], F32)
    wq_h = p.din("wq", [NH, D, DH], F32)
    wk_h = p.din("wk", [NH, D, DH], F32)
    wvd_h = p.din("wvd", [NH, D, DH], F32)
    wvu_h = p.din("wvu", [NH, DH, D], F32)
    wr_h = p.din("wr", [D, E], F32)
    w1_h = p.din("w1", [E, D, 4096], F32)
    b1_h = p.din("b1", [128, 8 * 32], F32)
    w2_h = p.din("w2", [E, 4096, D], F32)
    b2_h = p.din("b2", [E, D], F32)
    cnts_h = p.din("cnts", [8, K * E], F32)
    pmask_h = p.din("pmask", [8, 1], F32)
    y_h = p.dout("y", [D, T], F32)

    moe_hbm = p.nc.dram_tensor("moe_scr", [T, D], BF16).ap()
    c = emit_consts(p, ident_bf=True, ident_f32=True, tril_s=True)
    xn_bf = p.sbuf("xn_bf", [128, NCH * T], BF16)
    w_tm = p.sbuf("w_tm", [128, NTT * E], F32)
    ar_x1 = ExitStack()
    x_sb = p.sbuf("x_sb", [128, NCH * T], F32, ar_x1)

    emit_load_fm(p, x_h, x_sb)
    ln_ar = ExitStack()
    ln = emit_ln(p, c, x_sb, xn_bf, "ln", arena=ln_ar)

    ar_r = ExitStack()
    s_tm = p.sbuf("s_tm", [128, NTT * E], F32, ar_r)
    wr_sb = p.sbuf("wr_sb", [128, NCH * E], F32, ar_r)
    cnts_sb = p.sbuf("cnts_sb", [8, K * E], F32, ar_r)
    pmask_sb = p.sbuf("pmask_sb", [8, 1], F32, ar_r)
    coff_sb = p.sbuf("coff_sb", [1, K * E], F32, ar_r)
    ps_coff = p.psum("ps_coff", [1, K * E], F32, ar_r)
    tvs = p.sbuf("tvs", [128, NTT], F32, ar_r)
    ldr = p.sem()
    with p.block() as b:
        @b.sync
        def _(sy):
            ldr.dma(sy.dma_start(out=wr_sb.ap().rearrange("p (ci e) -> p ci e", ci=NCH),
                                 in_=wr_h.rearrange("(ci p) e -> p ci e", p=128)))
            ldr.dma(sy.dma_start(out=cnts_sb[:], in_=cnts_h))
            ldr.dma(sy.dma_start(out=pmask_sb[:], in_=pmask_h))
        @b.tensor
        def _(t):
            t.wait_ge(ldr.h, 48)
            t.matmul(ps_coff[:], pmask_sb[:], cnts_sb[:], start=True, stop=True).then_inc(ldr.h, 1)
        @b.vector
        def _(v):
            v.wait_ge(ldr.h, 49)
            v.tensor_copy(coff_sb[:], ps_coff[:])
    emit_scores(p, c, x_sb, ln, wr_sb, E, s_tm, "sc")
    emit_softmax_tm(p, c, s_tm, E, "sm")
    ohs, vals = emit_topk_tm(p, c, s_tm, E, K, "tk", persist_arena=ar_r)
    with p.block() as b:
        @b.vector
        def _(v):
            v.tensor_tensor(tvs[:], vals[0][:], vals[1][:], op=ALU.add)
    with p.block() as b:
        @b.vector
        def _(v):
            v.reciprocal(tvs[:], tvs[:])
    with p.block() as b:
        @b.gpsimd
        def _(g):
            g.tensor_tensor(vals[0][:], vals[0][:], tvs[:], op=ALU.mult)
            g.tensor_tensor(vals[1][:], vals[1][:], tvs[:], op=ALU.mult)
    with p.block() as b:
        @b.vector
        def _(v):
            v.memset(w_tm[:], 0.0)
    for k in range(K):
        pref_tm, _ = emit_prefix_tokens(p, c, ohs[k], E, coff_sb[0:1, E * k:E * (k + 1)],
                                        f"pf{k}", arena=ar_r)
        with p.block() as b:
            @b.vector
            def _(v):
                v.tensor_scalar(pref_tm[:], pref_tm[:], float(CAP - 1), None, op0=ALU.is_le)
        with p.block() as b:
            @b.vector
            def _(v):
                v.tensor_tensor(pref_tm[:], pref_tm[:], ohs[k][:], op=ALU.mult)
        with p.block() as b:
            @b.vector
            def _(v):
                kv = tiles(pref_tm, E)
                wv_ = tiles(w_tm, E)
                for tt in range(NTT):
                    v.scalar_tensor_tensor(wv_[:, tt], kv[:, tt], vals[k][:, tt:tt + 1],
                                           wv_[:, tt], op0=ALU.mult, op1=ALU.add)
    with p.block() as b:
        @b.vector
        def _(v):
            pass
    ar_r.close()
    ln_ar.close()
    ar_x1.close()

    if STOP in ("routing",):
        return p
    emit_l1_moe(p, c, xn_bf, w_tm, moe_hbm, w1_h, b1_h, w2_h, b2_h, E)
    if STOP == "moe":
        return p

    ar_x2 = ExitStack()
    x_sb = p.sbuf("x_sb2", [128, NCH * T], F32, ar_x2)
    emit_load_fm(p, x_h, x_sb)
    emit_l1_attn(p, c, x_sb, xn_bf, moe_hbm, wq_h, wk_h, wvd_h, wvu_h)
    if STOP is not None:
        return p

    st = p.sem()
    with p.block() as b:
        @b.gpsimd
        def _(g):
            st.dma(g.dma_start(out=y_h.rearrange("(cc p) t -> p cc t", p=128),
                               in_=chunks(x_sb)))
            g.wait_ge(st.h, 16)
    ar_x2.close()
    return p


def emit_l1_moe(p, c, xn_bf, w_tm, moe_hbm, w1_h, b1_h, w2_h, b2_h, E):
    ar = ExitStack()
    w1_sb = p.sbuf("m1_w1", [128, NCH * 4096], BF16, ar)
    w2g = [p.sbuf(f"m1_w2g{i}", [128, 8 * D], BF16, ar) for i in range(2)]
    b1_sb = p.sbuf("m1_b1", [128, E * 32], F32, ar)
    b2_sb = p.sbuf("m1_b2", [E, D], F32, ar)
    y_acc = p.sbuf("m1_y", [128, 8 * D], F32, ar)
    h_bf = p.sbuf("m1_h", [128, 128], BF16, ar)
    wtr = p.sbuf("m1_wtr", [E, 128], F32, ar)
    ps_h = p.psum("m1_ph", [128, 128], F32, ar)
    ps_y = [p.psum(f"m1_py{i}", [128, D], F32, ar) for i in range(2)]
    ps_b2 = p.psum("m1_pb2", [128, D], F32, ar)
    ps_wt = p.psum("m1_pwt", [E, 128], F32, ar)
    xnc = chunks(xn_bf)
    w1v = w1_sb.ap().rearrange("p (ci h) -> p ci h", ci=NCH)
    yv = y_acc.ap().rearrange("p (tt o) -> p tt o", tt=8)
    wv_ = tiles(w_tm, E)
    mh = moe_hbm.rearrange("(tt p) o -> tt p o", p=128)
    s_out = p.sem()
    s_b = p.sem()
    with p.block() as b:
        @b.sync
        def _(sy):
            s_b.dma(sy.dma_start(out=b1_sb[:], in_=b1_h))
            s_b.dma(sy.dma_start(out=b2_sb[:], in_=b2_h))
        @b.scalar
        def _(s):
            s.wait_ge(s_b.h, 32)
        @b.tensor
        def _(t):
            t.wait_ge(s_b.h, 32)

    for half in range(2):
        tb = half * 8
        s_l1 = p.sem()
        s_l2 = [p.sem(), p.sem()]
        s_hm = p.sem(); s_ge = p.sem(); s_ym = p.sem(); s_f = p.sem()
        with p.block() as b:
            @b.vector
            def _(v):
                v.memset(y_acc[:], 0.0)
        jobs = [(e, hg) for e in range(E) for hg in range(4)]
        NJ = len(jobs)
        with p.block() as b:
            @b.gpsimd
            def _(g):
                for i, (e, hg) in enumerate(jobs):
                    if hg == 0:
                        if e >= 1:
                            g.wait_ge(s_f.h, 32 * e)
                        s_l1.dma(g.dma_start(out=w1v,
                                             in_=w1_h[e].rearrange("(ci p) h -> p ci h", p=128)))
                    if i >= 2:
                        g.wait_ge(s_ym.h, 64 * (i - 1))
                    s_l2[i % 2].dma(g.dma_start(
                        out=w2g[i % 2].ap().rearrange("p (hb o) -> p hb o", hb=8),
                        in_=w2_h[e, 1024 * hg:1024 * (hg + 1)].rearrange("(hb p) o -> p hb o", p=128)))
            @b.tensor
            def _(t):
                for i, (e, hg) in enumerate(jobs):
                    if hg == 0:
                        t.wait_ge(s_l1.h, 16 * (e + 1))
                    t.wait_ge(s_l2[i % 2].h, 16 * (i // 2 + 1))
                    w2v = w2g[i % 2].ap().rearrange("p (hb o) -> p hb o", hb=8)
                    for tt in range(8):
                        j = 8 * i + tt
                        for hb in range(8):
                            hba = 8 * hg + hb
                            if j > 0 or hb > 0:
                                t.wait_ge(s_ge.h, 8 * j + hb)
                            for ci in range(NCH):
                                inst = t.matmul(ps_h[:], w1v[:, ci, 128 * hba:128 * (hba + 1)],
                                                xnc[:, ci, 128 * (tb + tt):128 * (tb + tt + 1)],
                                                start=(ci == 0), stop=(ci == NCH - 1))
                            s_hm.inc(inst)
                            t.wait_ge(s_ge.h, 8 * j + hb + 1)
                            if hb == 0 and j >= 2:
                                t.wait_ge(s_f.h, j - 1)
                            for o2 in range(2):
                                inst = t.matmul(ps_y[j % 2][:, 512 * o2:512 * (o2 + 1)],
                                                h_bf[:], w2v[:, hb, 512 * o2:512 * (o2 + 1)],
                                                start=(hb == 0), stop=(hb == 7))
                            s_ym.inc(inst)
            @b.scalar
            def _(s):
                for i, (e, hg) in enumerate(jobs):
                    for tt in range(8):
                        j = 8 * i + tt
                        for hb in range(8):
                            hba = 8 * hg + hb
                            s.wait_ge(s_hm.h, 8 * j + hb + 1)
                            if 8 * j + hb >= 1:
                                s.wait_ge(s_ym.h, 8 * j + hb)
                            s_ge.inc(s.activation(out=h_bf[:], in_=ps_h[:], func=AF.Gelu,
                                                  bias=b1_sb[:, 32 * e + hba:32 * e + hba + 1]))
            @b.vector
            def _(v):
                for i, (e, hg) in enumerate(jobs):
                    for tt in range(8):
                        j = 8 * i + tt
                        v.wait_ge(s_ym.h, 8 * (j + 1))
                        s_f.inc(v.scalar_tensor_tensor(
                            yv[:, tt], ps_y[j % 2][:], wv_[:, tb + tt, e:e + 1],
                            yv[:, tt], op0=ALU.mult, op1=ALU.add))
        s_t = p.sem(); s_te = p.sem(); s_bm = p.sem()
        for tt in range(8):
            with p.block() as b:
                @b.tensor
                def _(t):
                    s_t.inc(t.transpose(ps_wt[:], wv_[:, tb + tt], c["ident_f"][:]))
                @b.vector
                def _(v):
                    v.wait_ge(s_t.h, s_t.n)
                    v.tensor_copy(wtr[:], ps_wt[:])
            with p.block() as b:
                @b.tensor
                def _(t):
                    s_bm.inc(t.matmul(ps_b2[:, 0:512], wtr[:], b2_sb[:, 0:512],
                                      start=True, stop=True))
                    s_bm.inc(t.matmul(ps_b2[:, 512:1024], wtr[:], b2_sb[:, 512:1024],
                                      start=True, stop=True))
                @b.vector
                def _(v):
                    v.wait_ge(s_bm.h, s_bm.n)
                    v.tensor_tensor(yv[:, tt], yv[:, tt], ps_b2[:], op=ALU.add)
            with p.block() as b:
                @b.gpsimd
                def _(g):
                    s_out.dma(g.dma_start(out=mh[tb + tt], in_=yv[:, tt]))
                    g.wait_ge(s_out.h, s_out.n)
    with p.block() as b:
        @b.vector
        def _(v):
            pass
    ar.close()

def emit_l1_attn(p, c, x_sb, xn_bf, moe_hbm, wq_h, wk_h, wvd_h, wvu_h):
    NH, DH = 16, 64
    ar = ExitStack()
    w_sb = p.sbuf("l1_w", [128, NCH * D], BF16, ar)
    q_fm = p.sbuf("l1_q", [128, NCH * T], BF16, ar)
    dg_all = p.sbuf("l1_dg", [128, NH * NTT], F32, ar)
    z_all = p.sbuf("l1_z", [128, NH * NTT * 2], F32, ar)
    zs = p.sbuf("l1_zs", [128, NH * NTT], F32, ar)
    sc_d = p.sbuf("l1_scd", [128, 128], F32, ar)
    sc_dt = p.sbuf("l1_scdt", [128, 128], F32, ar)
    maskd = p.sbuf("l1_maskd", [128, 128], F32, ar)
    neginf_row = p.sbuf("l1_ninf", [1, 512], F32, ar)
    mcol = p.sbuf("l1_m", [128, 1], F32, ar)
    ar_ps1 = ExitStack()
    ps_s = p.psum("l1_pss", [128, T], F32, ar_ps1)
    k_fm = p.sbuf("l1_k", [128, NCH * T], BF16, ar_ps1)
    wv = w_sb.ap().rearrange("p (ci o) -> p ci o", ci=NCH)
    zv = z_all.ap().rearrange("p (n kt two) -> p n kt two", n=NH, kt=NTT)
    dgv = dg_all.ap().rearrange("p (n kt) -> p n kt", n=NH)
    xc = chunks(x_sb)
    xnc = chunks(xn_bf)

    def load_heads(wh):
        ld2 = p.sem()
        with p.block() as b:
            @b.gpsimd
            def _(g):
                for n in range(NH):
                    ld2.dma(g.dma_start(
                        out=wv[:, :, DH * n:DH * (n + 1)],
                        in_=wh[n].rearrange("(ci p) d -> p ci d", p=128)))
            @b.tensor
            def _(t):
                t.wait_ge(ld2.h, 16 * NH)

    load_heads(wq_h)
    emit_proj_fm(p, c, w_sb, xn_bf, q_fm, "l1_pq", evac_scale=0.125)
    load_heads(wk_h)
    emit_proj_fm(p, c, w_sb, xn_bf, k_fm, "l1_pk")
    if STOP == "proj":
        return

    with p.block() as b:
        @b.vector
        def _(v):
            v.tensor_scalar(maskd[:], c["iota_d"][:], 0, None, op0=ALU.is_ge)
            v.memset(neginf_row[:], NEG)
            v.memset(z_all[:], 1.0)
    with p.block() as b:
        @b.vector
        def _(v):
            v.tensor_scalar(maskd[:], maskd[:], 1.0e38, 1.0e38, op0=ALU.mult, op1=ALU.subtract)
            v.memset(z_all.ap().rearrange("p (n x) -> p n x", n=NH)[:, :, 2 * NTT - 1:2 * NTT], 0.0)

    s_mm = p.sem(); s_mskd = p.sem(); s_mskr = p.sem(); s_mx = p.sem()
    s_ex = p.sem(); s_dz = p.sem(); s_dg = p.sem()
    jobs = [(n, kt) for n in range(NH) for kt in range(NTT)]
    if JOBS_LIM is not None:
        jobs = jobs[:JOBS_LIM]
    excnt = []
    tot = 0
    for (n, kt) in jobs:
        tot += 1 if kt == NTT - 1 else 2
        excnt.append(tot)
    with p.block() as b:
        @b.tensor
        def _(t):
            for i, (n, kt) in enumerate(jobs):
                ci, base = (n * DH) // 128, (n * DH) % 128
                if i >= 1:
                    t.wait_ge(s_dg.h, i)
                    t.wait_ge(s_ex.h, excnt[i - 1])
                a0 = 512 * (kt // 4)
                nbad = 128 * kt - a0
                for j4 in range(kt // 4, 4):
                    last_in_slice = not (j4 == kt // 4 and nbad > 0)
                    inst = t.matmul(ps_s[:, 512 * j4:512 * (j4 + 1)],
                                    k_fm[base:base + DH, ci * T + 128 * kt: ci * T + 128 * (kt + 1)],
                                    q_fm[base:base + DH, ci * T + 512 * j4: ci * T + 512 * (j4 + 1)],
                                    start=True, stop=last_in_slice,
                                    tile_position=(base, 0))
                    if j4 == kt // 4 and nbad > 0:
                        inst = t.matmul(ps_s[:, a0:a0 + nbad], c["ones_f"][:],
                                        neginf_row[:, 0:nbad], start=False, stop=True)
                s_mm.inc(inst)
        @b.vector
        def _(v):
            for i, (n, kt) in enumerate(jobs):
                v.wait_ge(s_mm.h, i + 1)
                s_mskd.inc(v.tensor_tensor(ps_s[:, 128 * kt:128 * (kt + 1)],
                                           ps_s[:, 128 * kt:128 * (kt + 1)],
                                           maskd[:], op=ALU.add))
                v.wait_ge(s_mskr.h, i + 1)
                if i >= 1:
                    v.wait_ge(s_ex.h, excnt[i - 1])
                a0 = 512 * (kt // 4)
                s_mx.inc(v.tensor_reduce(mcol[:], ps_s[:, a0:T], axis=AX.X,
                                         op=ALU.max, negate=True))
                v.wait_ge(s_ex.h, excnt[i])
                if i >= 1:
                    v.wait_ge(s_dz.h, i)
                s_dg.inc(v.tensor_tensor(sc_d[:], ps_s[:, 128 * kt:128 * (kt + 1)],
                                         c["ident_f"][:], op=ALU.mult))
        @b.sync
        def _(sp):
            for i in range(len(jobs)):
                sp.wait_ge(s_mskd.h, i + 1)
                sp.nop().then_inc(s_mskr.h, 1)
        @b.scalar
        def _(s):
            for i, (n, kt) in enumerate(jobs):
                s.wait_ge(s_mx.h, i + 1)
                if i >= 1:
                    s.wait_ge(s_dg.h, i)
                s_ex.inc(s.activation(out=ps_s[:, 128 * kt:128 * (kt + 1)],
                                      in_=ps_s[:, 128 * kt:128 * (kt + 1)],
                                      func=AF.Exp, bias=mcol[:],
                                      accum_out=z_all[:, 2 * (NTT * n + kt):2 * (NTT * n + kt) + 1]))
                if kt < NTT - 1:
                    s_ex.inc(s.activation(out=ps_s[:, 128 * (kt + 1):T],
                                          in_=ps_s[:, 128 * (kt + 1):T],
                                          func=AF.Exp, bias=mcol[:],
                                          accum_out=z_all[:, 2 * (NTT * n + kt) + 1:2 * (NTT * n + kt) + 2]))
                s.wait_ge(s_dg.h, i + 1)
                s_dz.inc(s.activation(out=sc_dt[:], in_=sc_d[:], func=AF.Copy,
                                      accum_out=dgv[:, n, kt:kt + 1]))

    ar_ps1.close()
    if STOP == "scores":
        return
    with p.block() as b:
        @b.vector
        def _(v):
            v.tensor_tensor(zs[:], z_all.ap().rearrange("p (a two) -> p a two", two=2)[:, :, 0],
                            z_all.ap().rearrange("p (a two) -> p a two", two=2)[:, :, 1],
                            op=ALU.add)
    with p.block() as b:
        @b.vector
        def _(v):
            v.reciprocal(zs[:], zs[:])
    with p.block() as b:
        @b.gpsimd
        def _(g):
            g.tensor_tensor(dg_all[:], dg_all[:], zs[:], op=ALU.mult)
    with p.block() as b:
        @b.vector
        def _(v):
            v.tensor_scalar_mul(dg_all[:], dg_all[:], 1.0 / NH)

    with p.block() as b:
        @b.vector
        def _(v):
            for cc in range(NCH):
                v.tensor_tensor(xc[:, cc], xc[:, cc], xnc[:, cc], op=ALU.add)

    load_heads(wvd_h)
    emit_proj_fm(p, c, w_sb, xn_bf, q_fm, "l1_pvd")
    ar_diag = ExitStack()
    dtr = p.sbuf("l1_dtr", [NTT, 128], F32, ar_diag)
    drow = p.sbuf("l1_drow", [1, T], F32, ar_diag)
    diagb = p.sbuf("l1_diagb", [128, T], F32, ar_diag)
    ps_t16 = p.psum("l1_pt16", [NTT, 128], F32, ar_diag)
    ps_db = p.psum("l1_pdb", [128, 512], F32, ar_diag)
    s_a = p.sem(); s_bdma = p.sem(); s_cp = p.sem()
    for n in range(NH):
        ci, base = (n * DH) // 128, (n * DH) % 128
        ba = s_a.n
        bc = s_cp.n
        with p.block() as b:
            @b.tensor
            def _(t):
                s_a.inc(t.transpose(ps_t16[:], dgv[:, n], c["ident_f"][:]))
            @b.vector
            def _(v):
                v.wait_ge(s_a.h, ba + 1)
                v.tensor_copy(dtr[:], ps_t16[:])
        with p.block() as b:
            @b.gpsimd
            def _(g):
                s_bdma.dma(g.dma_start(out=drow.ap().rearrange("p (kt x) -> p kt x", kt=NTT),
                                       in_=dtr[:]))
                g.wait_ge(s_bdma.h, s_bdma.n)
        with p.block() as b:
            @b.tensor
            def _(t):
                for j4 in range(4):
                    if j4 >= 1:
                        t.wait_ge(s_cp.h, bc + j4)
                    s_a.inc(t.matmul(ps_db[:], c["ones_f"][:],
                                     drow[:, 512 * j4:512 * (j4 + 1)],
                                     start=True, stop=True))
            @b.vector
            def _(v):
                for j4 in range(4):
                    v.wait_ge(s_a.h, ba + j4 + 2)
                    s_cp.inc(v.tensor_copy(diagb[:, 512 * j4:512 * (j4 + 1)], ps_db[:]))
        with p.block() as b:
            @b.gpsimd
            def _(g):
                g.tensor_tensor(q_fm.ap().rearrange("p (cc t) -> p cc t", cc=NCH)[base:base + DH, ci],
                                q_fm.ap().rearrange("p (cc t) -> p cc t", cc=NCH)[base:base + DH, ci],
                                diagb[base:base + DH, :], op=ALU.mult)

    with p.block() as b:
        @b.vector
        def _(v):
            pass
    ar_diag.close()
    if STOP == "diag":
        return
    wvu_sb = p.sbuf("l1_wvu", [128, 8 * D], BF16, ar)
    ldu = p.sem()
    with p.block() as b:
        @b.gpsimd
        def _(g):
            for n in range(NH):
                ldu.dma(g.dma_start(
                    out=wvu_sb[64 * (n % 2):64 * (n % 2) + 64,
                               D * (n // 2):D * (n // 2) + D],
                    in_=wvu_h[n]))
        @b.tensor
        def _(t):
            t.wait_ge(ldu.h, 16 * NH)
    ps_d = [p.psum(f"l1_psd{i}", [128, 512], F32, ar) for i in range(2)]
    s_dm = p.sem(); s_dv2 = p.sem(); s_dvr = p.sem(); s_de = p.sem()
    djobs = [(oc, sl) for oc in range(NCH) for sl in range(4)]
    with p.block() as b:
        @b.tensor
        def _(t):
            for i, (oc, sl) in enumerate(djobs):
                if i >= 1:
                    t.wait_ge(s_de.h, i)
                for par in range(2):
                    ns = [n for n in range(NH) if n % 2 == par]
                    for k2, n in enumerate(ns):
                        ci = (n * DH) // 128
                        inst = t.matmul(ps_d[par][:],
                                        wvu_sb[64 * par:64 * par + 64,
                                               D * (n // 2) + 128 * oc:D * (n // 2) + 128 * (oc + 1)],
                                        q_fm[64 * par:64 * par + 64,
                                             ci * T + 512 * sl:ci * T + 512 * (sl + 1)],
                                        start=(k2 == 0), stop=(k2 == len(ns) - 1),
                                        tile_position=(64 * par, 0))
                s_dm.inc(inst)
        @b.vector
        def _(v):
            for i, (oc, sl) in enumerate(djobs):
                dst = xc[:, oc, 512 * sl:512 * (sl + 1)]
                v.wait_ge(s_dm.h, i + 1)
                s_dv2.inc(v.tensor_tensor(dst, dst, ps_d[0][:], op=ALU.add))
                v.wait_ge(s_dvr.h, i + 1)
                s_de.inc(v.tensor_tensor(dst, dst, ps_d[1][:], op=ALU.add))
        @b.sync
        def _(sp):
            for i in range(len(djobs)):
                sp.wait_ge(s_dv2.h, i + 1)
                s_dvr.inc(sp.nop())

    if STOP == "delta":
        return
    mt = [p.sbuf(f"l1_mt{i}", [128, D], BF16, ar) for i in range(2)]
    ps_mt = [p.psum(f"l1_pmt{i}", [128, 128], BF16, ar) for i in range(2)]
    s_ml = [p.sem(), p.sem()]
    s_mt = p.sem(); s_me = p.sem()
    mh = moe_hbm.rearrange("(tt p) o -> tt p o", p=128)
    with p.block() as b:
        @b.gpsimd
        def _(g):
            for tt in range(NTT):
                if tt >= 2:
                    g.wait_ge(s_mt.h, NCH * (tt - 1))
                s_ml[tt % 2].dma(g.dma_start(out=mt[tt % 2][:], in_=mh[tt]))
        @b.tensor
        def _(t):
            i = 0
            for tt in range(NTT):
                t.wait_ge(s_ml[tt % 2].h, 16 * (tt // 2 + 1))
                for oc in range(NCH):
                    if i >= 2:
                        t.wait_ge(s_me.h, i - 1)
                    s_mt.inc(t.transpose(ps_mt[i % 2][:],
                                         mt[tt % 2][:, 128 * oc:128 * (oc + 1)],
                                         c["ident_bf"][:]))
                    i += 1
        @b.vector
        def _(v):
            i = 0
            for tt in range(NTT):
                for oc in range(NCH):
                    v.wait_ge(s_mt.h, i + 1)
                    dst = xc[:, oc, 128 * tt:128 * (tt + 1)]
                    s_me.inc(v.tensor_tensor(dst, dst, ps_mt[i % 2][:], op=ALU.add))
                    i += 1
    with p.block() as b:
        @b.vector
        def _(v):
            pass
    ar.close()


def emit_transpose_tm_to_fm_add_x(p, c, src_tm, x_sb, tag):
    ar = ExitStack()
    ps = [p.psum(f"{tag}{i}", [128, 128], BF16, ar) for i in range(2)]
    s_t = p.sem(); s_e = p.sem()
    sv = src_tm.ap().rearrange("p (tt o) -> p tt o", tt=NTT)
    xc = chunks(x_sb)
    with p.block() as b:
        @b.tensor
        def _(t):
            i = 0
            for tt in range(NTT):
                for oc in range(NCH):
                    if i >= 2:
                        t.wait_ge(s_e.h, i - 1)
                    s_t.inc(t.transpose(ps[i % 2][:], sv[:, tt, 128 * oc:128 * (oc + 1)],
                                        c["ident_bf"][:]))
                    i += 1
        @b.vector
        def _(v):
            i = 0
            for tt in range(NTT):
                for oc in range(NCH):
                    v.wait_ge(s_t.h, i + 1)
                    dst = xc[:, oc, 128 * tt:128 * (tt + 1)]
                    s_e.inc(v.tensor_tensor(dst, dst, ps[i % 2][:], op=ALU.add))
                    i += 1
    with p.block() as b:
        @b.vector
        def _(v):
            pass
    ar.close()



def build_L2():
    p = Prog()
    NH, DH = 16, 64
    x_h = p.din("x", [D, T], F32)
    wq_h = p.din("wq", [D, D], F32)
    wk_h = p.din("wk", [D, D], F32)
    wv_h = p.din("wv", [D, D], F32)
    wo_h = p.din("wo", [D, D], F32)
    sig_h = p.din("sig", [D, 64], F32)
    knots_h = p.din("knots", [1, 8], F32)
    temp_h = p.din("temp", [1, 1], F32)
    cw1_h = p.din("cw1", [D, 256], F32)
    cb1_h = p.din("cb1", [128, 2], F32)
    cw2_h = p.din("cw2", [256, 2], F32)
    cb2_h = p.din("cb2", [1, 2], F32)
    spl_h = p.din("spl", [64, 65], F32)
    dirs_h = p.din("dirs", [64, D], F32)
    gam_h = p.din("gam", [128, NCH], F32)
    bet_h = p.din("bet", [128, NCH], F32)
    osc_h = p.din("osc", [1, 1], F32)
    y_h = p.dout("y", [D, T], F32)

    c = emit_consts(p, ident_bf=True, ident_f32=True, tril_s=True)
    x_sb = p.sbuf("x_sb", [128, NCH * T], F32)
    xn_bf = p.sbuf("xn_bf", [128, NCH * T], BF16)
    emit_load_fm(p, x_h, x_sb)
    ln_ar = ExitStack()
    ln = emit_ln(p, c, x_sb, xn_bf, "ln", arena=ln_ar)

    emit_l2_ffn(p, c, x_sb, xn_bf, ln, sig_h, knots_h, temp_h, cw1_h, cb1_h,
                cw2_h, cb2_h, spl_h, dirs_h, gam_h, bet_h, osc_h)
    ln_ar.close()
    emit_l2_attn(p, c, x_sb, xn_bf, wq_h, wk_h, wv_h, wo_h)

    st = p.sem()
    with p.block() as b:
        @b.gpsimd
        def _(g):
            st.dma(g.dma_start(out=y_h.rearrange("(cc p) t -> p cc t", p=128),
                               in_=chunks(x_sb)))
            g.wait_ge(st.h, 16)
    return p


def emit_l2_attn(p, c, x_sb, xn_bf, wq_h, wk_h, wv_h, wo_h):
    NH, DH = 16, 64
    SC = 0.3535533905932738
    ar = ExitStack()
    wq_sb = p.sbuf("t_wq", [128, NCH * D], BF16, ar)
    wk_sb = p.sbuf("t_wk", [128, NCH * D], BF16, ar)
    wv_sb = p.sbuf("t_wv", [128, NCH * D], BF16, ar)
    wo_sb = p.sbuf("t_wo", [128, NCH * D], BF16, ar)
    qh = p.sbuf("t_qh", [64, T], BF16, ar)
    kh = p.sbuf("t_kh", [64, T], BF16, ar)
    vh = p.sbuf("t_vh", [64, T], BF16, ar)
    ob2 = p.sbuf("t_ob2", [128, T], BF16, ar)
    negm = p.sbuf("t_negm", [128, NTT], F32, ar)
    ntr = p.sbuf("t_ntr", [NTT, 128], F32, ar)
    nrow = p.sbuf("t_nrow", [1, T], BF16, ar)
    maskq = p.sbuf("t_maskq", [128, 128], F32, ar)
    maskk = p.sbuf("t_maskk", [128, 128], F32, ar)
    neginf_row = p.sbuf("t_ninf", [1, 512], BF16, ar)
    vt65 = p.sbuf("t_vt65", [128, NTT * 65], BF16, ar)
    p_bf = p.sbuf("t_pbf", [128, T], BF16, ar)
    o_f32 = p.sbuf("t_of32", [64, T], F32, ar)
    zrec = p.sbuf("t_zrec", [1, T], BF16, ar)
    xc = chunks(x_sb)
    xnc = chunks(xn_bf)
    ld = p.sem()
    with p.block() as b:
        @b.gpsimd
        def _(g):
            for wh, wsb in [(wq_h, wq_sb), (wk_h, wk_sb), (wv_h, wv_sb), (wo_h, wo_sb)]:
                ld.dma(g.dma_start(out=wsb.ap().rearrange("p (ci o) -> p ci o", ci=NCH),
                                   in_=wh.rearrange("(ci p) o -> p ci o", p=128)))
        @b.vector
        def _(v):
            v.tensor_scalar(maskq[:], c["iota_d"][:], 0, None, op0=ALU.is_le)
            v.tensor_scalar(maskk[:], c["iota_d"][:], 0, None, op0=ALU.is_ge)
            v.memset(neginf_row[:], NEG)
            v.memset(vt65.ap().rearrange("p (kt x) -> p kt x", kt=NTT)[:, :, 64:65], 1.0)
        @b.tensor
        def _(t):
            t.wait_ge(ld.h, 64)
    with p.block() as b:
        @b.vector
        def _(v):
            v.tensor_scalar(maskq[:], maskq[:], 1.0e38, 1.0e38, op0=ALU.mult, op1=ALU.subtract)
            v.tensor_scalar(maskk[:], maskk[:], 1.0e38, 1.0e38, op0=ALU.mult, op1=ALU.subtract)

    SM = {k: p.sem() for k in ["pm", "pe", "vt", "ve", "mm", "mk", "mkr", "mx",
                               "nt", "nd", "km", "kk", "kkr", "ex", "av", "z",
                               "oe", "bc", "wm", "wv", "wvr", "we"]}
    SM["mkr_n"] = [0]
    SM["kkr_n"] = [0]
    for h in range(NH):
        emit_l2_head(p, c, h, x_sb, xn_bf, wq_sb, wk_sb, wv_sb, wo_sb,
                     qh, kh, vh, ob2, negm, ntr, nrow, maskq, maskk, neginf_row,
                     vt65, p_bf, o_f32, zrec, SM)
    with p.block() as b:
        @b.vector
        def _(v):
            pass
    ar.close()


def emit_l2_head(p, c, h, x_sb, xn_bf, wq_sb, wk_sb, wv_sb, wo_sb,
                 qh, kh, vh, ob2, negm, ntr, nrow, maskq, maskk, neginf_row,
                 vt65, p_bf, o_f32, zrec, SM):
    NH, DH = 16, 64
    SC = 0.3535533905932738
    xnc = chunks(xn_bf)
    xc = chunks(x_sb)
    co = DH * h
    s_pm = SM["pm"]; s_pe = SM["pe"]; s_vt = SM["vt"]; s_ve = SM["ve"]
    s_mm = SM["mm"]; s_mk = SM["mk"]; s_mkr = SM["mkr"]; s_mx = SM["mx"]
    s_nt = SM["nt"]; s_nd = SM["nd"]; s_km = SM["km"]; s_kk = SM["kk"]
    s_kkr = SM["kkr"]; s_ex = SM["ex"]; s_av = SM["av"]; s_z = SM["z"]
    s_oe = SM["oe"]; s_bc = SM["bc"]; s_wm = SM["wm"]; s_wv = SM["wv"]
    s_wvr = SM["wvr"]; s_we = SM["we"]
    b_pm = s_pm.n; b_pe = s_pe.n; b_vt = s_vt.n; b_ve = s_ve.n
    b_mm = s_mm.n; b_mk = s_mk.n; b_mkr = SM["mkr_n"][0]; b_mx = s_mx.n
    b_km = s_km.n; b_kk = s_kk.n; b_kkr = SM["kkr_n"][0]; b_ex = s_ex.n
    b_av = s_av.n

    ar_h1 = ExitStack()
    ps_p = [p.psum(f"t_psp{h}_{i}", [64, 512], F32, ar_h1) for i in range(2)]
    with p.block() as b:
        @b.tensor
        def _(t):
            i = 0
            for wsb, dst, scale in [(wq_sb, qh, SC), (wk_sb, kh, SC), (wv_sb, vh, None)]:
                wv2 = wsb.ap().rearrange("p (ci o) -> p ci o", ci=NCH)
                for sl in range(4):
                    if i >= 2:
                        t.wait_ge(s_pe.h, b_pe + i - 1)
                    for ci in range(NCH):
                        inst = t.matmul(ps_p[i % 2][:], wv2[:, ci, co:co + DH],
                                        xnc[:, ci, 512 * sl:512 * (sl + 1)],
                                        start=(ci == 0), stop=(ci == NCH - 1))
                    s_pm.inc(inst)
                    i += 1
        @b.vector
        def _(v):
            i = 0
            for wsb, dst, scale in [(wq_sb, qh, SC), (wk_sb, kh, SC), (wv_sb, vh, None)]:
                for sl in range(4):
                    v.wait_ge(s_pm.h, b_pm + i + 1)
                    if scale is not None:
                        s_pe.inc(v.tensor_scalar_mul(dst[:, 512 * sl:512 * (sl + 1)],
                                                     ps_p[i % 2][:], scale))
                    else:
                        s_pe.inc(v.tensor_copy(dst[:, 512 * sl:512 * (sl + 1)],
                                               ps_p[i % 2][:]))
                    i += 1
    vtv = vt65.ap().rearrange("p (kt x) -> p kt x", kt=NTT)
    ps_vtb = [p.psum(f"t_psvt{h}_{i}", [128, 64], BF16, ar_h1) for i in range(2)]
    with p.block() as b:
        @b.tensor
        def _(t):
            for kt in range(NTT):
                if kt >= 2:
                    t.wait_ge(s_ve.h, b_ve + kt - 1)
                s_vt.inc(t.transpose(ps_vtb[kt % 2][:],
                                     vh[:, 128 * kt:128 * (kt + 1)], c["ident_bf"][0:64, 0:64]))
        @b.vector
        def _(v):
            for kt in range(NTT):
                v.wait_ge(s_vt.h, b_vt + kt + 1)
                s_ve.inc(v.tensor_copy(vtv[:, kt, 0:64], ps_vtb[kt % 2][:]))
    with p.block() as b:
        @b.vector
        def _(v):
            pass
    ar_h1.close()
    ar_h2 = ExitStack()
    ps_s = p.psum(f"t_pss{h}", [128, T], F32, ar_h2)
    ps_o = p.psum(f"t_pso{h}", [65, T], F32, ar_h2)
    with p.block() as b:
        @b.tensor
        def _(t):
            for qt in range(NTT):
                if qt >= 1:
                    t.wait_ge(s_mx.h, b_mx + qt)
                nsl = qt // 4 + 1
                for j4 in range(nsl):
                    hi = min(512 * (j4 + 1), 128 * (qt + 1))
                    inst = t.matmul(ps_s[:, 512 * j4:hi],
                                    qh[:, 128 * qt:128 * (qt + 1)],
                                    kh[:, 512 * j4:hi],
                                    start=True, stop=True)
                s_mm.inc(inst)
        @b.vector
        def _(v):
            for qt in range(NTT):
                v.wait_ge(s_mm.h, b_mm + qt + 1)
                s_mk.inc(v.tensor_tensor(ps_s[:, 128 * qt:128 * (qt + 1)],
                                         ps_s[:, 128 * qt:128 * (qt + 1)],
                                         maskq[:], op=ALU.add))
                v.wait_ge(s_mkr.h, b_mkr + qt + 1)
                s_mx.inc(v.tensor_reduce(negm[:, qt:qt + 1], ps_s[:, 0:128 * (qt + 1)],
                                         axis=AX.X, op=ALU.max, negate=True))
        @b.sync
        def _(sp):
            for qt in range(NTT):
                sp.wait_ge(s_mk.h, b_mk + qt + 1)
                sp.nop().then_inc(s_mkr.h, 1)
    SM["mkr_n"][0] += NTT
    with p.block() as b:
        @b.tensor
        def _(t):
            s_nt.inc(t.transpose(ps_s[0:NTT, 0:128], negm[:], c["ident_f"][:]))
        @b.vector
        def _(v):
            v.wait_ge(s_nt.h, s_nt.n)
            v.tensor_copy(ntr[:], ps_s[0:NTT, 0:128])
    with p.block() as b:
        @b.gpsimd
        def _(g):
            s_nd.dma(g.dma_start(out=nrow.ap().rearrange("p (kt x) -> p kt x", kt=NTT),
                                 in_=ntr[:]))
            g.wait_ge(s_nd.h, s_nd.n)
    with p.block() as b:
        @b.tensor
        def _(t):
            for kt in range(NTT):
                a0 = 512 * (kt // 4)
                nbad = 128 * kt - a0
                if kt >= 1:
                    t.wait_ge(s_ex.h, b_ex + kt)
                for j4 in range(kt // 4, 4):
                    t.matmul(ps_s[:, 512 * j4:512 * (j4 + 1)],
                             kh[:, 128 * kt:128 * (kt + 1)],
                             qh[:, 512 * j4:512 * (j4 + 1)],
                             start=True, stop=False)
                    stop = not (j4 == kt // 4 and nbad > 0)
                    inst = t.matmul(ps_s[:, 512 * j4:512 * (j4 + 1)], c["ones_bf"][:],
                                    nrow[:, 512 * j4:512 * (j4 + 1)],
                                    start=False, stop=stop)
                    if not stop:
                        inst = t.matmul(ps_s[:, a0:a0 + nbad], c["ones_bf"][:],
                                        neginf_row[:, 0:nbad], start=False, stop=True)
                s_km.inc(inst)
                t.wait_ge(s_ex.h, b_ex + kt + 1)
                for j4 in range(kt // 4, 4):
                    inst = t.matmul(ps_o[:, 512 * j4:512 * (j4 + 1)],
                                    vtv[:, kt], p_bf[:, 512 * j4:512 * (j4 + 1)],
                                    start=(kt == 0), stop=(kt == min(4 * (j4 + 1), NTT) - 1))
                s_av.inc(inst)
        @b.vector
        def _(v):
            for kt in range(NTT):
                v.wait_ge(s_km.h, b_km + kt + 1)
                s_kk.inc(v.tensor_tensor(ps_s[:, 128 * kt:128 * (kt + 1)],
                                         ps_s[:, 128 * kt:128 * (kt + 1)],
                                         maskk[:], op=ALU.add))
        @b.sync
        def _(sp):
            for kt in range(NTT):
                sp.wait_ge(s_kk.h, b_kk + kt + 1)
                sp.nop().then_inc(s_kkr.h, 1)
        @b.scalar
        def _(s):
            for kt in range(NTT):
                a0 = 512 * (kt // 4)
                s.wait_ge(s_kkr.h, b_kkr + kt + 1)
                if kt >= 1:
                    s.wait_ge(s_av.h, b_av + kt)
                s_ex.inc(s.activation(out=p_bf[:, a0:T], in_=ps_s[:, a0:T],
                                      func=AF.Exp))
    SM["kkr_n"][0] += NTT
    base = 64 * (h % 2)
    with p.block() as b:
        @b.scalar
        def _(s):
            s_z.inc(s.copy(zrec[:], ps_o[64:65, :]))
            s_oe.inc(s.copy(o_f32[:], ps_o[0:64, :]))
        @b.vector
        def _(v):
            v.wait_ge(s_z.h, s_z.n)
            with p.nc.allow_low_precision(reason="bf16 softmax normalizer"):
                v.reciprocal(zrec[:], zrec[:])
    with p.block() as b:
        @b.tensor
        def _(t):
            for j4 in range(4):
                inst = t.matmul(ps_o[0:64, 512 * j4:512 * (j4 + 1)],
                                c["ones_bf"][:, 0:64], zrec[:, 512 * j4:512 * (j4 + 1)],
                                start=True, stop=True)
            s_bc.inc(inst)
        @b.vector
        def _(v):
            v.wait_ge(s_bc.h, s_bc.n)
            v.tensor_tensor(ob2[base:base + 64, :], o_f32[:], ps_o[0:64, :], op=ALU.mult)
    with p.block() as b:
        @b.vector
        def _(v):
            pass
    ar_h2.close()
    if h % 2 == 1:
        ar_h3 = ExitStack()
        ps_wo = [p.psum(f"t_pswo{h}{i2}", [128, 512], F32, ar_h3) for i2 in range(2)]
        b_wm = s_wm.n
        b_wv = s_wv.n
        b_wvr = s_wvr.n
        b_we = s_we.n
        with p.block() as b:
            @b.tensor
            def _(t):
                i = 0
                for oc in range(NCH):
                    for sl in range(4):
                        if i >= 1:
                            t.wait_ge(s_we.h, b_we + i)
                        for g2 in range(2):
                            hh = h - 1 + g2
                            b2 = 64 * g2
                            ci2 = (DH * hh) // 128
                            inst = t.matmul(ps_wo[g2][:],
                                            wo_sb[b2:b2 + 64,
                                                  1024 * ci2 + 128 * oc:1024 * ci2 + 128 * (oc + 1)],
                                            ob2[b2:b2 + 64, 512 * sl:512 * (sl + 1)],
                                            start=True, stop=True,
                                            tile_position=(b2, 0))
                        s_wm.inc(inst)
                        i += 1
            @b.vector
            def _(v):
                i = 0
                for oc in range(NCH):
                    for sl in range(4):
                        dst = xc[:, oc, 512 * sl:512 * (sl + 1)]
                        v.wait_ge(s_wm.h, b_wm + i + 1)
                        s_wv.inc(v.tensor_tensor(dst, dst, ps_wo[0][:], op=ALU.add))
                        v.wait_ge(s_wvr.h, b_wvr + i + 1)
                        s_we.inc(v.tensor_tensor(dst, dst, ps_wo[1][:], op=ALU.add))
                        i += 1
            @b.sync
            def _(sp):
                for i in range(NCH * 4):
                    sp.wait_ge(s_wv.h, b_wv + i + 1)
                    s_wvr.inc(sp.nop())
        with p.block() as b:
            @b.vector
            def _(v):
                pass
        ar_h3.close()

def emit_l2_ffn(p, c, x_sb, xn_bf, ln, sig_h, knots_h, temp_h, cw1_h, cb1_h,
                cw2_h, cb2_h, spl_h, dirs_h, gam_h, bet_h, osc_h):
    NT_TILES, TPC, GRID = 64, 8, 16
    ar = ExitStack()
    ln2_needed = True
    sig_sb = p.sbuf("f_sig", [128, NCH * 64], F32, ar)
    q_sb = p.sbuf("f_q", [128, NCH * 64], F32, ar)
    wall = p.sbuf("f_wall", [128, NCH * 72], F32, ar)
    s_tm = p.sbuf("f_stm", [128, NTT * 72], F32, ar)
    gates = p.sbuf("f_gates", [128, NTT * 72], F32, ar)
    knots = p.sbuf("f_knots", [1, 8], F32, ar)
    kd_b = p.sbuf("f_kdb", [128, 8], F32, ar)
    tinv_b = p.sbuf("f_tinv", [128, 1], F32, ar)
    osc_b = p.sbuf("f_osc", [128, 1], F32, ar)
    scr1 = p.sbuf("f_scr1", [128, NTT * 72], F32, ar)
    mcol = p.sbuf("f_m", [128, NTT], F32, ar)
    icol = p.sbuf("f_i", [128, NTT], F32, ar)
    cidx = p.sbuf("f_cidx", [128, NTT], F32, ar)
    lidx = p.sbuf("f_lidx", [128, NTT], F32, ar)
    tidx = p.sbuf("f_tidx", [128, NTT], F32, ar)
    selg = p.sbuf("f_selg", [128, NTT * 8], F32, ar)
    oh_tm = p.sbuf("f_oh", [128, NTT * 64], F32, ar)
    iota_b = p.sbuf("f_iob", [128, 72], F32, ar)
    row1 = p.sbuf("f_row1", [1, 512], F32, ar)
    comp_tm = p.sbuf("f_comp", [128, NTT * 2], F32, ar)
    gath = p.sbuf("f_gath", [128, NTT * 65], F32, ar)
    t1g = p.sbuf("f_t1g", [128, NTT * GRID], F32, ar)
    t2g = p.sbuf("f_t2g", [128, NTT * GRID], F32, ar)
    mag = p.sbuf("f_mag", [128, NTT], F32, ar)
    ohs = p.sbuf("f_ohs", [128, NTT * 64], BF16, ar)
    ohsT = p.sbuf("f_ohsT", [64, T], BF16, ar)
    spl_sb = p.sbuf("f_spl", [64, 65], F32, ar)
    dirs_sb = p.sbuf("f_dirs", [64, D], BF16, ar)
    gam_sb = p.sbuf("f_gam", [128, NCH], F32, ar)
    bet_sb = p.sbuf("f_bet", [128, NCH], F32, ar)
    xnc = chunks(xn_bf)
    xc = chunks(x_sb)
    sv = tiles(s_tm, 72)
    gv = tiles(gates, 72)
    ld = p.sem()
    ldg = p.sem()
    with p.block() as b:
        @b.sync
        def _(sy):
            ld.dma(sy.dma_start(out=sig_sb.ap().rearrange("p (ci e) -> p ci e", ci=NCH),
                                in_=sig_h.rearrange("(ci p) e -> p ci e", p=128)))
            ld.dma(sy.dma_start(out=knots[:], in_=knots_h))
            ld.dma(sy.dma_start(out=spl_sb[:], in_=spl_h))
            ld.dma(sy.dma_start(out=gam_sb[:], in_=gam_h))
            ld.dma(sy.dma_start(out=bet_sb[:], in_=bet_h))
        @b.gpsimd
        def _(g):
            ldg.dma(g.dma_start(out=dirs_sb[:], in_=dirs_h))
        @b.vector
        def _(v):
            v.wait_ge(ld.h, ld.n)
            v.wait_ge(ldg.h, ldg.n)
            v.memset(row1[:], 1.0)
    sc_t = p.sem()
    tmp_sc = p.sbuf("f_tsc", [1, 2], F32, ar)
    kd_row = p.sbuf("f_kdr", [1, 8], F32, ar)
    ar_t = ExitStack()
    ps_t = p.psum("f_pst", [128, 96], F32, ar_t)
    with p.block() as b:
        @b.sync
        def _(sy):
            sc_t.dma(sy.dma_start(out=tmp_sc[:, 0:1], in_=temp_h))
            sc_t.dma(sy.dma_start(out=tmp_sc[:, 1:2], in_=osc_h))
        @b.vector
        def _(v):
            v.wait_ge(sc_t.h, 32)
            v.tensor_scalar_add(tmp_sc[:, 0:1], tmp_sc[:, 0:1], 1.0e-6)
    with p.block() as b:
        @b.vector
        def _(v):
            v.reciprocal(tmp_sc[:, 0:1], tmp_sc[:, 0:1])
            v.tensor_copy(kd_row[:, 0:1], knots[:, 0:1])
    with p.block() as b:
        @b.gpsimd
        def _(g):
            g.tensor_tensor(kd_row[:, 1:8], knots[:, 1:8], knots[:, 0:7], op=ALU.subtract)
    with p.block() as b:
        @b.tensor
        def _(t):
            sc_t.inc(t.matmul(ps_t[:, 0:8], c["ones_f"][:], kd_row[:], start=True, stop=True))
            sc_t.inc(t.matmul(ps_t[:, 8:9], c["ones_f"][:], tmp_sc[:, 0:1], start=True, stop=True))
            sc_t.inc(t.matmul(ps_t[:, 9:10], c["ones_f"][:], tmp_sc[:, 1:2], start=True, stop=True))
        @b.vector
        def _(v):
            v.wait_ge(sc_t.h, 35)
            v.tensor_copy(kd_b[:], ps_t[:, 0:8])
            v.tensor_copy(tinv_b[:], ps_t[:, 8:9])
            v.tensor_copy(osc_b[:], ps_t[:, 9:10])
    with p.block() as b:
        @b.vector
        def _(v):
            v.tensor_scalar(q_sb[:], sig_sb[:], 0.3, None, op0=ALU.is_gt)
        @b.gpsimd
        def _(g):
            g.tensor_scalar(scr1[:, 0:NCH * 64], sig_sb[:], -0.3, None, op0=ALU.is_lt)
    with p.block() as b:
        @b.vector
        def _(v):
            v.tensor_tensor(q_sb[:], q_sb[:], scr1[:, 0:NCH * 64], op=ALU.subtract)
    with p.block() as b:
        @b.vector
        def _(v):
            v.tensor_reduce(wall.ap().rearrange("p (ci e) -> p ci e", ci=NCH)[:, :, 64:72],
                            q_sb.ap().rearrange("p (ci cl tpc) -> p ci cl tpc", ci=NCH, cl=8),
                            axis=AX.X, op=ALU.add)
    with p.block() as b:
        @b.scalar
        def _(s):
            s.sign(wall.ap().rearrange("p (ci e) -> p ci e", ci=NCH)[:, :, 64:72],
                   wall.ap().rearrange("p (ci e) -> p ci e", ci=NCH)[:, :, 64:72])
        @b.vector
        def _(v):
            v.tensor_copy(wall.ap().rearrange("p (ci e) -> p ci e", ci=NCH)[:, :, 0:64],
                          q_sb.ap().rearrange("p (ci e) -> p ci e", ci=NCH)[:, :, :])
    emit_scores(p, c, x_sb, ln, wall, 72, s_tm, "fsc")

    with p.block() as b:
        @b.scalar
        def _(s_):
            s_.activation(out=gates[:], in_=s_tm[:], func=AF.Sigmoid, scale=tinv_b[:])
        @b.vector
        def _(v):
            v.memset(scr1[:], 0.0)
    for j in range(7):
        with p.block() as b:
            @b.vector
            def _(v):
                v.tensor_scalar(s_tm[:], gates[:], 7.0, float(j), op0=ALU.mult, op1=ALU.subtract)
        with p.block() as b:
            @b.gpsimd
            def _(g):
                g.tensor_scalar(s_tm[:], s_tm[:], 0.0, 1.0, op0=ALU.max, op1=ALU.min)
        with p.block() as b:
            @b.vector
            def _(v):
                v.scalar_tensor_tensor(scr1[:], s_tm[:], kd_b[:, j + 1:j + 2], scr1[:],
                                       op0=ALU.mult, op1=ALU.add)
    with p.block() as b:
        @b.vector
        def _(v):
            v.tensor_scalar(gates[:], scr1[:], kd_b[:, 0:1], None, op0=ALU.add)

    gvw = tiles(gates, 72)
    BIG = 1.0e9
    with p.block() as b:
        @b.tensor
        def _(t):
            t.matmul(ps_t[:, 16:88], c["ones_f"][:], c["iota_f"][:, 0:72],
                     start=True, stop=True).then_inc(sc_t.h, 1)
        @b.vector
        def _(v):
            v.wait_ge(sc_t.h, sc_t.n + 1)
            v.tensor_copy(iota_b[:], ps_t[:, 16:88])
    sc_t.n += 1

    def argmax8(src_view, idx_out, tag2):
        with p.block() as b:
            @b.vector
            def _(v):
                v.tensor_reduce(mcol[:], src_view, axis=AX.X, op=ALU.max)
        with p.block() as b:
            @b.vector
            def _(v):
                for tt in range(NTT):
                    v.tensor_scalar(scr1.ap().rearrange("p (tt e) -> p tt e", tt=NTT)[:, tt, 0:8],
                                    src_view[:, tt], mcol[:, tt:tt + 1], None, op0=ALU.not_equal)
        with p.block() as b:
            @b.gpsimd
            def _(g):
                g.tensor_scalar(scr1.ap().rearrange("p (tt e) -> p tt e", tt=NTT)[:, :, 0:8],
                                scr1.ap().rearrange("p (tt e) -> p tt e", tt=NTT)[:, :, 0:8],
                                BIG, None, op0=ALU.mult)
        with p.block() as b:
            @b.vector
            def _(v):
                for tt in range(NTT):
                    v.tensor_tensor(scr1.ap().rearrange("p (tt e) -> p tt e", tt=NTT)[:, tt, 0:8],
                                    scr1.ap().rearrange("p (tt e) -> p tt e", tt=NTT)[:, tt, 0:8],
                                    iota_b[:, 0:8], op=ALU.add)
        with p.block() as b:
            @b.gpsimd
            def _(g):
                g.tensor_reduce(idx_out[:], scr1.ap().rearrange("p (tt e) -> p tt e", tt=NTT)[:, :, 0:8],
                                axis=AX.X, op=ALU.min) if False else None
        with p.block() as b:
            @b.vector
            def _(v):
                v.tensor_reduce(idx_out[:], scr1.ap().rearrange("p (tt e) -> p tt e", tt=NTT)[:, :, 0:8],
                                axis=AX.X, op=ALU.min)

    argmax8(gvw[:, :, 64:72], cidx, "am_c")
    with p.block() as b:
        @b.vector
        def _(v):
            v.memset(selg[:], 0.0)
    for cc in range(8):
        with p.block() as b:
            @b.vector
            def _(v):
                v.tensor_scalar(mcol[:], cidx[:], float(cc), None, op0=ALU.is_equal)
        with p.block() as b:
            @b.vector
            def _(v):
                sgv = tiles(selg, 8)
                for tt in range(NTT):
                    v.scalar_tensor_tensor(sgv[:, tt], gvw[:, tt, 8 * cc:8 * (cc + 1)],
                                           mcol[:, tt:tt + 1], sgv[:, tt],
                                           op0=ALU.mult, op1=ALU.add)
    argmax8(tiles(selg, 8), lidx, "am_l")
    with p.block() as b:
        @b.vector
        def _(v):
            v.scalar_tensor_tensor(tidx[:], cidx[:], 8.0, lidx[:], op0=ALU.mult, op1=ALU.add)
    with p.block() as b:
        @b.vector
        def _(v):
            ohv = tiles(oh_tm, 64)
            for tt in range(NTT):
                v.tensor_scalar(ohv[:, tt], iota_b[:, 0:64], tidx[:, tt:tt + 1], None,
                                op0=ALU.is_equal)

    ar_cp = ExitStack()
    cw1_sb = p.sbuf("f_cw1", [128, NCH * 256], BF16, ar_cp)
    cb1_sb = p.sbuf("f_cb1", [128, 2], F32, ar_cp)
    cw2_sb = p.sbuf("f_cw2", [128, 2 * 2], BF16, ar_cp)
    hcomp = p.sbuf("f_hc", [128, 2 * T], BF16, ar_cp)
    comp_fm = p.sbuf("f_cfm", [2, T], F32, ar_cp)
    cb2_sb = p.sbuf("f_cb2", [1, 2], F32, ar_cp)
    ps_a = p.psum("f_psa", [128, 512], F32, ar_cp)
    ps_b = p.psum("f_psb", [128, 512], F32, ar_cp)
    s_cl = p.sem()
    s_cg = p.sem()
    with p.block() as b:
        @b.sync
        def _(sy):
            s_cl.dma(sy.dma_start(out=cb1_sb[:], in_=cb1_h))
            s_cl.dma(sy.dma_start(out=cb2_sb[:], in_=cb2_h))
        @b.gpsimd
        def _(g):
            s_cg.dma(g.dma_start(out=cw1_sb.ap().rearrange("p (ci o) -> p ci o", ci=NCH),
                                 in_=cw1_h.rearrange("(ci p) o -> p ci o", p=128)))
            s_cg.dma(g.dma_start(out=cw2_sb.ap().rearrange("p (ci o) -> p ci o", ci=2),
                                 in_=cw2_h.rearrange("(ci p) o -> p ci o", p=128)))
        @b.scalar
        def _(s_):
            s_.wait_ge(s_cl.h, 32)
            s_.wait_ge(s_cg.h, 32)
        @b.tensor
        def _(t):
            t.wait_ge(s_cl.h, 32)
            t.wait_ge(s_cg.h, 32)
    s_cm = p.sem(); s_ce = p.sem()
    cw1v = cw1_sb.ap().rearrange("p (ci o) -> p ci o", ci=NCH)
    with p.block() as b:
        @b.tensor
        def _(t):
            i = 0
            for oc2 in range(2):
                for sl in range(4):
                    if i >= 2:
                        t.wait_ge(s_ce.h, i - 1)
                    for ci in range(NCH):
                        inst = t.matmul(ps_a[:] if i % 2 == 0 else ps_b[:],
                                        cw1v[:, ci, 128 * oc2:128 * (oc2 + 1)],
                                        xnc[:, ci, 512 * sl:512 * (sl + 1)],
                                        start=(ci == 0), stop=(ci == NCH - 1))
                    s_cm.inc(inst)
                    i += 1
        @b.scalar
        def _(s_):
            i = 0
            hcv = hcomp.ap().rearrange("p (o t) -> p o t", o=2)
            for oc2 in range(2):
                for sl in range(4):
                    s_.wait_ge(s_cm.h, i + 1)
                    s_ce.inc(s_.activation(out=hcv[:, oc2, 512 * sl:512 * (sl + 1)],
                                           in_=ps_a[:] if i % 2 == 0 else ps_b[:],
                                           func=AF.Gelu, bias=cb1_sb[:, oc2:oc2 + 1]))
                    i += 1
    s_c2 = p.sem()
    ar_c2 = ExitStack()
    ps_c2 = p.psum("f_psc2", [2, T], F32, ar_c2)
    with p.block() as b:
        @b.tensor
        def _(t):
            hcv = hcomp.ap().rearrange("p (o t) -> p o t", o=2)
            cw2v = cw2_sb.ap().rearrange("p (ci o) -> p ci o", ci=2)
            for sl in range(4):
                for ci in range(2):
                    t.matmul(ps_c2[:, 512 * sl:512 * (sl + 1)], cw2v[:, ci],
                             hcv[:, ci, 512 * sl:512 * (sl + 1)],
                             start=(ci == 0), stop=False)
                inst = t.matmul(ps_c2[:, 512 * sl:512 * (sl + 1)], cb2_sb[:],
                                row1[:, 0:512], start=False, stop=True)
            s_c2.inc(inst)
        @b.scalar
        def _(s_):
            s_.wait_ge(s_c2.h, 1)
            s_c2.inc(s_.activation(out=comp_fm[:], in_=ps_c2[:], func=AF.Tanh))
    s_c3 = p.sem(); s_c4 = p.sem()
    with p.block() as b:
        @b.tensor
        def _(t):
            for tt in range(NTT):
                if tt >= 2:
                    t.wait_ge(s_c4.h, tt - 1)
                sl2 = 2 * (tt % 2)
                s_c3.inc(t.transpose(ps_t[:, sl2:sl2 + 2], comp_fm[:, 128 * tt:128 * (tt + 1)],
                                     c["ident_f"][0:2, 0:2]))
        @b.vector
        def _(v):
            cv2 = tiles(comp_tm, 2)
            for tt in range(NTT):
                v.wait_ge(s_c3.h, tt + 1)
                sl2 = 2 * (tt % 2)
                s_c4.inc(v.tensor_copy(cv2[:, tt], ps_t[:, sl2:sl2 + 2]))

    with p.block() as b:
        @b.vector
        def _(v):
            pass
    ar_c2.close()
    ar_cp.close()
    s_g1 = p.sem(); s_g2 = p.sem(); s_g3 = p.sem(); s_g4 = p.sem()
    ar_g = ExitStack()
    ohT = p.sbuf("f_ohT", [64, 128], F32, ar_g)
    ps_g = p.psum("f_psg", [128, 65], F32, ar_g)
    ps_g2 = p.psum("f_psg2", [64, 128], F32, ar_g)
    ohv = tiles(oh_tm, 64)
    gathv = tiles(gath, 65)
    for tt in range(NTT):
        with p.block() as b:
            @b.tensor
            def _(t):
                s_g1.inc(t.transpose(ps_g2[:], ohv[:, tt], c["ident_f"][:]))
            @b.vector
            def _(v):
                v.wait_ge(s_g1.h, s_g1.n)
                v.tensor_copy(ohT[:], ps_g2[:])
        with p.block() as b:
            @b.tensor
            def _(t):
                s_g2.inc(t.matmul(ps_g[:], ohT[:], spl_sb[:], start=True, stop=True))
            @b.vector
            def _(v):
                v.wait_ge(s_g2.h, s_g2.n)
                v.tensor_copy(gathv[:, tt], ps_g[:])

    with p.block() as b:
        @b.vector
        def _(v):
            pass
    ar_g.close()
    t1v = tiles(t1g, GRID)
    t2v = tiles(t2g, GRID)
    cv2 = tiles(comp_tm, 2)
    with p.block() as b:
        @b.vector
        def _(v):
            for tt in range(NTT):
                v.scalar_tensor_tensor(t1v[:, tt], gathv[:, tt, 0:GRID],
                                       cv2[:, tt, 0:1], gathv[:, tt, 32:48],
                                       op0=ALU.mult, op1=ALU.add)
    with p.block() as b:
        @b.vector
        def _(v):
            for tt in range(NTT):
                v.scalar_tensor_tensor(t2v[:, tt], gathv[:, tt, GRID:2 * GRID],
                                       cv2[:, tt, 1:2], t1v[:, tt],
                                       op0=ALU.mult, op1=ALU.add)
    with p.block() as b:
        @b.gpsimd
        def _(g):
            g.tensor_scalar(t2g[:], t2g[:], 0.0, None, op0=ALU.max)
    with p.block() as b:
        @b.vector
        def _(v):
            for tt in range(NTT):
                v.tensor_tensor(t1v[:, tt], t2v[:, tt], gathv[:, tt, 48:64],
                                op=ALU.mult)
    with p.block() as b:
        @b.scalar
        def _(s_):
            for tt in range(NTT):
                s_.activation(out=t2v[:, tt], in_=t1v[:, tt], func=AF.Copy,
                              accum_out=mag[:, tt:tt + 1])
    with p.block() as b:
        @b.gpsimd
        def _(g):
            for tt in range(NTT):
                g.tensor_tensor(mag[:, tt:tt + 1], mag[:, tt:tt + 1],
                                gathv[:, tt, 64:65], op=ALU.add)
    with p.block() as b:
        @b.gpsimd
        def _(g):
            g.tensor_scalar(mag[:], mag[:], osc_b[:], None, op0=ALU.mult)
    with p.block() as b:
        @b.vector
        def _(v):
            osv = tiles(ohs, 64)
            for tt in range(NTT):
                v.tensor_scalar(osv[:, tt], ohv[:, tt], mag[:, tt:tt + 1], None,
                                op0=ALU.mult)
    s_t1 = p.sem(); s_t2 = p.sem()
    ar_ot = ExitStack()
    ps_ot = [p.psum(f"f_psot{i}", [64, 128], BF16, ar_ot) for i in range(2)]
    with p.block() as b:
        @b.tensor
        def _(t):
            osv = tiles(ohs, 64)
            for tt in range(NTT):
                if tt >= 2:
                    t.wait_ge(s_t2.h, tt - 1)
                s_t1.inc(t.transpose(ps_ot[tt % 2][:], osv[:, tt], c["ident_bf"][:]))
        @b.vector
        def _(v):
            for tt in range(NTT):
                v.wait_ge(s_t1.h, tt + 1)
                s_t2.inc(v.tensor_copy(ohsT[:, 128 * tt:128 * (tt + 1)], ps_ot[tt % 2][:]))

    with p.block() as b:
        @b.vector
        def _(v):
            pass
    ar_ot.close()
    ar_t.close()
    ar_y = ExitStack()
    lhs_mu = p.sbuf("f_lmu", [128, 2], F32, ar_y)
    lhs_sq = p.sbuf("f_lsq", [128, 2], F32, ar_y)
    y_scr = p.sbuf("f_yscr", [128, T], F32, ar_y)
    sq_scr = p.sbuf("f_sqscr", [128, T], F32, ar_y)
    stats2b = ln["stats2"]
    mu2 = stats2b[0:1, :]
    rstd2 = ln["rstd_row"]
    mu2_b = ln["mu_b"]
    rstd2_b = ln["rstd_b"]
    ps_yc = p.psum("f_psyc", [128, T], F32, ar_y)
    ps_st2 = p.psum("f_pst2", [2, T], F32, ar_y)
    with p.block() as b:
        @b.vector
        def _(v):
            v.memset(lhs_mu[:], 0.0)
            v.memset(lhs_sq[:], 0.0)
    with p.block() as b:
        @b.vector
        def _(v):
            v.memset(lhs_mu[:, 0:1], 1.0 / D)
            v.memset(lhs_sq[:, 1:2], 1.0 / D)

    def emit_y_chunk(oc, s_my, s_ye, tag):
        with p.block() as b:
            @b.tensor
            def _(t):
                for tt in range(NTT):
                    inst = t.matmul(ps_yc[:, 128 * tt:128 * (tt + 1)],
                                    dirs_sb[:, 128 * oc:128 * (oc + 1)],
                                    ohsT[:, 128 * tt:128 * (tt + 1)],
                                    start=True, stop=True)
                s_my.inc(inst)
            @b.vector
            def _(v):
                v.wait_ge(s_my.h, s_my.n)
                s_ye.inc(v.tensor_tensor(y_scr[:], xnc[:, oc], ps_yc[:], op=ALU.add))

    s_my = p.sem(); s_ye = p.sem(); s_sq2 = p.sem(); s_sm2 = p.sem()
    for oc in range(NCH):
        emit_y_chunk(oc, s_my, s_ye, f"yA{oc}")
        with p.block() as b:
            @b.scalar
            def _(s_):
                s_sq2.inc(s_.activation(out=sq_scr[:], in_=y_scr[:], func=AF.Square))
            @b.tensor
            def _(t):
                for j in range(4):
                    t.matmul(ps_st2[:, 512 * j:512 * (j + 1)], lhs_mu[:],
                             y_scr[:, 512 * j:512 * (j + 1)],
                             start=(oc == 0), stop=False)
                t.wait_ge(s_sq2.h, oc + 1)
                for j in range(4):
                    inst = t.matmul(ps_st2[:, 512 * j:512 * (j + 1)], lhs_sq[:],
                                    sq_scr[:, 512 * j:512 * (j + 1)],
                                    start=False, stop=(oc == NCH - 1))
                s_sm2.inc(inst)
    with p.block() as b:
        @b.vector
        def _(v):
            v.tensor_copy(stats2b[:], ps_st2[:])
    s_x2 = p.sem()
    with p.block() as b:
        @b.gpsimd
        def _(g):
            s_x2.dma(g.dma_start(out=rstd2[:], in_=stats2b[1:2, :]))
            g.wait_ge(s_x2.h, 16)
        @b.vector
        def _(v):
            pass
    with p.block() as b:
        @b.vector
        def _(v):
            v.tensor_tensor(sq_scr[0:1, :], mu2, mu2, op=ALU.mult)
    with p.block() as b:
        @b.vector
        def _(v):
            v.tensor_tensor(rstd2[:], rstd2[:], sq_scr[0:1, :], op=ALU.subtract)
    with p.block() as b:
        @b.vector
        def _(v):
            v.tensor_scalar_add(rstd2[:], rstd2[:], 1.0e-5)
    with p.block() as b:
        @b.scalar
        def _(s_):
            s_.activation(out=rstd2[:], in_=rstd2[:], func=AF.Sqrt)
    with p.block() as b:
        @b.vector
        def _(v):
            v.reciprocal(rstd2[:], rstd2[:])
    s_bc2 = p.sem()
    with p.block() as b:
        @b.tensor
        def _(t):
            for j in range(4):
                inst = t.matmul(ps_yc[:, 512 * j:512 * (j + 1)],
                                c["ones_f"][:], mu2[:, 512 * j:512 * (j + 1)],
                                start=True, stop=True)
                s_bc2.inc(inst)
        @b.vector
        def _(v):
            v.wait_ge(s_bc2.h, 4)
            s_bc2.inc(v.tensor_copy(mu2_b[:], ps_yc[:]))
    with p.block() as b:
        @b.tensor
        def _(t):
            t.wait_ge(s_bc2.h, 5)
            for j in range(4):
                inst = t.matmul(ps_yc[:, 512 * j:512 * (j + 1)], c["ones_f"][:],
                                rstd2[:, 512 * j:512 * (j + 1)], start=True, stop=True)
                s_bc2.inc(inst)
        @b.vector
        def _(v):
            v.wait_ge(s_bc2.h, 9)
            v.tensor_copy(rstd2_b[:], ps_yc[:])
    s_my2 = p.sem(); s_ye2 = p.sem()
    for oc in range(NCH):
        emit_y_chunk(oc, s_my2, s_ye2, f"yB{oc}")
        with p.block() as b:
            @b.vector
            def _(v):
                v.tensor_tensor(y_scr[:], y_scr[:], mu2_b[:], op=ALU.subtract)
        with p.block() as b:
            @b.gpsimd
            def _(g):
                g.tensor_tensor(y_scr[:], y_scr[:], rstd2_b[:], op=ALU.mult)
        with p.block() as b:
            @b.vector
            def _(v):
                v.scalar_tensor_tensor(xc[:, oc], y_scr[:], gam_sb[:, oc:oc + 1],
                                       xc[:, oc], op0=ALU.mult, op1=ALU.add)
        with p.block() as b:
            @b.gpsimd
            def _(g):
                g.tensor_scalar(xc[:, oc], xc[:, oc], bet_sb[:, oc:oc + 1], None,
                                op0=ALU.add)
    with p.block() as b:
        @b.vector
        def _(v):
            pass
    ar_y.close()
    ar.close()



def build_L3():
    p = Prog()
    NH, DK, DV, CH = 4, 256, 256, 32
    x_h = p.din("x", [D, T], F32)
    wq_h = p.din("wq", [D, D], F32)
    wk_h = p.din("wk", [D, D], F32)
    wv_h = p.din("wv", [D, D], F32)
    wb_h = p.din("wb", [D, 4], F32)
    cqw_h = p.din("cqw", [D, 4], F32)
    ckw_h = p.din("ckw", [D, 4], F32)
    cvw_h = p.din("cvw", [D, 4], F32)
    fsw_h = p.din("fsw", [D, 5], F32)
    flw_h = p.din("flw", [D, 64], F32)
    mw1_h = p.din("mw1", [D, D], F32)
    mw1b_h = p.din("mw1b", [16, D], F32)
    mb1_h = p.din("mb1", [128, NCH], F32)
    mw2_h = p.din("mw2", [D, 4], F32)
    mb2_h = p.din("mb2", [1, 16], F32)
    ltemp_h = p.din("ltemp", [4, 1], F32)
    onw_h = p.din("onw", [128, 2], F32)
    wo_h = p.din("wo", [D, D], F32)
    fw1_h = p.din("fw1", [D, 4096], F32)
    fs1_h = p.din("fs1", [128, 32], F32)
    fw2_h = p.din("fw2", [D, 4096], F32)
    fs2_h = p.din("fs2", [128, 32], F32)
    fh1_h = p.din("fh1", [4096, 4096], F32)
    fh1s_h = p.din("fh1s", [128, 32], F32)
    fh2_h = p.din("fh2", [4096, 4096], F32)
    fh2s_h = p.din("fh2s", [128, 32], F32)
    fw3_h = p.din("fw3", [4096, D], F32)
    fs3_h = p.din("fs3", [128, 8], F32)
    y_h = p.dout("y", [D, T], F32)

    c = emit_consts(p, ident_bf=True, ident_f32=True, tril_s=True)
    xn_bf = p.sbuf("xn_bf", [128, NCH * T], BF16)

    m0 = p.sem_mark()
    ar_x1 = ExitStack()
    x_sb = p.sbuf("x_sb", [128, NCH * T], F32, ar_x1)
    emit_load_fm(p, x_h, x_sb)
    ln_ar = ExitStack()
    ln = emit_ln(p, c, x_sb, xn_bf, "ln", arena=ln_ar)
    ln_ar.close()
    ar_x1.close()
    p.sem_free_since(m0)

    attn_hbm = p.nc.dram_tensor("attn_scr", [D, T], BF16).ap()
    emit_l3_attn(p, c, xn_bf, attn_hbm,
                 wq_h, wk_h, wv_h, wb_h, cqw_h, ckw_h, cvw_h, fsw_h, flw_h,
                 mw1_h, mw1b_h, mb1_h, mw2_h, mb2_h, ltemp_h, onw_h, wo_h)
    emit_l3_ffn(p, c, xn_bf, x_h, attn_hbm, y_h,
                fw1_h, fs1_h, fw2_h, fs2_h, fh1_h, fh1s_h, fh2_h, fh2s_h,
                fw3_h, fs3_h)
    return p


def emit_l3_attn(p, c, xn_bf, attn_hbm,
                 wq_h, wk_h, wv_h, wb_h, cqw_h, ckw_h, cvw_h, fsw_h, flw_h,
                 mw1_h, mw1b_h, mb1_h, mw2_h, mb2_h, ltemp_h, onw_h, wo_h):
    NH, DK, DV, CH = 4, 256, 256, 32
    PADT = 64 + T
    ar = ExitStack()
    v_fm = p.sbuf("d_v", [128, NCH * PADT], BF16, ar)
    nbeta_tm = p.sbuf("d_nbtm", [128, NTT * NH], F32, ar)
    beta_tm = p.sbuf("d_btm", [128, NTT * NH], F32, ar)
    o_hbm = p.nc.dram_tensor("o_scr", [T, D], BF16).ap()
    ktm_hbm = p.nc.dram_tensor("ktm_scr", [T, D], BF16).ap()
    vtm_hbm = p.nc.dram_tensor("vtm_scr", [T, D], BF16).ap()
    ar_qk = ExitStack()
    q_fm = p.sbuf("d_q", [128, NCH * T], BF16, ar_qk)
    k_fm = p.sbuf("d_k", [128, NCH * T], BF16, ar_qk)
    xnc = chunks(xn_bf)
    vv = v_fm.ap().rearrange("p (ci t) -> p ci t", ci=NCH)

    m_pj = p.sem_mark()
    ar_pr = ExitStack()
    w_sb = p.sbuf("d_w", [128, NCH * D], BF16, ar_pr)
    pad = p.sbuf("d_pad", [128, NCH * PADT], BF16, ar_pr)
    cw_sb = p.sbuf("d_cw", [128, 3 * NCH * 4], F32, ar_pr)
    acc = p.sbuf("d_acc", [128, T], F32, ar_pr)
    tmp_c = p.sbuf("d_tmpc", [128, T], BF16, ar_pr)
    tmpB_c = p.sbuf("d_tmpbc", [128, T], BF16, ar_pr)
    padv = pad.ap().rearrange("p (ci t) -> p ci t", ci=NCH)
    ldc = p.sem()
    with p.block() as b:
        @b.sync
        def _(sy):
            cwv = cw_sb.ap().rearrange("p (w ci f) -> p w ci f", w=3, ci=NCH)
            for i, wh in enumerate([cqw_h, ckw_h, cvw_h]):
                ldc.dma(sy.dma_start(out=cwv[:, i], in_=wh.rearrange("(ci p) f -> p ci f", p=128)))
        @b.vector
        def _(v):
            v.memset(padv[:, :, 0:64], 0.0)
            v.memset(vv[:, :, 0:64], 0.0)
            v.wait_ge(ldc.h, 48)
    for wi, (wh, dst) in enumerate([(wq_h, q_fm), (wk_h, k_fm), (wv_h, v_fm)]):
        ld2 = p.sem()
        with p.block() as b:
            @b.gpsimd
            def _(g):
                ld2.dma(g.dma_start(out=w_sb.ap().rearrange("p (ci o) -> p ci o", ci=NCH),
                                    in_=wh.rearrange("(ci p) o -> p ci o", p=128)))
            @b.tensor
            def _(t):
                t.wait_ge(ld2.h, 16)
        emit_proj_fm(p, c, w_sb, xn_bf, pad, f"d_pj{wi}",
                     col_jobs=[(512 * j, 64 + 512 * j, 512) for j in range(4)])
        s1 = p.sem(); s2 = p.sem(); s3 = p.sem()
        voff = 64 if wi == 2 else 0
        dpitch = PADT if wi == 2 else T
        with p.block() as b:
            @b.vector
            def _(v):
                for ci in range(NCH):
                    wb0 = (wi * NCH + ci) * 4
                    if ci >= 1:
                        v.wait_ge(s3.h, ci)
                    s1.inc(v.tensor_scalar(acc[:], padv[:, ci, 64:64 + T],
                                           cw_sb[:, wb0 + 3:wb0 + 4], None, op0=ALU.mult))
                    v.wait_ge(s2.h, 3 * ci + 1)
                    s1.inc(v.scalar_tensor_tensor(acc[:], padv[:, ci, 62:62 + T],
                                                  cw_sb[:, wb0 + 1:wb0 + 2], acc[:],
                                                  op0=ALU.mult, op1=ALU.add))
                    v.wait_ge(s2.h, 3 * ci + 2)
                    s1.inc(v.tensor_tensor(acc[:], acc[:], tmp_c[:], op=ALU.add))
            @b.gpsimd
            def _(g):
                for ci in range(NCH):
                    wb0 = (wi * NCH + ci) * 4
                    g.wait_ge(s1.h, 3 * ci + 1)
                    s2.inc(g.tensor_scalar(tmp_c[:], padv[:, ci, 63:63 + T],
                                           cw_sb[:, wb0 + 2:wb0 + 3], None, op0=ALU.mult))
                    g.wait_ge(s1.h, 3 * ci + 2)
                    s2.inc(g.tensor_scalar(tmpB_c[:], padv[:, ci, 61:61 + T],
                                           cw_sb[:, wb0 + 0:wb0 + 1], None, op0=ALU.mult))
                    g.wait_ge(s1.h, 3 * ci + 3)
                    s2.inc(g.tensor_tensor(acc[:], acc[:], tmpB_c[:], op=ALU.add))
            @b.scalar
            def _(s):
                dv_ = dst.ap().rearrange("p (ci t) -> p ci t", ci=NCH)
                for ci in range(NCH):
                    s.wait_ge(s2.h, 3 * ci + 3)
                    s3.inc(s.activation(out=dv_[:, ci, voff:voff + T], in_=acc[:],
                                        func=AF.Silu))

    with p.block() as b:
        @b.vector
        def _(v):
            pass
    ar_pr.close()
    p.sem_free_since(m_pj)

    m_bn = p.sem_mark()
    ar_pr2 = ExitStack()
    s_b1 = p.sem(); s_b2 = p.sem()
    bfm = p.sbuf("d_bfm", [4, T], F32, ar_pr2)
    ar_bb = ExitStack()
    ps_bb = p.psum("d_psbb", [4, T], F32, ar_bb)
    wb_sb = p.sbuf("d_wb", [128, NCH * 4], BF16, ar_bb)
    with p.block() as b:
        @b.gpsimd
        def _(g):
            s_b1.dma(g.dma_start(out=wb_sb.ap().rearrange("p (ci f) -> p ci f", ci=NCH),
                                 in_=wb_h.rearrange("(ci p) f -> p ci f", p=128)))
        @b.tensor
        def _(t):
            t.wait_ge(s_b1.h, 16)
            wbv = wb_sb.ap().rearrange("p (ci f) -> p ci f", ci=NCH)
            for sl in range(4):
                for ci in range(NCH):
                    inst = t.matmul(ps_bb[:, 512 * sl:512 * (sl + 1)], wbv[:, ci],
                                    xnc[:, ci, 512 * sl:512 * (sl + 1)],
                                    start=(ci == 0), stop=(ci == NCH - 1))
            s_b2.inc(inst)
        @b.scalar
        def _(s):
            s.wait_ge(s_b2.h, 1)
            s_b2.inc(s.activation(out=bfm[:], in_=ps_bb[:], func=AF.Sigmoid))
    ar_bb.close()
    emit_transpose_to_tm(p, c, bfm, 4, beta_tm, "d_bt", dt=F32)
    with p.block() as b:
        @b.vector
        def _(v):
            v.tensor_scalar_mul(nbeta_tm[:], beta_tm[:], -1.0)

    sqs = p.sbuf("d_sqs", [128, T], F32, ar_pr2)
    nrm8 = p.sbuf("d_nrm8", [8, T], F32, ar_pr2)
    ps_n = p.psum("d_psn", [8, T], F32, ar_pr2)
    ps_nb = p.psum("d_psnb", [128, T], F32, ar_pr2)
    s_n1 = p.sem(); s_n2 = p.sem(); s_n3 = p.sem(); s_n4 = p.sem()
    with p.block() as b:
        @b.scalar
        def _(s):
            for fi, src in enumerate([q_fm, k_fm]):
                sv_ = src.ap().rearrange("p (ci t) -> p ci t", ci=NCH)
                for ci in range(NCH):
                    i = fi * NCH + ci
                    if i >= 1:
                        s.wait_ge(s_n1.h, i)
                    s_n2.inc(s.activation(out=sqs[:], in_=sv_[:, ci], func=AF.Square))
        @b.tensor
        def _(t):
            for fi in range(2):
                for ci in range(NCH):
                    i = fi * NCH + ci
                    r = fi * 4 + ci // 2
                    t.wait_ge(s_n2.h, i + 1)
                    for j in range(4):
                        inst = t.matmul(ps_n[:, 512 * j:512 * (j + 1)],
                                        c["stairs16"][:, 15 - r:23 - r],
                                        sqs[:, 512 * j:512 * (j + 1)],
                                        start=(i == 0), stop=(i == 15))
                    s_n1.inc(inst)
    with p.block() as b:
        @b.vector
        def _(v):
            v.tensor_scalar_add(nrm8[:], ps_n[:], 1.0e-6)
    with p.block() as b:
        @b.scalar
        def _(s):
            s.activation(out=nrm8[:], in_=nrm8[:], func=AF.Sqrt)
    with p.block() as b:
        @b.vector
        def _(v):
            v.reciprocal(nrm8[:], nrm8[:])
    nrow1 = p.sbuf("d_nrow1", [1, T], F32, ar_pr2)
    s_bc3 = p.sem()
    s_bc4 = p.sem()
    for fi, src in enumerate([q_fm, k_fm]):
        sv_ = src.ap().rearrange("p (ci t) -> p ci t", ci=NCH)
        for hh in range(NH):
            r = fi * 4 + hh
            b3 = s_bc3.n
            b4 = s_bc4.n
            with p.block() as b:
                @b.gpsimd
                def _(g):
                    s_bc3.dma(g.dma_start(out=nrow1[:], in_=nrm8[r:r + 1, :]))
                    g.wait_ge(s_bc3.h, b3 + 16)
            with p.block() as b:
                @b.tensor
                def _(t):
                    for j in range(4):
                        inst = t.matmul(ps_nb[:, 512 * j:512 * (j + 1)], c["ones_f"][:],
                                        nrow1[:, 512 * j:512 * (j + 1)],
                                        start=True, stop=True)
                    s_bc4.inc(inst)
                @b.vector
                def _(v):
                    v.wait_ge(s_bc4.h, b4 + 1)
                    v.tensor_tensor(sv_[:, 2 * hh], sv_[:, 2 * hh], ps_nb[:], op=ALU.mult)
                    v.tensor_tensor(sv_[:, 2 * hh + 1], sv_[:, 2 * hh + 1], ps_nb[:], op=ALU.mult)
    with p.block() as b:
        @b.vector
        def _(v):
            pass
    ar_pr2.close()
    p.sem_free_since(m_bn)

    m_dl = p.sem_mark()
    ar_d = ExitStack()
    emit_transpose_fm_to_hbm(p, c, k_fm, ktm_hbm, "d_kt")
    emit_transpose_fm_to_hbm(p, c, v_fm, vtm_hbm, "d_vt", src_pitch=PADT, src_off=64)
    ktb = p.sbuf("d_ktb", [128, 2 * D], BF16, ar_d)
    vtb = p.sbuf("d_vtb", [128, 2 * D], BF16, ar_d)
    o_stg = p.sbuf("d_ostg", [128, D], BF16, ar_d)

    m_sl = p.sbuf("d_msl", [128, 32], F32, ar_d)
    m_il = p.sbuf("d_milT", [128, 32], F32, ar_d)
    id32 = p.sbuf("d_id32", [128, 32], F32, ar_d)
    id32b = p.sbuf("d_id32b", [128, 32], BF16, ar_d)
    with p.block() as b:
        @b.vector
        def _(v):
            for blk in range(4):
                sl_ = slice(32 * blk, 32 * (blk + 1))
                v.tensor_scalar(m_sl[sl_, :], c["iota_d"][sl_, 0:32], -32 * blk,
                                None, op0=ALU.is_lt)
                v.tensor_scalar(m_il[sl_, :], c["iota_d"][sl_, 0:32], -32 * blk,
                                None, op0=ALU.is_ge)
                v.tensor_scalar(id32[sl_, :], c["iota_d"][sl_, 0:32], -32 * blk,
                                None, op0=ALU.is_equal)
    with p.block() as b:
        @b.vector
        def _(v):
            v.tensor_copy(id32b[:], id32[:])
    S_sb = p.sbuf("d_S", [128, NH * 2 * 256], F32, ar_d)
    S_bf = p.sbuf("d_Sbf", [128, NH * 2 * 256], BF16, ar_d)
    A_s = p.sbuf("d_A", [128, NH * 32], F32, ar_d)
    At_s = p.sbuf("d_At", [128, NH * 32], F32, ar_d)
    P_s = p.sbuf("d_P", [128, NH * 32], F32, ar_d)
    Pt_s = p.sbuf("d_Pt", [128, NH * 32], F32, ar_d)
    P2_s = p.sbuf("d_P2", [128, NH * 32], F32, ar_d)
    P2t_s = p.sbuf("d_P2t", [128, NH * 32], F32, ar_d)
    alT_s = p.sbuf("d_alT", [128, NH * 32], F32, ar_d)
    yu_s = p.sbuf("d_yu", [128, NH * 256], F32, ar_d)
    yw_s = p.sbuf("d_yw", [128, NH * 256], F32, ar_d)
    ua_s = p.sbuf("d_ua", [128, NH * 256], F32, ar_d)
    wfm_s = p.sbuf("d_wfm", [128, NH * 2 * 32], F32, ar_d)
    chb_s = p.sbuf("d_chb", [128, NH * 256], BF16, ar_d)
    ps_g = p.psum("d_psg", [128, NH * 32], F32, ar_d)
    ps_g2 = p.psum("d_psg2", [128, NH * 32], F32, ar_d)
    ps_pool = p.psum("d_pspool", [128, 2048], F32, ar_d)
    with p.block() as b:
        @b.vector
        def _(v):
            v.memset(S_sb[:], 0.0)
            v.memset(S_bf[:], 0.0)
    Av = A_s.ap().rearrange("p (h x) -> p h x", h=NH)
    Atv = At_s.ap().rearrange("p (h x) -> p h x", h=NH)
    Pv = P_s.ap().rearrange("p (h x) -> p h x", h=NH)
    Ptv = Pt_s.ap().rearrange("p (h x) -> p h x", h=NH)
    P2v = P2_s.ap().rearrange("p (h x) -> p h x", h=NH)
    P2tv = P2t_s.ap().rearrange("p (h x) -> p h x", h=NH)
    alv = alT_s.ap().rearrange("p (h x) -> p h x", h=NH)
    yuv = yu_s.ap().rearrange("p (h x) -> p h x", h=NH)
    ywv = yw_s.ap().rearrange("p (h x) -> p h x", h=NH)
    uav = ua_s.ap().rearrange("p (h x) -> p h x", h=NH)
    wfv = wfm_s.ap().rearrange("p (h pc x) -> p h pc x", h=NH, pc=2)
    chv = chb_s.ap().rearrange("p (h x) -> p h x", h=NH)
    Sv = S_sb.ap().rearrange("p (h pc x) -> p h pc x", h=NH, pc=2)
    Sbv = S_bf.ap().rearrange("p (h pc x) -> p h pc x", h=NH, pc=2)
    gv = ps_g.ap().rearrange("p (h x) -> p h x", h=NH)
    g2v = ps_g2.ap().rearrange("p (h x) -> p h x", h=NH)
    poolA = ps_pool[:, 0:1024].rearrange("p (h x) -> p h x", h=NH)
    poolB = ps_pool[:, 1024:2048].rearrange("p (h x) -> p h x", h=NH)
    qv = q_fm.ap().rearrange("p (ci t) -> p ci t", ci=NCH)
    kv = k_fm.ap().rearrange("p (ci t) -> p ci t", ci=NCH)
    ktqv = ktb.ap().rearrange("p (bb e) -> p bb e", bb=2)
    vtqv = vtb.ap().rearrange("p (bb e) -> p bb e", bb=2)
    btv = tiles(beta_tm, NH)
    nbtv = tiles(nbeta_tm, NH)

    D3 = dict(locals())
    emit_delta_chunks(p, c, D3)
    with p.block() as b:
        @b.vector
        def _(v):
            pass
    ar_d.close()
    ar_qk.close()
    p.sem_free_since(m_dl)
    emit_l3_post(p, c, xn_bf, v_fm, o_hbm, attn_hbm, fsw_h, flw_h,
                 mw1_h, mw1b_h, mb1_h, mw2_h, mb2_h, ltemp_h, onw_h, wo_h)
    with p.block() as b:
        @b.vector
        def _(v):
            pass
    ar.close()

def emit_transpose_to_tm_wide(p, c, src_fm, dst_tm, tag, src_pitch=T, src_off=0):
    ar = ExitStack()
    ps = [p.psum(f"{tag}{i}", [128, 1024], BF16, ar) for i in range(2)]
    s_t = p.sem(); s_e = p.sem()
    sv = src_fm.ap().rearrange("p (ci t) -> p ci t", ci=NCH)
    dv = tiles(dst_tm, D)
    with p.block() as b:
        @b.tensor
        def _(t):
            i = 0
            for tt in range(NTT):
                if tt >= 2:
                    t.wait_ge(s_e.h, tt - 1)
                for ci in range(NCH):
                    inst = t.transpose(ps[tt % 2][:, 128 * ci:128 * (ci + 1)],
                                       sv[:, ci, src_off + 128 * tt:src_off + 128 * (tt + 1)],
                                       c["ident_bf"][:])
                s_t.inc(inst)
                i += 1
        @b.vector
        def _(v):
            for tt in range(NTT):
                v.wait_ge(s_t.h, tt + 1)
                s_e.inc(v.tensor_copy(dv[:, tt], ps[tt % 2][:]))
    with p.block() as b:
        @b.vector
        def _(v):
            pass
    ar.close()


def emit_transpose_fm_to_hbm(p, c, src_fm, dst_hbm, tag, src_pitch=T, src_off=0):
    ar = ExitStack()
    ps = [p.psum(f"{tag}{i}", [128, 1024], BF16, ar) for i in range(2)]
    stg = [p.sbuf(f"{tag}s{i}", [128, 1024], BF16, ar) for i in range(2)]
    s_t = p.sem(f"{tag}_t"); s_e = p.sem(f"{tag}_e")
    s_d = [p.sem(f"{tag}_d{i}") for i in range(2)]
    sv = src_fm.ap().rearrange("p (ci t) -> p ci t", ci=NCH)
    with p.block() as b:
        @b.tensor
        def _(t):
            for tt in range(NTT):
                if tt >= 2:
                    t.wait_ge(s_e.h, tt - 1)
                for ci in range(NCH):
                    inst = t.transpose(ps[tt % 2][:, 128 * ci:128 * (ci + 1)],
                                       sv[:, ci, src_off + 128 * tt:src_off + 128 * (tt + 1)],
                                       c["ident_bf"][:])
                s_t.inc(inst)
        @b.vector
        def _(v):
            for tt in range(NTT):
                v.wait_ge(s_t.h, tt + 1)
                if tt >= 2:
                    v.wait_ge(s_d[tt % 2].h, 16 * (tt // 2))
                s_e.inc(v.tensor_copy(stg[tt % 2][:], ps[tt % 2][:]))
        @b.sync
        def _(sy):
            for tt in range(NTT):
                sy.wait_ge(s_e.h, tt + 1)
                s_d[tt % 2].dma(sy.dma_start(out=dst_hbm[128 * tt:128 * (tt + 1), :],
                                             in_=stg[tt % 2][:]))
            sy.wait_ge(s_d[0].h, 16 * (NTT // 2))
            sy.wait_ge(s_d[1].h, 16 * (NTT // 2))
    with p.block() as b:
        @b.vector
        def _(v):
            pass
    ar.close()


def emit_delta_chunks(p, c, X):
    NH, CH = 4, 32
    qv, kv = X["qv"], X["kv"]
    ktqv, vtqv, o_stg = X["ktqv"], X["vtqv"], X["o_stg"]
    ktm_hbm, vtm_hbm, o_hbm = X["ktm_hbm"], X["vtm_hbm"], X["o_hbm"]
    btv, nbtv = X["btv"], X["nbtv"]
    Av, Atv = X["Av"], X["Atv"]
    A_s, At_s = X["A_s"], X["At_s"]
    m_sl, m_il, id32 = X["m_sl"], X["m_il"], X["id32"]
    yu_s, yw_s, ua_s, wfm_s = X["yu_s"], X["yw_s"], X["ua_s"], X["wfm_s"]
    yuv, ywv, uav, wfv = X["yuv"], X["ywv"], X["uav"], X["wfv"]
    S_sb, Sv = X["S_sb"], X["Sv"]
    S_bf, Sbv = X["S_bf"], X["Sbv"]
    chb_s, chv = X["chb_s"], X["chv"]
    ps_g, ps_g2, ps_pool = X["ps_g"], X["ps_g2"], X["ps_pool"]
    gv, g2v = X["gv"], X["g2v"]
    poolA, poolB = X["poolA"], X["poolB"]
    pw = {}
    for nm in ["p2", "p2t", "p4", "p4t", "p8", "p8t", "p16t"]:
        pw[nm] = p.sbuf(f"d_{nm}", [128, NH * 32], F32, X["ar_d"])
    pwv = {nm: t.ap().rearrange("p (h x) -> p h x", h=NH) for nm, t in pw.items()}
    sm = {k: p.sem(f"dl_{k}") for k in ["ab", "asc", "amk", "att", "ate", "pw", "pwe",
                               "ci", "ch", "cha", "wt", "wte", "ws", "uad",
                               "al", "ale", "o", "oe", "su", "sa", "g2",
                               "kvl0", "kvl1", "odm0", "odm1", "odm2", "odm3",
                               "uab", "sbf", "ob"]}
    N = {k: 0 for k in sm}

    def W(eng, k, n):
        eng.wait_ge(sm[k].h, n)

    with p.block() as blk:
        @blk.tensor
        def _(t):
            for ci_ in range(64):
                tile, rb = ci_ // 4, 32 * (ci_ % 4)
                cc = slice(32 * ci_, 32 * (ci_ + 1))
                rbs = slice(rb, rb + 32)
                if ci_ >= 1:
                    W(t, "asc", 4 * ci_)
                    W(t, "ate", ci_)
                for h in range(NH):
                    for pc in range(2):
                        inst = t.matmul(gv[rbs, h], kv[:, 2 * h + pc, cc],
                                        kv[:, 2 * h + pc, cc],
                                        start=(pc == 0), stop=(pc == 1),
                                        tile_position=(0, rb))
                sm["ab"].inc(inst)
                if ci_ >= 1:
                    W(t, "ale", ci_)
                for h in range(NH):
                    for pc in range(2):
                        inst = t.matmul(g2v[rbs, h], kv[:, 2 * h + pc, cc],
                                        qv[:, 2 * h + pc, cc],
                                        start=(pc == 0), stop=(pc == 1),
                                        tile_position=(0, rb))
                sm["g2"].inc(inst)
                W(t, "amk", ci_ + 1)
                for h in range(NH):
                    inst = t.transpose(gv[0:32, h], A_s[rbs, 32 * h:32 * (h + 1)],
                                       id32[rbs, :], tile_position=(rb, 0))
                sm["att"].inc(inst)
                W(t, "ate", ci_ + 1)
                seqs = [("A", "At", "p2", "p2t"), ("p2", "p2t", "p4", "p4t"),
                        ("p4", "p4t", "p8", "p8t"), ("p8", "p8t", None, "p16t")]
                for li, (pn, ptn, qn, qtn) in enumerate(seqs):
                    pv = Av if pn == "A" else pwv[pn]
                    ptv_ = Atv if ptn == "At" else pwv[ptn]
                    W(t, "pwe", 2 * (4 * ci_ + li))
                    for h in range(NH):
                        if qn is not None:
                            t.matmul(gv[rbs, h], ptv_[rbs, h], pv[rbs, h],
                                     start=True, stop=True, tile_position=(rb, rb))
                        inst = t.matmul(g2v[rbs, h], pv[rbs, h], ptv_[rbs, h],
                                        start=True, stop=True, tile_position=(rb, rb))
                    sm["pw"].inc(inst)
                W(t, "ci", 2 * (ci_ + 1))
                if ci_ >= 1:
                    W(t, "sa", ci_)
                for li, ptn in enumerate(["p16t", "p8t", "p4t", "p2t", "At"]):
                    ptv_ = Atv if ptn == "At" else pwv[ptn]
                    W(t, "cha", 2 * (5 * ci_ + li))
                    if li < 4:
                        W(t, "pwe", 8 * ci_ + 2 * (4 - li))
                    for h in range(NH):
                        t.matmul(poolA[rbs, h], ptv_[rbs, h], yuv[rbs, h],
                                 start=True, stop=True, tile_position=(rb, rb))
                        inst = t.matmul(poolB[rbs, h], ptv_[rbs, h], ywv[rbs, h],
                                        start=True, stop=True, tile_position=(rb, rb))
                    sm["ch"].inc(inst)
                W(t, "cha", 2 * (5 * ci_ + 5))
                for h in range(NH):
                    for pc in range(2):
                        inst = t.transpose(ps_pool[:, 64 * h + 32 * pc:64 * h + 32 * pc + 32],
                                           yw_s[rbs, 256 * h + 128 * pc:256 * h + 128 * (pc + 1)],
                                           id32[rbs, :], tile_position=(rb, 0))
                sm["wt"].inc(inst)
                W(t, "wte", ci_ + 1)
                W(t, "sa", ci_)
                for h in range(NH):
                    for pc in range(2):
                        inst = t.matmul(poolB[rbs, h], wfv[:, h, pc, :],
                                        Sv[:, h, pc, :],
                                        start=(pc == 0), stop=(pc == 1),
                                        tile_position=(0, rb))
                sm["ws"].inc(inst)
                W(t, "uad", ci_ + 1)
                W(t, "ale", ci_ + 1)
                W(t, "sbf", ci_)
                for h in range(NH):
                    for pc in range(2):
                        t.matmul(poolA[rbs, h], qv[:, 2 * h + pc, cc],
                                 Sbv[:, h, pc, :],
                                 start=(pc == 0), stop=(pc == 1),
                                 tile_position=(0, rb))
                    inst = t.matmul(poolB[rbs, h], X["alv"][rbs, h],
                                    uav[rbs, h],
                                    start=True, stop=True,
                                    tile_position=(rb, rb))
                sm["o"].inc(inst)
                W(t, "oe", ci_ + 1)
                W(t, f"kvl{tile % 2}", 32 * (tile // 2 + 1))
                W(t, "uab", ci_ + 1)
                for h in range(NH):
                    for pc in range(2):
                        inst = t.matmul(ps_pool[:, 512 * h + 256 * pc:512 * h + 256 * (pc + 1)],
                                        ktqv[rbs, tile % 2, 256 * h + 128 * pc:256 * h + 128 * (pc + 1)],
                                        chb_s[rbs, 256 * h:256 * (h + 1)],
                                        start=True, stop=True,
                                        tile_position=(rb, 0))
                sm["su"].inc(inst)

        @blk.vector
        def _(v):
            for ci_ in range(64):
                tile, rb = ci_ // 4, 32 * (ci_ % 4)
                rbs = slice(rb, rb + 32)
                W(v, "ab", ci_ + 1)
                for h in range(NH):
                    inst = v.tensor_scalar(A_s[rbs, 32 * h:32 * (h + 1)], gv[rbs, h],
                                           nbtv[rbs, tile, h:h + 1], None, op0=ALU.mult)
                sm["asc"].inc(inst, 4)
                W(v, "att", ci_ + 1)
                sm["ate"].inc(v.tensor_copy(At_s[rbs, :], ps_g[0:32, :]))
                for li, (qn, qtn) in enumerate([("p2", "p2t"), ("p4", "p4t"),
                                                ("p8", "p8t"), (None, "p16t")]):
                    W(v, "pw", 4 * ci_ + li + 1)
                    if qn is not None:
                        v.tensor_copy(pw[qn][rbs, :], ps_g[rbs, :])
                    sm["pwe"].inc(v.tensor_copy(pw[qtn][rbs, :], ps_g2[rbs, :]), 2)
                for li in range(5):
                    W(v, "ch", 5 * ci_ + li + 1)
                    sm["cha"].inc(v.tensor_tensor(yu_s[rbs, :], yu_s[rbs, :],
                                                  ps_pool[rbs, 0:1024], op=ALU.add))
                    sm["cha"].inc(v.tensor_tensor(yw_s[rbs, :], yw_s[rbs, :],
                                                  ps_pool[rbs, 1024:2048], op=ALU.add))
                W(v, "wt", ci_ + 1)
                sm["wte"].inc(v.tensor_copy(wfm_s[:, :], ps_pool[:, 0:256]))
                W(v, "ws", ci_ + 1)
                sm["uad"].inc(v.tensor_tensor(ua_s[rbs, :], yu_s[rbs, :],
                                              ps_pool[rbs, 1024:2048], op=ALU.subtract))
                W(v, "g2", ci_ + 1)
                sm["ale"].inc(v.tensor_tensor(
                    X["alT_s"][rbs, :].rearrange("p (h x) -> p h x", h=NH),
                    ps_g2[rbs, :].rearrange("p (h x) -> p h x", h=NH),
                    m_il[rbs, :].unsqueeze(1).broadcast_to([32, NH, 32]),
                    op=ALU.mult))
                W(v, "ob", ci_ + 1)
                sm["oe"].inc(v.tensor_tensor(o_stg[rbs, :], o_stg[rbs, :],
                                             ps_pool[rbs, 0:1024], op=ALU.add))
                W(v, "su", ci_ + 1)
                sm["sa"].inc(v.tensor_tensor(S_sb[:, :], S_sb[:, :],
                                             ps_pool[:, 0:2048], op=ALU.add))

        @blk.gpsimd
        def _(g):
            for ci_ in range(64):
                tile, rb = ci_ // 4, 32 * (ci_ % 4)
                rbs = slice(rb, rb + 32)
                W(g, "asc", 4 * (ci_ + 1))
                sm["amk"].inc(g.tensor_tensor(
                    A_s[rbs, :].rearrange("p (h x) -> p h x", h=NH),
                    A_s[rbs, :].rearrange("p (h x) -> p h x", h=NH),
                    m_sl[rbs, :].unsqueeze(1).broadcast_to([32, NH, 32]),
                    op=ALU.mult))
                W(g, "oe", ci_)
                W(g, f"kvl{tile % 2}", 32 * (tile // 2 + 1))
                for h in range(NH):
                    g.tensor_scalar(yuv[rbs, h], vtqv[rbs, tile % 2, 256 * h:256 * (h + 1)],
                                    btv[rbs, tile, h:h + 1], None, op0=ALU.mult)
                    inst = g.tensor_scalar(ywv[rbs, h], ktqv[rbs, tile % 2, 256 * h:256 * (h + 1)],
                                           btv[rbs, tile, h:h + 1], None, op0=ALU.mult)
                sm["ci"].inc(inst, 2)

        @blk.scalar
        def _(s_):
            for ci_ in range(64):
                tile, rb = ci_ // 4, 32 * (ci_ % 4)
                rbs = slice(rb, rb + 32)
                W(s_, "uad", ci_ + 1)
                if ci_ >= 1:
                    W(s_, "su", ci_)
                sm["uab"].inc(s_.copy(chb_s[rbs, :], ua_s[rbs, :]))
                W(s_, "o", ci_ + 1)
                if ci_ >= 4:
                    W(s_, f"odm{ci_ % 4}", 16 * (ci_ // 4))
                sm["ob"].inc(s_.copy(o_stg[rbs, :], ps_pool[rbs, 1024:2048]))
                W(s_, "sa", ci_ + 1)
                sm["sbf"].inc(s_.copy(S_bf[:], S_sb[:]))

        @blk.sync
        def _(sy):
            for ci_ in range(64):
                tile, rb = ci_ // 4, 32 * (ci_ % 4)
                rbs = slice(rb, rb + 32)
                if ci_ % 4 == 0:
                    if tile >= 2:
                        W(sy, "su", 4 * tile - 4)
                        W(sy, "ci", 2 * (4 * tile - 4))
                    sm[f"kvl{tile % 2}"].dma(sy.dma_start(out=ktqv[:, tile % 2, :],
                                             in_=ktm_hbm[128 * tile:128 * (tile + 1), :]))
                    sm[f"kvl{tile % 2}"].dma(sy.dma_start(out=vtqv[:, tile % 2, :],
                                             in_=vtm_hbm[128 * tile:128 * (tile + 1), :]))
                W(sy, "oe", ci_ + 1)
                sm[f"odm{ci_ % 4}"].dma(sy.dma_start(out=o_hbm[32 * ci_:32 * (ci_ + 1), :],
                                        in_=o_stg[rbs, :]))
            for q4 in range(4):
                sy.wait_ge(sm[f"odm{q4}"].h, 16 * 16)
    return

def emit_l3_post(p, c, xn_bf, v_fm, o_hbm, attn_hbm, fsw_h, flw_h,
                 mw1_h, mw1b_h, mb1_h, mw2_h, mb2_h, ltemp_h, onw_h, wo_h):
    NH, DV = 4, 256
    PADT = 64 + T
    m_p1 = p.sem_mark()
    ar = ExitStack()
    ls_fm = p.sbuf("g_ls", [128, NCH * T], BF16, ar)
    ll_hbm = p.nc.dram_tensor("ll_scr", [D, T], BF16).ap()
    llh = ll_hbm.rearrange("(cc p) t -> p cc t", p=128)
    fw_sb = p.sbuf("g_fw", [128, NCH * 69], F32, ar)
    diag_bf = [p.sbuf(f"g_diag{i}", [128, 128], BF16, ar) for i in range(2)]
    mw1_sb = p.sbuf("g_mw1", [128, NCH * D], BF16, ar)
    mw1b_sb = p.sbuf("g_mw1b", [16, D], BF16, ar)
    mb1_sb = p.sbuf("g_mb1", [128, NCH], F32, ar)
    mw2_sb = p.sbuf("g_mw2", [128, NCH * 4], BF16, ar)
    mb2_sb = p.sbuf("g_mb2", [1, 16], F32, ar)
    onw_sb = p.sbuf("g_onw", [128, 2], F32, ar)
    tinv_b = p.sbuf("g_tinv", [128, 4], F32, ar)
    ltp = p.sbuf("g_ltp", [4, 1], F32, ar)
    ltr = p.sbuf("g_ltr", [1, 4], F32, ar)
    xnc = chunks(xn_bf)
    vv = v_fm.ap().rearrange("p (ci t) -> p ci t", ci=NCH)
    lsv = chunks(ls_fm)
    ld = p.sem(); ldg = p.sem()
    with p.block() as b:
        @b.sync
        def _(sy):
            fwv = fw_sb.ap().rearrange("p (ci f) -> p ci f", ci=NCH)
            ld.dma(sy.dma_start(out=fwv[:, :, 0:5], in_=fsw_h.rearrange("(ci p) f -> p ci f", p=128)))
            ld.dma(sy.dma_start(out=fwv[:, :, 5:69], in_=flw_h.rearrange("(ci p) f -> p ci f", p=128)))
            ld.dma(sy.dma_start(out=mb1_sb[:], in_=mb1_h))
            ld.dma(sy.dma_start(out=mb2_sb[:], in_=mb2_h))
            ld.dma(sy.dma_start(out=onw_sb[:], in_=onw_h))
            ld.dma(sy.dma_start(out=ltp[:], in_=ltemp_h))
        @b.gpsimd
        def _(g):
            ldg.dma(g.dma_start(out=mw1_sb.ap().rearrange("p (ci o) -> p ci o", ci=NCH),
                                in_=mw1_h.rearrange("(ci p) o -> p ci o", p=128)))
            ldg.dma(g.dma_start(out=mw1b_sb[:], in_=mw1b_h))
            ldg.dma(g.dma_start(out=mw2_sb.ap().rearrange("p (ci o) -> p ci o", ci=NCH),
                                in_=mw2_h.rearrange("(ci p) o -> p ci o", p=128)))
        @b.vector
        def _(v):
            v.wait_ge(ld.h, 96)
            v.wait_ge(ldg.h, 48)
    s_t = p.sem()
    ar_t = ExitStack()
    ps_t4 = p.psum("g_pst4", [128, 8], F32, ar_t)
    with p.block() as b:
        @b.scalar
        def _(s):
            s_t.inc(s.activation(out=ltp[:], in_=ltp[:], func=AF.Exp, scale=-1.0))
        @b.tensor
        def _(t):
            t.wait_ge(s_t.h, 1)
            s_t.inc(t.transpose(ps_t4[0:1, 0:4], ltp[:], c["ident_f"][0:4, 0:4]))
        @b.vector
        def _(v):
            v.wait_ge(s_t.h, 2)
            v.tensor_copy(ltr[:], ps_t4[0:1, 0:4])
    s_t2 = p.sem()
    with p.block() as b:
        @b.tensor
        def _(t):
            s_t2.inc(t.matmul(ps_t4[:, 4:8], c["ones_f"][:], ltr[:], start=True, stop=True))
        @b.vector
        def _(v):
            v.wait_ge(s_t2.h, 1)
            v.tensor_copy(tinv_b[:], ps_t4[:, 4:8])
    with p.block() as b:
        @b.vector
        def _(v):
            pass
    ar_t.close()

    ar_f = ExitStack()
    ar_fl = ExitStack()
    facc = p.sbuf("g_facc", [128, T], F32, ar_fl)
    tmpA_f = p.sbuf("g_tmpa", [128, T], BF16, ar_fl)
    tmpB_f = p.sbuf("g_tmpb", [128, T], BF16, ar_fl)
    s_f1 = p.sem(); s_f2 = p.sem(); s_f3 = p.sem()
    fwv = fw_sb.ap().rearrange("p (ci f) -> p ci f", ci=NCH)
    with p.block() as b:
        @b.vector
        def _(v):
            for ci in range(NCH):
                if ci >= 1:
                    v.wait_ge(s_f3.h, ci)
                s_f1.inc(v.tensor_scalar(facc[:], vv[:, ci, 64:64 + T],
                                         fwv[:, ci, 4:5], None, op0=ALU.mult))
                v.wait_ge(s_f2.h, 3 * ci + 1)
                s_f1.inc(v.scalar_tensor_tensor(facc[:], vv[:, ci, 62:62 + T],
                                                fwv[:, ci, 2:3], facc[:],
                                                op0=ALU.mult, op1=ALU.add))
                v.wait_ge(s_f2.h, 3 * ci + 2)
                s_f1.inc(v.tensor_tensor(facc[:], facc[:], tmpA_f[:], op=ALU.add))
                v.wait_ge(s_f2.h, 3 * ci + 3)
                s_f1.inc(v.scalar_tensor_tensor(facc[:], vv[:, ci, 60:60 + T],
                                                fwv[:, ci, 0:1], facc[:],
                                                op0=ALU.mult, op1=ALU.add))
        @b.gpsimd
        def _(g):
            for ci in range(NCH):
                g.wait_ge(s_f1.h, 4 * ci + 1)
                s_f2.inc(g.tensor_scalar(tmpA_f[:], vv[:, ci, 63:63 + T],
                                         fwv[:, ci, 3:4], None, op0=ALU.mult))
                g.wait_ge(s_f1.h, 4 * ci + 2)
                s_f2.inc(g.tensor_scalar(tmpB_f[:], vv[:, ci, 61:61 + T],
                                         fwv[:, ci, 1:2], None, op0=ALU.mult))
                g.wait_ge(s_f1.h, 4 * ci + 3)
                s_f2.inc(g.tensor_tensor(facc[:], facc[:], tmpB_f[:], op=ALU.add))
        @b.scalar
        def _(s_):
            for ci in range(NCH):
                s_.wait_ge(s_f1.h, 4 * ci + 4)
                s_f3.inc(s_.copy(lsv[:, ci], facc[:]))

    s_d1 = p.sem(); s_d2 = p.sem(); s_d3 = p.sem(); s_lld = p.sem()
    ps_fl = p.psum("g_psfl", [128, 2048], F32, ar_fl)
    ll_stg = p.sbuf("g_llstg", [128, T], BF16, ar_fl)
    with p.block() as b:
        @b.vector
        def _(v):
            ij = 0
            for ci in range(NCH):
                for j in range(64):
                    if ij >= 2:
                        v.wait_ge(s_d2.h, ij - 1)
                    s_d1.inc(v.tensor_scalar(diag_bf[ij % 2][:], c["ident_f"][:],
                                             fwv[:, ci, 5 + j:6 + j], None, op0=ALU.mult))
                    ij += 1
        @b.tensor
        def _(t):
            ij = 0
            for ci in range(NCH):
                if ci >= 1:
                    t.wait_ge(s_d3.h, ci)
                for j in range(64):
                    t.wait_ge(s_d1.h, ij + 1)
                    for sl in range(4):
                        inst = t.matmul(ps_fl[:, 512 * sl:512 * (sl + 1)],
                                        diag_bf[ij % 2][:],
                                        vv[:, ci, 1 + j + 512 * sl:1 + j + 512 * (sl + 1)],
                                        start=(j == 0), stop=(j == 63))
                    s_d2.inc(inst)
                    ij += 1
        @b.scalar
        def _(s_):
            for ci in range(NCH):
                s_.wait_ge(s_d2.h, 64 * (ci + 1))
                if ci >= 1:
                    s_.wait_ge(s_lld.h, 16 * ci)
                s_d3.inc(s_.copy(ll_stg[:], ps_fl[:]))
        @b.gpsimd
        def _(g):
            for ci in range(NCH):
                g.wait_ge(s_d3.h, ci + 1)
                s_lld.dma(g.dma_start(out=llh[:, ci], in_=ll_stg[:]))
            g.wait_ge(s_lld.h, 16 * NCH)

    ar_fl.close()
    p.sem_free_since(m_p1)
    m_p2 = p.sem_mark()
    dsl_fm = p.sbuf("g_dsl", [128, NCH * 512], BF16, ar_f)
    sq_scr = p.sbuf("g_sq", [128, 512], F32, ar_f)
    ab_scr = p.sbuf("g_ab", [128, 512], F32, ar_f)
    st_sum = p.sbuf("g_stsum", [4, 512], F32, ar_f)
    st_sq = p.sbuf("g_stsq", [4, 512], F32, ar_f)
    st_ab = p.sbuf("g_stab", [4, 512], F32, ar_f)
    stats16 = p.sbuf("g_st16", [16, NH * 512], BF16, ar_f)
    sh_bf = p.sbuf("g_shbf", [128, 512], BF16, ar_f)
    logit_tm = p.sbuf("g_lgtm", [128, 4 * 4], F32, ar_f)
    prb_tm = p.sbuf("g_prb", [128, 4 * 4 * 4], F32, ar_f)
    prow = p.sbuf("g_prow", [1, 4 * 512], F32, ar_f)
    mixa = p.sbuf("g_mixa", [128, NCH * 512], F32, ar_f)
    nrow = p.sbuf("g_nrow", [1, 2048], F32, ar_f)
    ll_sl = p.sbuf("g_llsl", [128, NCH * 512], BF16, ar_f)
    o_stg2 = p.sbuf("g_ostg2", [128, 2 * D], BF16, ar_f)
    SMS = {k: p.sem(f"sms_{k}") for k in ["t1", "t2", "q1", "q2", "m1", "m1b", "dma",
                                "ld", "lo0", "lo1",
                                "g1", "g2", "g3", "g4", "g5", "g6", "g7", "g8", "mx1",
                                "n1", "n2", "n3", "n4", "n5"]}
    for sl in range(4):
        emit_l3_slice(p, c, sl, xn_bf, v_fm, o_hbm, ls_fm, locals())
    with p.block() as b:
        @b.vector
        def _(v):
            pass
    ar_f.close()
    p.sem_free_since(m_p2)

    m_p3 = p.sem_mark()
    ar_w = ExitStack()
    wo_sb = p.sbuf("g_wo", [128, NCH * D], BF16, ar_w)
    obuf = p.sbuf("g_obuf", [128, T], BF16, ar_w)
    ps_wo = [p.psum(f"g_pswo{i}", [128, 512], F32, ar_w) for i in range(2)]
    ldw = p.sem(); s_w1 = p.sem(); s_w2 = p.sem(); s_w3 = p.sem()
    with p.block() as b:
        @b.gpsimd
        def _(g):
            ldw.dma(g.dma_start(out=wo_sb.ap().rearrange("p (ci o) -> p ci o", ci=NCH),
                                in_=wo_h.rearrange("(ci p) o -> p ci o", p=128)))
        @b.tensor
        def _(t):
            t.wait_ge(ldw.h, 16)
    ahv = attn_hbm.rearrange("(cc p) t -> p cc t", p=128)
    wov = wo_sb.ap().rearrange("p (ci o) -> p ci o", ci=NCH)
    with p.block() as b:
        @b.tensor
        def _(t):
            i = 0
            for oc in range(NCH):
                for sl in range(4):
                    if i >= 2:
                        t.wait_ge(s_w2.h, i - 1)
                    for ci in range(NCH):
                        inst = t.matmul(ps_wo[i % 2][:], wov[:, ci, 128 * oc:128 * (oc + 1)],
                                        vv[:, ci, 64 + 512 * sl:64 + 512 * (sl + 1)],
                                        start=(ci == 0), stop=(ci == NCH - 1))
                    s_w1.inc(inst)
                    i += 1
        @b.vector
        def _(v):
            i = 0
            for oc in range(NCH):
                for sl in range(4):
                    v.wait_ge(s_w1.h, i + 1)
                    if sl == 0 and oc >= 1:
                        v.wait_ge(s_w3.h, 16 * oc)
                    s_w2.inc(v.tensor_copy(obuf[:, 512 * sl:512 * (sl + 1)], ps_wo[i % 2][:]))
                    i += 1
        @b.gpsimd
        def _(g):
            for oc in range(NCH):
                g.wait_ge(s_w2.h, 4 * (oc + 1))
                s_w3.dma(g.dma_start(out=ahv[:, oc], in_=obuf[:]))
            g.wait_ge(s_w3.h, s_w3.n)
    with p.block() as b:
        @b.vector
        def _(v):
            pass
    ar_w.close()
    ar.close()
    p.sem_free_since(m_p3)

def emit_l3_slice(p, c, sl, xn_bf, v_fm, o_hbm, ls_fm, X):
    NH, DV = 4, 256
    t0 = 512 * sl
    xnc = chunks(xn_bf)
    vv = v_fm.ap().rearrange("p (ci t) -> p ci t", ci=NCH)
    lsv = chunks(ls_fm)
    llh = X["llh"]
    ll_slv = X["ll_sl"].ap().rearrange("p (ci t) -> p ci t", ci=NCH)
    o_stg2v = X["o_stg2"].ap().rearrange("p (bb e) -> p bb e", bb=2)
    dslv = X["dsl_fm"].ap().rearrange("p (cc x) -> p cc x", cc=NCH)
    sq_scr, ab_scr = X["sq_scr"], X["ab_scr"]
    st_sum, st_sq, st_ab = X["st_sum"], X["st_sq"], X["st_ab"]
    stats16 = X["stats16"]
    sh_bf, logit_tm, prow, mixa, nrow = (X["sh_bf"], X["logit_tm"], X["prow"],
                                         X["mixa"], X["nrow"])
    mw1_sb, mw1b_sb, mb1_sb, mw2_sb, mb2_sb = (X["mw1_sb"], X["mw1b_sb"],
                                               X["mb1_sb"], X["mw2_sb"], X["mb2_sb"])
    onw_sb, tinv_b = X["onw_sb"], X["tinv_b"]
    mw1v = mw1_sb.ap().rearrange("p (ci o) -> p ci o", ci=NCH)
    mw2v = mw2_sb.ap().rearrange("p (ci o) -> p ci o", ci=NCH)

    SMS = X["SMS"]
    s_t1 = SMS["t1"]; s_t2 = SMS["t2"]; s_q1 = SMS["q1"]; s_q2 = SMS["q2"]
    s_m1 = SMS["m1"]; s_m1b = SMS["m1b"]; s_dma = SMS["dma"]
    s_g1 = SMS["g1"]; s_g2 = SMS["g2"]; s_g3 = SMS["g3"]; s_g4 = SMS["g4"]
    s_g5 = SMS["g5"]; s_g6 = SMS["g6"]; s_g7 = SMS["g7"]; s_g8 = SMS["g8"]
    s_mx1 = SMS["mx1"]
    ar_s = ExitStack()
    ps_tr = [p.psum(f"s{sl}_tr{i}", [128, 1024], BF16, ar_s) for i in range(2)]
    bt1 = s_t1.n
    bt2 = s_t2.n
    s_ld = X["SMS"]["ld"]
    s_lo = [X["SMS"]["lo0"], X["SMS"]["lo1"]]
    bld = s_ld.n
    blo = [s_lo[0].n, s_lo[1].n]
    with p.block() as b:
        @b.sync
        def _(sy):
            s_ld.dma(sy.dma_start(out=ll_slv, in_=llh[:, :, t0:t0 + 512]))
            for sub in range(4):
                tt = 4 * sl + sub
                if sub >= 2:
                    sy.wait_ge(s_t1.h, bt1 + sub - 1)
                s_lo[sub % 2].dma(sy.dma_start(out=o_stg2v[:, sub % 2, :],
                                  in_=o_hbm[128 * tt:128 * (tt + 1), :]))
        @b.tensor
        def _(t):
            for sub in range(4):
                if sub >= 2:
                    t.wait_ge(s_t2.h, bt2 + sub - 1)
                t.wait_ge(s_lo[sub % 2].h, blo[sub % 2] + 16 * (sub // 2 + 1))
                for ci in range(NCH):
                    inst = t.transpose(ps_tr[sub % 2][:, 128 * ci:128 * (ci + 1)],
                                       o_stg2v[:, sub % 2, 128 * ci:128 * (ci + 1)],
                                       c["ident_bf"][:])
                s_t1.inc(inst)
        @b.vector
        def _(v):
            v.wait_ge(s_ld.h, bld + 16)
            for sub in range(4):
                v.wait_ge(s_t1.h, bt1 + sub + 1)
                s_t2.inc(v.tensor_copy(dslv[:, :, 128 * sub:128 * (sub + 1)],
                                       ps_tr[sub % 2][:].rearrange("p (cc x) -> p cc x", cc=NCH)))

    srcs = [("ls", lambda ci: lsv[:, ci, t0:t0 + 512]),
            ("ll", lambda ci: ll_slv[:, ci]),
            ("dl", lambda ci: dslv[:, ci]),
            ("vd", lambda ci: vv[:, ci, 64 + t0:64 + t0 + 512])]
    ps_st = p.psum(f"s{sl}_st", [4, 512], F32, ar_s)
    ps_sq = p.psum(f"s{sl}_sq", [4, 512], F32, ar_s)
    ps_ab = p.psum(f"s{sl}_ab", [4, 512], F32, ar_s)
    for si, (nm, getter) in enumerate(srcs):
        bq1 = s_q1.n
        bm1 = s_m1.n
        with p.block() as b:
            @b.scalar
            def _(s_):
                for ci in range(NCH):
                    if ci >= 2:
                        s_.wait_ge(s_m1.h, bm1 + ci - 1)
                    s_q1.inc(s_.activation(out=sq_scr[:] if ci % 2 == 0 else ab_scr[:],
                                           in_=getter(ci), func=AF.Square))
            @b.tensor
            def _(t):
                for ci in range(NCH):
                    hh = ci // 2
                    r = 15 - hh
                    t.matmul(ps_st[:], c["stairs16b"][:, r:r + 4], getter(ci),
                             start=(ci == 0), stop=(ci == NCH - 1))
                    t.wait_ge(s_q1.h, bq1 + ci + 1)
                    inst = t.matmul(ps_sq[:], c["stairs16"][:, r:r + 4],
                                    sq_scr[:] if ci % 2 == 0 else ab_scr[:],
                                    start=(ci == 0), stop=(ci == NCH - 1))
                    s_m1.inc(inst)
        bq2 = s_q2.n
        bm1b = s_m1b.n
        with p.block() as b:
            @b.scalar
            def _(s_):
                for ci in range(NCH):
                    if ci >= 2:
                        s_.wait_ge(s_m1b.h, bm1b + ci - 1)
                    s_q2.inc(s_.activation(out=sq_scr[:] if ci % 2 == 0 else ab_scr[:],
                                           in_=getter(ci), func=AF.Abs))
            @b.tensor
            def _(t):
                for ci in range(NCH):
                    hh = ci // 2
                    r = 15 - hh
                    t.wait_ge(s_q2.h, bq2 + ci + 1)
                    inst = t.matmul(ps_ab[:], c["stairs16"][:, r:r + 4],
                                    sq_scr[:] if ci % 2 == 0 else ab_scr[:],
                                    start=(ci == 0), stop=(ci == NCH - 1))
                    s_m1b.inc(inst)
        with p.block() as b:
            @b.vector
            def _(v):
                v.tensor_scalar_mul(st_sum[:], ps_st[:], 1.0 / 256.0)
                v.tensor_scalar_mul(st_ab[:], ps_ab[:], 1.0 / 256.0)
        with p.block() as b:
            @b.gpsimd
            def _(g):
                g.tensor_tensor(sq_scr[0:4, 0:512], st_sum[:], st_sum[:], op=ALU.mult)
            @b.scalar
            def _(s_):
                s_.activation(out=ab_scr[0:4, 0:512], in_=ps_sq[:], func=AF.Sqrt)
        with p.block() as b:
            @b.vector
            def _(v):
                v.tensor_scalar_mul(st_sq[:], ps_sq[:], 1.0 / 256.0)
        with p.block() as b:
            @b.vector
            def _(v):
                v.tensor_tensor(st_sq[:], st_sq[:], sq_scr[0:4, 0:512], op=ALU.subtract)
        with p.block() as b:
            @b.gpsimd
            def _(g):
                for kind, src_t in [(0, st_sum), (1, st_sq), (2, st_ab)]:
                    s_dma.dma(g.dma_start(
                        out=stats16[4 * si + kind:4 * si + kind + 1, :].rearrange("p (h x) -> p h x", h=NH),
                        in_=src_t[:]))
                s_dma.dma(g.dma_start(
                    out=stats16[4 * si + 3:4 * si + 4, :].rearrange("p (h x) -> p h x", h=NH),
                    in_=ab_scr[0:4, 0:512]))
                g.wait_ge(s_dma.h, s_dma.n)

    FLOOR = 0.05
    ps_h1 = p.psum(f"s{sl}_h1", [128, 512], F32, ar_s)
    ps_lg = p.psum(f"s{sl}_lg", [128, 16], F32, ar_s)
    ps_pr = p.psum(f"s{sl}_pr", [4, 512], F32, ar_s)
    prtm = p.sbuf(f"s{sl}_prtm", [128, 16], F32, ar_s)
    for h in range(NH):
        bg1 = s_g1.n
        bg2 = s_g2.n
        bg3 = s_g3.n
        bg4 = s_g4.n
        bg8 = s_g8.n
        with p.block() as b:
            @b.tensor
            def _(t):
                s_g4.inc(t.matmul(ps_lg[:, 0:16], c["ones_f"][:], mb2_sb[:],
                                  start=True, stop=True))
                for oc in range(NCH):
                    if oc >= 1:
                        t.wait_ge(s_g2.h, bg2 + oc)
                    for ci in range(NCH):
                        t.matmul(ps_h1[:], mw1v[:, ci, 128 * oc:128 * (oc + 1)],
                                 xnc[:, ci, t0:t0 + 512],
                                 start=(ci == 0), stop=False)
                    inst = t.matmul(ps_h1[:], mw1b_sb[:, 128 * oc:128 * (oc + 1)],
                                    stats16[:, 512 * h:512 * (h + 1)],
                                    start=False, stop=True)
                    s_g1.inc(inst)
                    t.wait_ge(s_g2.h, bg2 + oc + 1)
                    t.wait_ge(s_g8.h, bg8 + oc + 1)
                    for sub in range(4):
                        inst = t.matmul(ps_lg[:, 4 * sub:4 * (sub + 1)],
                                        sh_bf[:, 128 * sub:128 * (sub + 1)],
                                        mw2v[:, oc, 0:4],
                                        start=True, stop=True)
                    s_g3.inc(inst)
            @b.scalar
            def _(s_):
                for oc in range(NCH):
                    s_.wait_ge(s_g1.h, bg1 + oc + 1)
                    if oc >= 1:
                        s_.wait_ge(s_g3.h, bg3 + oc)
                    s_g2.inc(s_.activation(out=sh_bf[:], in_=ps_h1[:], func=AF.Gelu,
                                           bias=mb1_sb[:, oc:oc + 1]))
            @b.vector
            def _(v):
                v.wait_ge(s_g4.h, bg4 + 1)
                s_g8.inc(v.tensor_copy(logit_tm[:], ps_lg[:, 0:16]))
                for oc in range(NCH):
                    v.wait_ge(s_g3.h, bg3 + oc + 1)
                    s_g8.inc(v.tensor_tensor(logit_tm[:], logit_tm[:],
                                             ps_lg[:, 0:16], op=ALU.add))
        with p.block() as b:
            @b.vector
            def _(v):
                v.tensor_scalar(prtm[:], logit_tm[:], tinv_b[:, h:h + 1], None,
                                op0=ALU.mult)
        with p.block() as b:
            @b.vector
            def _(v):
                v.tensor_reduce(sq_scr[0:128, 0:4], prtm.ap().rearrange("p (s f) -> p s f", s=4),
                                axis=AX.X, op=ALU.max, negate=True)
        with p.block() as b:
            @b.scalar
            def _(s_):
                pv2 = prtm.ap().rearrange("p (s f) -> p s f", s=4)
                for sub in range(4):
                    s_.activation(out=pv2[:, sub], in_=pv2[:, sub], func=AF.Exp,
                                  bias=sq_scr[0:128, sub:sub + 1],
                                  accum_out=ab_scr[0:128, sub:sub + 1])
        with p.block() as b:
            @b.vector
            def _(v):
                v.reciprocal(ab_scr[0:128, 0:4], ab_scr[0:128, 0:4])
        with p.block() as b:
            @b.gpsimd
            def _(g):
                pv2 = prtm.ap().rearrange("p (s f) -> p s f", s=4)
                for sub in range(4):
                    g.tensor_scalar(pv2[:, sub], pv2[:, sub], ab_scr[0:128, sub:sub + 1],
                                    None, op0=ALU.mult)
        with p.block() as b:
            @b.vector
            def _(v):
                v.tensor_scalar(prtm[:], prtm[:], 1.0 - 4.0 * FLOOR, FLOOR,
                                op0=ALU.mult, op1=ALU.add)
        with p.block() as b:
            @b.tensor
            def _(t):
                for sub in range(4):
                    inst = t.transpose(ps_pr[:, 128 * sub:128 * (sub + 1)],
                                       prtm[:, 4 * sub:4 * (sub + 1)],
                                       c["ident_f"][:])
                s_g5.inc(inst)
            @b.vector
            def _(v):
                v.wait_ge(s_g5.h, s_g5.n)
                v.wait_ge(s_g6.h, s_g6.n)
                v.tensor_copy(st_sum[:], ps_pr[:])
        bg6 = s_g6.n
        with p.block() as b:
            @b.gpsimd
            def _(g):
                s_g6.dma(g.dma_start(out=prow.ap().rearrange("p (s x) -> p s x", s=4),
                                     in_=st_sum[:]))
                g.wait_ge(s_g6.h, bg6 + 16)
        ps_pb = ps_h1
        s_g7b = s_g7
        with p.block() as b:
            @b.tensor
            def _(t):
                inst = t.matmul(ps_pb[:], c["ones_f"][:], prow[:, 0:512],
                                start=True, stop=True)
                s_g7.inc(inst)
            @b.vector
            def _(v):
                v.wait_ge(s_g7.h, s_g7.n)
                for cpart in range(2):
                    ci = 2 * h + cpart
                    v.tensor_tensor(mixa[:, 512 * ci:512 * (ci + 1)],
                                    lsv[:, ci, t0:t0 + 512], ps_pb[:], op=ALU.mult)
        for si2, nm2 in [(1, "ll"), (2, "dl"), (3, "vd")]:
            bmx = s_mx1.n
            with p.block() as b:
                @b.tensor
                def _(t):
                    inst = t.matmul(ps_pb[:], c["ones_f"][:],
                                    prow[:, 512 * si2:512 * (si2 + 1)],
                                    start=True, stop=True)
                    s_g7.inc(inst)
                @b.vector
                def _(v):
                    v.wait_ge(s_g7.h, s_g7.n)
                    for cpart in range(2):
                        ci = 2 * h + cpart
                        src = (ll_slv[:, ci] if si2 == 1 else
                               dslv[:, ci] if si2 == 2 else
                               vv[:, ci, 64 + t0:64 + t0 + 512])
                        s_mx1.inc(v.tensor_tensor(sq_scr[:] if cpart == 0 else ab_scr[:],
                                                  src, ps_pb[:], op=ALU.mult))
                @b.gpsimd
                def _(g):
                    for cpart in range(2):
                        ci = 2 * h + cpart
                        g.wait_ge(s_mx1.h, bmx + cpart + 1)
                        g.tensor_tensor(mixa[:, 512 * ci:512 * (ci + 1)],
                                        mixa[:, 512 * ci:512 * (ci + 1)],
                                        sq_scr[:] if cpart == 0 else ab_scr[:],
                                        op=ALU.add)
    sn1 = SMS["n1"]; sn2 = SMS["n2"]; sn3 = SMS["n3"]; sn4 = SMS["n4"]; sn5 = SMS["n5"]
    bn1 = sn1.n; bn2 = sn2.n; bn3 = sn3.n; bn4 = sn4.n; bn5 = sn5.n
    ps_nr = ps_st
    with p.block() as b:
        @b.scalar
        def _(s_):
            for ci in range(NCH):
                if ci >= 2:
                    s_.wait_ge(sn2.h, bn2 + ci - 1)
                sn1.inc(s_.activation(out=sq_scr[:] if ci % 2 == 0 else ab_scr[:],
                                      in_=mixa[:, 512 * ci:512 * (ci + 1)], func=AF.Square))
        @b.tensor
        def _(t):
            for ci in range(NCH):
                hh = ci // 2
                r = 15 - hh
                t.wait_ge(sn1.h, bn1 + ci + 1)
                inst = t.matmul(ps_nr[:], c["stairs16"][:, r:r + 4],
                                sq_scr[:] if ci % 2 == 0 else ab_scr[:],
                                start=(ci == 0), stop=(ci == NCH - 1))
                sn2.inc(inst)
    with p.block() as b:
        @b.vector
        def _(v):
            v.tensor_scalar(st_sum[:], ps_nr[:], 1.0 / 256.0, 1.0e-5,
                            op0=ALU.mult, op1=ALU.add)
    with p.block() as b:
        @b.scalar
        def _(s_):
            s_.activation(out=st_sum[:], in_=st_sum[:], func=AF.Sqrt)
    with p.block() as b:
        @b.vector
        def _(v):
            v.reciprocal(st_sum[:], st_sum[:])
    with p.block() as b:
        @b.gpsimd
        def _(g):
            sn3.dma(g.dma_start(out=nrow.ap().rearrange("p (h x) -> p h x", h=NH),
                                in_=st_sum[:]))
            g.wait_ge(sn3.h, sn3.n)
    for h in range(NH):
        bn4h = sn4.n
        bn5h = sn5.n
        with p.block() as b:
            @b.tensor
            def _(t):
                sn4.inc(t.matmul(ps_h1[:], c["ones_f"][:], nrow[:, 512 * h:512 * (h + 1)],
                                 start=True, stop=True))
            @b.vector
            def _(v):
                v.wait_ge(sn4.h, bn4h + 1)
                for cpart in range(2):
                    ci = 2 * h + cpart
                    sn5.inc(v.tensor_tensor(sq_scr[:] if cpart == 0 else ab_scr[:],
                                            mixa[:, 512 * ci:512 * (ci + 1)],
                                            ps_h1[:], op=ALU.mult))
            @b.gpsimd
            def _(g):
                for cpart in range(2):
                    ci = 2 * h + cpart
                    g.wait_ge(sn5.h, bn5h + cpart + 1)
                    g.tensor_scalar(vv[:, ci, 64 + t0:64 + t0 + 512],
                                    sq_scr[:] if cpart == 0 else ab_scr[:],
                                    onw_sb[:, ci % 2:ci % 2 + 1], None, op0=ALU.mult)
    with p.block() as b:
        @b.vector
        def _(v):
            pass
    ar_s.close()

def emit_l3_ffn(p, c, xn_bf, x_h, attn_hbm, y_h,
                fw1_h, fs1_h, fw2_h, fs2_h, fh1_h, fh1s_h, fh2_h, fh2s_h,
                fw3_h, fs3_h):
    HC = 32
    ar = ExitStack()
    fs1_sb = p.sbuf("f_s1", [128, HC], F32, ar)
    fs2_sb = p.sbuf("f_s2", [128, HC], F32, ar)
    fh1s_sb = p.sbuf("f_h1s", [128, HC], F32, ar)
    fh2s_sb = p.sbuf("f_h2s", [128, HC], F32, ar)
    fs3_sb = p.sbuf("f_s3", [128, NCH], F32, ar)
    w12 = p.sbuf("f_w12", [128, 2 * 2 * NCH * 128], BF16, ar)
    scr = [p.sbuf(f"f_scr{i}", [128, 512], F32, ar) for i in range(2)]
    h0_bf = p.sbuf("f_h0", [128, HC * 1024], BF16, ar)
    ps_a = [p.psum(f"f_psa{i}", [128, 512], F32, ar) for i in range(2)]
    ps_b = [p.psum(f"f_psb{i}", [128, 512], F32, ar) for i in range(2)]
    xnc = chunks(xn_bf)
    h0v = h0_bf.ap().rearrange("p (cc t) -> p cc t", cc=HC)
    w12v = w12.ap().rearrange("p (par w ci o) -> p par w ci o", par=2, w=2, ci=NCH)
    fw1v = fw1_h.rearrange("(ci p) o -> p ci o", p=128)
    fw2v = fw2_h.rearrange("(ci p) o -> p ci o", p=128)
    fh1v = fh1_h.rearrange("(ci p) o -> p ci o", p=128)
    fh2v = fh2_h.rearrange("(ci p) o -> p ci o", p=128)
    fw3v = fw3_h.rearrange("(ci p) o -> p ci o", p=128)
    xv = x_h.rearrange("(cc p) t -> p cc t", p=128)
    ahv = attn_hbm.rearrange("(cc p) t -> p cc t", p=128)
    yv = y_h.rearrange("(cc p) t -> p cc t", p=128)

    ld = p.sem()
    with p.block() as b:
        @b.sync
        def _(sy):
            for dst, src in [(fs1_sb, fs1_h), (fs2_sb, fs2_h), (fh1s_sb, fh1s_h),
                             (fh2s_sb, fh2s_h), (fs3_sb, fs3_h)]:
                ld.dma(sy.dma_start(out=dst[:], in_=src))
        @b.vector
        def _(v):
            v.wait_ge(ld.h, ld.n)

    s_wg = [p.sem("f_wg0"), p.sem("f_wg1")]
    s_mm = p.sem("f_mm"); s_ac = p.sem("f_ac"); s_dv = p.sem("f_dv")
    s_xa = [p.sem("f_xa0"), p.sem("f_xa1")]
    s_gp = p.sem("f_gp")
    s_yo = [p.sem("f_yo0"), p.sem("f_yo1")]
    for half in range(2):
        toff = 1024 * half
        bwg = [s_wg[0].n, s_wg[1].n]
        bmm = s_mm.n; bac = s_ac.n; bdv = s_dv.n
        with p.block() as b:
            @b.gpsimd
            def _(g):
                for oc in range(HC):
                    par = oc % 2
                    if oc >= 2:
                        g.wait_ge(s_dv.h, bdv + 2 * oc - 2)
                    s_wg[par].dma(g.dma_start(out=w12v[:, par, 0],
                                              in_=fw1v[:, :, 128 * oc:128 * (oc + 1)]))
                    s_wg[par].dma(g.dma_start(out=w12v[:, par, 1],
                                              in_=fw2v[:, :, 128 * oc:128 * (oc + 1)]))
            @b.tensor
            def _(t):
                for oc in range(HC):
                    par = oc % 2
                    t.wait_ge(s_wg[par].h, bwg[par] + 32 * (oc // 2 + 1))
                    for j in range(2):
                        idx = 2 * oc + j
                        bank = idx % 2
                        if idx >= 2:
                            t.wait_ge(s_ac.h, bac + idx - 1)
                            t.wait_ge(s_dv.h, bdv + idx - 1)
                        for ci in range(NCH):
                            inst = t.matmul(ps_a[bank][:], w12v[:, par, 0, ci, :],
                                            xnc[:, ci, toff + 512 * j:toff + 512 * (j + 1)],
                                            start=(ci == 0), stop=(ci == NCH - 1))
                        s_mm.inc(inst)
                        for ci in range(NCH):
                            inst = t.matmul(ps_b[bank][:], w12v[:, par, 1, ci, :],
                                            xnc[:, ci, toff + 512 * j:toff + 512 * (j + 1)],
                                            start=(ci == 0), stop=(ci == NCH - 1))
                        s_mm.inc(inst)
            @b.scalar
            def _(s_):
                for idx in range(2 * HC):
                    oc = idx // 2
                    s_.wait_ge(s_mm.h, bmm + 2 * idx + 1)
                    if idx >= 2:
                        s_.wait_ge(s_dv.h, bdv + idx - 1)
                    s_ac.inc(s_.activation(out=scr[idx % 2][:], in_=ps_a[idx % 2][:],
                                           func=AF.Silu, scale=fs1_sb[:, oc:oc + 1]))
            @b.vector
            def _(v):
                for idx in range(2 * HC):
                    oc, j = idx // 2, idx % 2
                    v.wait_ge(s_mm.h, bmm + 2 * idx + 2)
                    v.wait_ge(s_ac.h, bac + idx + 1)
                    s_dv.inc(v.scalar_tensor_tensor(h0v[:, oc, 512 * j:512 * (j + 1)],
                                                    ps_b[idx % 2][:], fs2_sb[:, oc:oc + 1],
                                                    scr[idx % 2][:],
                                                    op0=ALU.mult, op1=ALU.mult))
        ar_bc = ExitStack()
        h1_bf = p.sbuf(f"f_h1_{half}", [128, HC * 1024], BF16, ar_bc)
        wh = p.sbuf(f"f_wh_{half}", [128, 2 * HC * 128], BF16, ar_bc)
        h1v = h1_bf.ap().rearrange("p (cc t) -> p cc t", cc=HC)
        whv = wh.ap().rearrange("p (par ci o) -> p par ci o", par=2, ci=HC)
        for stage, (wsrc, ssb, src_v, dst_v) in enumerate(
                [(fh1v, fh1s_sb, h0v, h1v), (fh2v, fh2s_sb, h1v, h0v)]):
            bwg = [s_wg[0].n, s_wg[1].n]
            bmm = s_mm.n; bac = s_ac.n
            with p.block() as b:
                @b.gpsimd
                def _(g):
                    for oc in range(HC):
                        par = oc % 2
                        if oc >= 2:
                            g.wait_ge(s_ac.h, bac + 2 * oc - 2)
                        s_wg[par].dma(g.dma_start(out=whv[:, par],
                                                  in_=wsrc[:, :, 128 * oc:128 * (oc + 1)]))
                @b.tensor
                def _(t):
                    for oc in range(HC):
                        par = oc % 2
                        t.wait_ge(s_wg[par].h, bwg[par] + 16 * (oc // 2 + 1))
                        for j in range(2):
                            idx = 2 * oc + j
                            if idx >= 2:
                                t.wait_ge(s_ac.h, bac + idx - 1)
                            for ci in range(HC):
                                inst = t.matmul(ps_a[idx % 2][:], whv[:, par, ci, :],
                                                src_v[:, ci, 512 * j:512 * (j + 1)],
                                                start=(ci == 0), stop=(ci == HC - 1))
                            s_mm.inc(inst)
                @b.scalar
                def _(s_):
                    for idx in range(2 * HC):
                        oc, j = idx // 2, idx % 2
                        s_.wait_ge(s_mm.h, bmm + idx + 1)
                        s_ac.inc(s_.activation(out=dst_v[:, oc, 512 * j:512 * (j + 1)],
                                               in_=ps_a[idx % 2][:], func=AF.Gelu,
                                               scale=ssb[:, oc:oc + 1]))
        with p.block() as b:
            @b.vector
            def _(v):
                pass
        ar_bc.close()
        ar_d = ExitStack()
        wh3 = p.sbuf(f"f_wh3_{half}", [128, NCH * HC * 128], BF16, ar_d)
        xs = [p.sbuf(f"f_xs{half}{i}", [128, 512], F32, ar_d) for i in range(2)]
        as_ = [p.sbuf(f"f_as{half}{i}", [128, 512], BF16, ar_d) for i in range(2)]
        y_sb = [p.sbuf(f"f_ys{half}{i}", [128, 512], F32, ar_d) for i in range(2)]
        wh3v = wh3.ap().rearrange("p (cc ci o) -> p cc ci o", cc=NCH, ci=HC)
        bwg3 = s_wg[0].n
        bmm = s_mm.n; bdv = s_dv.n; bgp = s_gp.n
        bxa = [s_xa[0].n, s_xa[1].n]
        byo = [s_yo[0].n, s_yo[1].n]
        with p.block() as b:
            @b.gpsimd
            def _(g):
                for oc in range(NCH):
                    s_wg[0].dma(g.dma_start(out=wh3v[:, oc],
                                            in_=fw3v[:, :, 128 * oc:128 * (oc + 1)]))
                for idx in range(2 * NCH):
                    g.wait_ge(s_dv.h, bdv + idx + 1)
                    if idx >= 2:
                        g.wait_ge(s_yo[idx % 2].h, byo[idx % 2] + 16 * (idx // 2))
                    s_gp.inc(g.tensor_tensor(y_sb[idx % 2][:], scr[idx % 2][:],
                                             xs[idx % 2][:], op=ALU.add))
            @b.sync
            def _(sy):
                for idx in range(2 * NCH):
                    oc, j = idx // 2, idx % 2
                    tt = toff + 512 * j
                    if idx >= 2:
                        sy.wait_ge(s_gp.h, bgp + idx - 1)
                    s_xa[idx % 2].dma(sy.dma_start(out=xs[idx % 2][:],
                                      in_=xv[:, oc, tt:tt + 512]))
                    s_xa[idx % 2].dma(sy.dma_start(out=as_[idx % 2][:],
                                      in_=ahv[:, oc, tt:tt + 512]))
                    if idx >= 1:
                        po, pj = (idx - 1) // 2, (idx - 1) % 2
                        ptt = toff + 512 * pj
                        sy.wait_ge(s_gp.h, bgp + idx)
                        s_yo[(idx - 1) % 2].dma(sy.dma_start(out=yv[:, po, ptt:ptt + 512],
                                                in_=y_sb[(idx - 1) % 2][:]))
                sy.wait_ge(s_gp.h, bgp + 2 * NCH)
                s_yo[1].dma(sy.dma_start(out=yv[:, NCH - 1, toff + 512:toff + 1024],
                                         in_=y_sb[1][:]))
                sy.wait_ge(s_yo[0].h, byo[0] + 16 * NCH)
                sy.wait_ge(s_yo[1].h, byo[1] + 16 * NCH)
            @b.tensor
            def _(t):
                t.wait_ge(s_wg[0].h, bwg3 + 16 * NCH)
                for oc in range(NCH):
                    for j in range(2):
                        idx = 2 * oc + j
                        if idx >= 2:
                            t.wait_ge(s_dv.h, bdv + idx - 1)
                        for ci in range(HC):
                            inst = t.matmul(ps_a[idx % 2][:], wh3v[:, oc, ci, :],
                                            h0v[:, ci, 512 * j:512 * (j + 1)],
                                            start=(ci == 0), stop=(ci == HC - 1))
                        s_mm.inc(inst)
            @b.vector
            def _(v):
                for idx in range(2 * NCH):
                    v.wait_ge(s_mm.h, bmm + idx + 1)
                    v.wait_ge(s_xa[idx % 2].h, bxa[idx % 2] + 32 * (idx // 2 + 1))
                    if idx >= 2:
                        v.wait_ge(s_gp.h, bgp + idx - 1)
                    s_dv.inc(v.scalar_tensor_tensor(scr[idx % 2][:], ps_a[idx % 2][:],
                                                    fs3_sb[:, idx // 2:idx // 2 + 1],
                                                    as_[idx % 2][:],
                                                    op0=ALU.mult, op1=ALU.add))
        ar_d.close()
    with p.block() as b:
        @b.vector
        def _(v):
            v.wait_ge(s_yo[0].h, s_yo[0].n)
            v.wait_ge(s_yo[1].h, s_yo[1].n)
    ar.close()

def build_L0cnt():
    p = Prog()
    E, K = 32, 4
    x_h = p.din("x", [D, T], F32)
    wg_h = p.din("wg", [D, E], F32)
    cnt_d = p.dout("cnt", [1, E], F32)
    c = emit_consts(p, ident_bf=True, ident_f32=True, tril_s=True)
    x_sb = p.sbuf("x_sb", [128, NCH * T], F32)
    xn_bf = p.sbuf("xn_bf", [128, NCH * T], BF16)
    wg_sb = p.sbuf("wg_sb", [128, NCH * E], F32)
    s_tm = p.sbuf("s_tm", [128, NTT * E], F32)
    cnt_tm = p.sbuf("cnt_tm", [128, NTT * E], F32)
    emit_load_fm(p, x_h, x_sb)
    ldw = p.sem()
    with p.block() as b:
        @b.sync
        def _(sy):
            ldw.dma(sy.dma_start(out=wg_sb.ap().rearrange("p (ci e) -> p ci e", ci=NCH),
                                 in_=wg_h.rearrange("(ci p) e -> p ci e", p=128)))
        @b.vector
        def _(v):
            v.wait_ge(ldw.h, 16)
    ln = emit_ln(p, c, x_sb, xn_bf, "ln", None)
    emit_scores(p, c, x_sb, ln, wg_sb, E, s_tm, "sc")
    emit_softmax_tm(p, c, s_tm, E, "sm")
    ohs, vals = emit_topk_tm(p, c, s_tm, E, K, "tk")
    with p.block() as b:
        @b.vector
        def _(v):
            v.tensor_tensor(cnt_tm[:], ohs[0][:], ohs[1][:], op=ALU.add)
    with p.block() as b:
        @b.gpsimd
        def _(g):
            g.tensor_tensor(cnt_tm[:], cnt_tm[:], ohs[2][:], op=ALU.add)
    with p.block() as b:
        @b.vector
        def _(v):
            v.tensor_tensor(cnt_tm[:], cnt_tm[:], ohs[3][:], op=ALU.add)
    pref_tm, cnt_row = emit_prefix_tokens(p, c, cnt_tm, E, None, "pf")
    st = p.sem()
    with p.block() as b:
        @b.gpsimd
        def _(g):
            st.dma(g.dma_start(out=cnt_d, in_=cnt_row[:]))
            g.wait_ge(st.h, 16)
    return p


def build_L1cnt():
    p = Prog()
    E, K = 8, 2
    x_h = p.din("x", [D, T], F32)
    wr_h = p.din("wr", [D, E], F32)
    cnt_d = p.dout("cnt", [1, K * E], F32)
    c = emit_consts(p, ident_bf=True, ident_f32=True, tril_s=True)
    x_sb = p.sbuf("x_sb", [128, NCH * T], F32)
    xn_bf = p.sbuf("xn_bf", [128, NCH * T], BF16)
    wr_sb = p.sbuf("wr_sb", [128, NCH * E], F32)
    s_tm = p.sbuf("s_tm", [128, NTT * E], F32)
    cnt16 = p.sbuf("cnt16", [1, K * E], F32)
    emit_load_fm(p, x_h, x_sb)
    ldw = p.sem()
    with p.block() as b:
        @b.sync
        def _(sy):
            ldw.dma(sy.dma_start(out=wr_sb.ap().rearrange("p (ci e) -> p ci e", ci=NCH),
                                 in_=wr_h.rearrange("(ci p) e -> p ci e", p=128)))
        @b.vector
        def _(v):
            v.wait_ge(ldw.h, 16)
    ln = emit_ln(p, c, x_sb, xn_bf, "ln", None)
    emit_scores(p, c, x_sb, ln, wr_sb, E, s_tm, "sc")
    emit_softmax_tm(p, c, s_tm, E, "sm")
    ohs, vals = emit_topk_tm(p, c, s_tm, E, K, "tk")
    for k in range(K):
        pref_tm, cnt_row = emit_prefix_tokens(p, c, ohs[k], E, None, f"pf{k}")
        with p.block() as b:
            @b.vector
            def _(v):
                v.tensor_copy(cnt16[0:1, E * k:E * (k + 1)], cnt_row[:])
    st = p.sem()
    with p.block() as b:
        @b.gpsimd
        def _(g):
            st.dma(g.dma_start(out=cnt_d, in_=cnt16[:]))
            g.wait_ge(st.h, 16)
    return p

from concourse.bass_utils import run_bass_kernel_spmd


def _f32(a):
    return np.ascontiguousarray(np.asarray(a, np.float32))


def kernel(**inp):
    g = lambda k: np.asarray(inp[k], np.float32)
    x = g('L0_attn_x')
    B = x.shape[0]
    pmasks = [(np.arange(8) < i).astype(np.float32).reshape(8, 1) for i in range(B)]

    p = build_L0cnt()
    in_maps = [{"x": _f32(x[i].T), "wg": g('L0_ffn_Wg')} for i in range(B)]
    res = run_bass_kernel_spmd(p.nc, in_maps, core_ids=list(range(B)))
    cnts0 = np.concatenate([res.results[i]["cnt"] for i in range(B)], 0)

    p = build_L0()
    w = {"wq": g('L0_attn_Wq'), "wk": g('L0_attn_Wk'), "wv": g('L0_attn_Wv'),
         "wo": g('L0_attn_Wo'), "wg": g('L0_ffn_Wg'),
         "w1": g('L0_ffn_W1'), "w2": g('L0_ffn_W2'), "w3": g('L0_ffn_W3'),
         "ws1": g('L0_ffn_Ws1'), "ws2": g('L0_ffn_Ws2'), "ws3": g('L0_ffn_Ws3'),
         "cnts": _f32(cnts0)}
    in_maps = []
    for i in range(B):
        m = dict(w)
        m["x"] = _f32(x[i].T)
        m["pmask"] = pmasks[i]
        in_maps.append(m)
    res = run_bass_kernel_spmd(p.nc, in_maps, core_ids=list(range(B)))
    x1 = [np.asarray(res.results[i]["y"], np.float32) for i in range(B)]

    p = build_L1cnt()
    in_maps = [{"x": x1[i], "wr": g('L1_ffn_Wr')} for i in range(B)]
    res = run_bass_kernel_spmd(p.nc, in_maps, core_ids=list(range(B)))
    cnts1 = np.concatenate([res.results[i]["cnt"] for i in range(B)], 0)

    p = build_L1()
    b1 = g('L1_ffn_b1')
    w = {"wq": g('L1_attn_WQ'), "wk": g('L1_attn_WK'),
         "wvd": g('L1_attn_WV_down'), "wvu": g('L1_attn_WV_up'),
         "wr": g('L1_ffn_Wr'), "w1": g('L1_ffn_W1'),
         "b1": _f32(b1.reshape(8, 32, 128).transpose(2, 0, 1).reshape(128, 8 * 32)),
         "w2": g('L1_ffn_W2'), "b2": g('L1_ffn_b2'),
         "cnts": _f32(cnts1)}
    in_maps = []
    for i in range(B):
        m = dict(w)
        m["x"] = x1[i]
        m["pmask"] = pmasks[i]
        in_maps.append(m)
    res = run_bass_kernel_spmd(p.nc, in_maps, core_ids=list(range(B)))
    x2 = [np.asarray(res.results[i]["y"], np.float32) for i in range(B)]

    p = build_L2()
    w = {"wq": g('L2_attn_Wq'), "wk": g('L2_attn_Wk'),
         "wv": g('L2_attn_Wv'), "wo": g('L2_attn_Wo'),
         "sig": _f32(g('L2_ffn_signatures_raw').T),
         "knots": _f32(g('L2_ffn_knot_values').reshape(1, 8)),
         "temp": _f32(g('L2_ffn_temperature').reshape(1, 1)),
         "cw1": g('L2_ffn_compress_w1'),
         "cb1": _f32(g('L2_ffn_compress_b1').reshape(2, 128).T),
         "cw2": g('L2_ffn_compress_w2'),
         "cb2": _f32(g('L2_ffn_compress_b2').reshape(1, 2)),
         "spl": _f32(np.concatenate([
             g('L2_ffn_spline_w1').reshape(64, 32), g('L2_ffn_spline_b1'),
             g('L2_ffn_spline_w2').reshape(64, 16), g('L2_ffn_spline_b2')], axis=1)),
         "dirs": g('L2_ffn_directions'),
         "gam": _f32(g('L2_ffn_norm_gamma').reshape(8, 128).T),
         "bet": _f32(g('L2_ffn_norm_beta').reshape(8, 128).T),
         "osc": _f32(g('L2_ffn_output_scale').reshape(1, 1))}
    in_maps = []
    for i in range(B):
        m = dict(w)
        m["x"] = x2[i]
        in_maps.append(m)
    res = run_bass_kernel_spmd(p.nc, in_maps, core_ids=list(range(B)))
    x3 = [np.asarray(res.results[i]["y"], np.float32) for i in range(B)]

    p = build_L3()
    mw1 = g('L3_attn_mlp_w1')
    w = {"wq": g('L3_attn_Wq'), "wk": g('L3_attn_Wk'), "wv": g('L3_attn_Wv'),
         "wb": g('L3_attn_Wb'),
         "cqw": g('L3_attn_conv_q_w'), "ckw": g('L3_attn_conv_k_w'),
         "cvw": g('L3_attn_conv_v_w'),
         "fsw": _f32(g('L3_attn_fir_short_w').reshape(1024, 5)),
         "flw": _f32(g('L3_attn_fir_long_w').reshape(1024, 64)),
         "mw1": _f32(mw1[0:1024]), "mw1b": _f32(mw1[1024:1040]),
         "mb1": _f32(g('L3_attn_mlp_b1').reshape(8, 128).T),
         "mw2": g('L3_attn_mlp_w2'),
         "mb2": _f32(np.tile(g('L3_attn_mlp_b2'), 4).reshape(1, 16)),
         "ltemp": _f32(g('L3_attn_log_temp').reshape(4, 1)),
         "onw": _f32(g('L3_attn_o_norm_w').reshape(2, 128).T),
         "wo": g('L3_attn_Wo'),
         "fw1": _f32(g('L3_ffn_w1_w').T),
         "fs1": _f32(g('L3_ffn_w1_s').reshape(32, 128).T),
         "fw2": _f32(g('L3_ffn_w2_w').T),
         "fs2": _f32(g('L3_ffn_w2_s').reshape(32, 128).T),
         "fh1": _f32(g('L3_ffn_h1_w').T),
         "fh1s": _f32(g('L3_ffn_h1_s').reshape(32, 128).T),
         "fh2": _f32(g('L3_ffn_h2_w').T),
         "fh2s": _f32(g('L3_ffn_h2_s').reshape(32, 128).T),
         "fw3": _f32(g('L3_ffn_w3_w').T),
         "fs3": _f32(g('L3_ffn_w3_s').reshape(8, 128).T)}
    in_maps = []
    for i in range(B):
        m = dict(w)
        m["x"] = x3[i]
        in_maps.append(m)
    res = run_bass_kernel_spmd(p.nc, in_maps, core_ids=list(range(B)))
    out = np.stack([np.asarray(res.results[i]["y"], np.float32).T for i in range(B)])
    return out
